# version 05
# baseline (speedup 1.0000x reference)
from contextlib import ExitStack

import numpy as np
import ml_dtypes

import concourse.bass as bass
import concourse.mybir as mybir
from concourse.bass_utils import run_bass_kernel_spmd

F32 = mybir.dt.float32
BF16 = mybir.dt.bfloat16
I32 = mybir.dt.int32
F16 = mybir.dt.float16
AF = mybir.ActivationFunctionType
ALU = mybir.AluOpType
AXL = mybir.AxisListType

NCORE = 8
B, S, D = 2, 8192, 2048
TLOC = 2048
DC = D // 128
HFF = 8192
HC = HFF // 128
NHEAD, DKH, SEG = 16, 128, 512
NSEG_LOC = TLOC // SEG
I1 = 5632
EPS_LN = 1e-5
EPS_RMS = 1e-6
INV_SCALE_ATT = float(1.0 / np.float32(np.sqrt(np.float32(128.0))))
FWHT_SCALE = float(1.0 / np.sqrt(8192.0))

ENGS = ("sync", "scalar", "vector", "tensor", "gpsimd")
_S2_STOP = None
_S2_NHEADS = 16
_S2_NPROJ = 12


class Prog:
    def __init__(self, nc, ctx):
        self.nc = nc
        self.ctx = ctx
        self.ops = {e: [] for e in ENGS}
        self.cnt = {}
        self.sems = {}
        self.pend = {e: [] for e in ENGS}
        self.dirty = {e: False for e in ENGS}

    def sem(self, name):
        if name not in self.sems:
            self.sems[name] = self.ctx.enter_context(self.nc.semaphore(name))
            self.cnt[name] = 0
        return name

    AUTOCHAIN = ("vector", "scalar")

    def op(self, eng, fn, wait=(), inc=None, is_dma=False):
        waits = [w for w in (list(self.pend[eng]) + list(wait))
                 if w is not None and w[1] > 0]
        self.pend[eng] = []
        chain = eng in self.AUTOCHAIN and not is_dma
        if chain:
            prev = getattr(self, "chain_mark", {}).get(eng)
            if prev is not None:
                waits.append(prev)
            if inc is None:
                inc = eng + "_ch"
        incs = []
        mark = None
        if inc is not None:
            self.sem(inc)
            k = 16 if is_dma else 1
            self.cnt[inc] += k
            mark = (inc, self.cnt[inc])
            incs.append((inc, k))
        if chain:
            if not hasattr(self, "chain_mark"):
                self.chain_mark = {}
            self.chain_mark[eng] = mark
        for (s, _) in waits:
            self.sem(s)
        self.ops[eng].append([fn, waits, incs, is_dma])
        self.dirty[eng] = True
        return mark

    def dma(self, q, fn, wait=(), stream=None, slot=0):
        sem = f"ds_{stream}_{slot}" if stream else "d_" + q
        return self.op(q, fn, wait=wait, inc=sem, is_dma=True)

    def val(self, s):
        return self.cnt.get(s, 0)

    def mark(self, s):
        return (s, self.cnt.get(s, 0))

    def milestone(self, eng):
        assert self.ops[eng], f"milestone on empty engine {eng}"
        last = self.ops[eng][-1]
        if last[2]:
            s, k = last[2][-1]
            return (s, self.cnt[s])
        s = eng + "_ms"
        self.sem(s)
        k = 16 if last[3] else 1
        self.cnt[s] += k
        last[2].append((s, k))
        return (s, self.cnt[s])

    def barrier(self, exclude=()):
        marks = []
        for e in ENGS:
            if not self.dirty[e] or not self.ops[e]:
                continue
            last = self.ops[e][-1]
            if last[2]:
                sname, k = last[2][-1]
                marks.append((sname, self.cnt[sname]))
            else:
                sname = "bm_" + e
                self.sem(sname)
                k = 16 if last[3] else 1
                self.cnt[sname] += k
                last[2].append((sname, k))
                marks.append((sname, self.cnt[sname]))
            self.dirty[e] = False
        for s, v in self.cnt.items():
            if (s.startswith("d_") or s.startswith("ds_")) and v > 0:
                marks.append((s, v))
        marks = [(s, v) for (s, v) in marks if s not in exclude]
        marks = list({s: (s, v) for (s, v) in marks}.values())
        for e in ENGS:
            self.pend[e] = self.pend[e] + marks
        return marks

    def finalize(self, final_waits):
        for (s, v) in final_waits:
            if v > 0:
                self.sem(s)
                self.ops["gpsimd"].append([None, [(s, v)], [], False])
        blk = self.ctx.enter_context(self.nc.Block())
        prog = self

        def runner(ename):
            def run(eng):
                for fn, waits, incs, _ in prog.ops[ename]:
                    for (s, v) in waits:
                        eng.wait_ge(prog.sems[s], v)
                    if fn is None:
                        continue
                    inst = fn(eng)
                    for (s, k) in incs:
                        inst.then_inc(prog.sems[s], k)
            return run

        blk.sync(runner("sync"))
        blk.scalar(runner("scalar"))
        blk.vector(runner("vector"))
        blk.tensor(runner("tensor"))
        blk.gpsimd(runner("gpsimd"))


def tile_w_out_in(W, n_out_chunks, n_k_chunks):
    O, K = W.shape
    t = W.reshape(n_out_chunks, 128, n_k_chunks, 128).transpose(0, 2, 3, 1)
    return np.ascontiguousarray(t.transpose(0, 2, 1, 3))


def tile_w_in_out(W, n_out_chunks, n_k_chunks):
    t = W.reshape(n_k_chunks, 128, n_out_chunks, 128).transpose(2, 0, 1, 3)
    return np.ascontiguousarray(t.transpose(0, 2, 1, 3))


def hadamard128():
    H = np.array([[1.0]], dtype=np.float32)
    for _ in range(7):
        H = np.block([[H, H], [H, -H]])
    return np.ascontiguousarray(H.astype(np.float32))


def causal_masks():
    m = np.zeros((4, 128, 512), dtype=np.float32)
    t = np.arange(512)[None, :]
    for j in range(4):
        i = np.arange(128)[:, None] + 128 * j
        m[j] = np.where(t <= i, 0.0, np.float32(-1e38)).astype(np.float32)
    return m


def core_slices(x):
    out = []
    for c in range(NCORE):
        b, q = c // 4, c % 4
        out.append(np.ascontiguousarray(x[b, q * TLOC:(q + 1) * TLOC, :].T))
    return out


def stage_weights_cast(p, nc, ext_full, n_per_rank, tag, split=None):
    wbf = nc.dram_tensor(f"{tag}_bf", [NCORE * n_per_rank], BF16,
                         kind="Internal")
    spl = split if split is not None else n_per_rank
    mA = None
    for r in range(NCORE):
        o0 = r * n_per_rank
        mA = p.dma("gpsimd", lambda g, o0=o0, spl=spl: g.dma_start(
            out=wbf.ap()[o0:o0 + spl], in_=ext_full[o0:o0 + spl]),
            stream=tag + "c", slot=0)
    mB = mA
    if split is not None:
        for r in range(NCORE):
            o0 = r * n_per_rank + spl
            n = n_per_rank - spl
            mB = p.dma(
                "gpsimd", lambda g, o0=o0, n=n: g.dma_start(
                    out=wbf.ap()[o0:o0 + n], in_=ext_full[o0:o0 + n]),
                stream=tag + "c", slot=1)
    return wbf, [mA] * NCORE + [mB] * NCORE


def stage_weights_ag(p, nc, ext_flat, n_per_rank, tag):
    wbf = nc.dram_tensor(f"{tag}_bf", [n_per_rank], BF16, kind="Internal")
    wfull = nc.dram_tensor(f"{tag}_full", [NCORE * n_per_rank], BF16,
                           kind="Internal")
    CH = 4 * 1024 * 1024
    last = None
    for off in range(0, n_per_rank, CH):
        n = min(CH, n_per_rank - off)
        last = p.dma("gpsimd", lambda g, off=off, n=n: g.dma_start(
            out=wbf.ap()[off:off + n], in_=ext_flat[off:off + n]))
    mark = p.op("gpsimd", lambda g: g.collective_compute(
        "AllGather", ALU.bypass, replica_groups=[list(range(NCORE))],
        ins=[wbf.ap().opt()], outs=[wfull.ap().opt()]),
        wait=[last], inc="cc")
    return wfull, mark


class LNCtx:

    def __init__(self, nc, ctx, tag, T):
        sb = lambda n, s, d: ctx.enter_context(nc.sbuf_tensor(tag + n, s, d))
        self.T = T
        self.sqbf = sb("_sqbf", [128, T], BF16)
        self.tmp32 = sb("_tmp32", [128, T], F32)
        self.mbc = sb("_mbc", [128, T], F32)
        self.rbc = sb("_rbc", [128, T], F32)
        self.mrow = sb("_mrow", [1, T], F32)
        self.vrow = sb("_vrow", [1, T], F32)
        self.m2row = sb("_m2row", [1, T], F32)
        self.rrow = sb("_rrow", [1, T], F32)
        self.onesb = sb("_onesb", [128, 1], BF16)
        self.ones1f = sb("_ones1f", [1, 128], F32)
        self.init_done = False


_LN_UID = [0]


def ln_compute(p, nc, ctx, ln, xbuf, n_chunks, xload_mark, eps=EPS_LN):
    T = ln.T
    nsp = T // 512
    _LN_UID[0] += 1
    uid = _LN_UID[0]
    if not ln.init_done:
        p.op("vector", lambda v: v.memset(ln.onesb[:], 1.0), inc="dve")
        p.op("vector", lambda v: v.memset(ln.ones1f[:], 1.0), inc="dve")
        ln.init_done = True
    ones_mark = p.mark("dve")
    with ExitStack() as pctx:
        ps_sx = pctx.enter_context(
            nc.psum_tensor(f"lnps_sx{uid}", [1, T], F32))
        ps_sxx = pctx.enter_context(
            nc.psum_tensor(f"lnps_sxx{uid}", [1, T], F32))
        sq_marks = {}
        pe_kc = {}
        for kc in range(n_chunks):
            w = [xload_mark, ones_mark] if kc == 0 else [pe_kc.get(kc - 1)]
            sq_marks[kc] = p.op(
                "scalar", lambda s, kc=kc: s.activation(
                    out=ln.sqbf[:], in_=xbuf[:, kc, :], func=AF.Square),
                wait=[w_ for w_ in w if w_], inc="act")
            for tsp in range(nsp):
                tl = slice(tsp * 512, (tsp + 1) * 512)
                p.op("tensor", lambda t, kc=kc, tl=tl: t.matmul(
                    ps_sx[:, tl], ln.onesb[:], xbuf[:, kc, tl],
                    start=(kc == 0), stop=(kc == n_chunks - 1),
                    skip_group_check=True),
                    wait=[xload_mark, ones_mark] if kc == 0 and tsp == 0
                    else ())
                pe_kc[kc] = p.op("tensor", lambda t, kc=kc, tl=tl: t.matmul(
                    ps_sxx[:, tl], ln.onesb[:], ln.sqbf[:, tl],
                    start=(kc == 0), stop=(kc == n_chunks - 1),
                    skip_group_check=True),
                    wait=[sq_marks[kc]] if tsp == 0 else (),
                    inc="pe" if tsp == nsp - 1 else None)
        stats_pe = pe_kc[n_chunks - 1]
        n_el = 128 * n_chunks
        p.op("vector", lambda v: v.tensor_scalar_mul(
            ln.mrow[:], ps_sx[:], 1.0 / n_el), wait=[stats_pe], inc="dve")
        p.op("vector", lambda v: v.tensor_scalar_mul(
            ln.vrow[:], ps_sxx[:], 1.0 / n_el), inc="dve")
        p.op("vector", lambda v: v.tensor_mul(
            ln.m2row[:], ln.mrow[:], ln.mrow[:]), inc="dve")
        p.op("vector", lambda v: v.tensor_sub(
            ln.vrow[:], ln.vrow[:], ln.m2row[:]), inc="dve")
        rows_dve = p.op("vector", lambda v: v.tensor_scalar_add(
            ln.vrow[:], ln.vrow[:], float(eps)), inc="dve")
    p.op("scalar", lambda s: s.activation(
        out=ln.vrow[:], in_=ln.vrow[:], func=AF.Ln),
        wait=[rows_dve], inc="act")
    rstd_act = p.op("scalar", lambda s: s.activation(
        out=ln.rrow[:], in_=ln.vrow[:], func=AF.Exp, scale=-0.5), inc="act")
    with ExitStack() as pctx:
        ps_bc = pctx.enter_context(
            nc.psum_tensor(f"lnps_bc{uid}", [128, T], F32))
        ps_bc2 = pctx.enter_context(
            nc.psum_tensor(f"lnps_bc2{uid}", [128, T], F32))
        for tsp in range(nsp):
            tl = slice(tsp * 512, (tsp + 1) * 512)
            p.op("tensor", lambda t, tl=tl: t.matmul(
                ps_bc[:, tl], ln.ones1f[:], ln.mrow[:, tl],
                start=True, stop=True),
                wait=[rstd_act] if tsp == 0 else ())
            bc_pe = p.op("tensor", lambda t, tl=tl: t.matmul(
                ps_bc2[:, tl], ln.ones1f[:], ln.rrow[:, tl],
                start=True, stop=True),
                inc="pe" if tsp == nsp - 1 else None)
        p.op("vector", lambda v: v.tensor_copy(ln.mbc[:], ps_bc[:]),
             wait=[bc_pe], inc="dve")
        bc_dve = p.op("vector", lambda v: v.tensor_copy(ln.rbc[:], ps_bc2[:]),
                      inc="dve")
    return bc_dve


def ln_normalize(p, ln, xbuf, n_chunks, stats_mark, affine=None, extra=()):
    m = None
    for kc in range(n_chunks):
        p.op("vector", lambda v, kc=kc: v.tensor_sub(
            ln.tmp32[:], xbuf[:, kc, :], ln.mbc[:]),
            wait=[stats_mark] + list(extra) if kc == 0 else (), inc="dve")
        if affine is None:
            m = p.op("vector", lambda v, kc=kc: v.tensor_mul(
                xbuf[:, kc, :], ln.tmp32[:], ln.rbc[:]), inc="dve")
        else:
            wcol, bcol = affine
            p.op("vector", lambda v, kc=kc: v.tensor_mul(
                ln.tmp32[:], ln.tmp32[:], ln.rbc[:]), inc="dve")
            m = p.op("vector", lambda v, kc=kc: v.tensor_scalar(
                xbuf[:, kc, :], ln.tmp32[:],
                wcol[:, kc:kc + 1], bcol[:, kc:kc + 1],
                ALU.mult, ALU.add), inc="dve")
    return m


def load_xbuf(p, x_in, xbuf, t0=0, T=TLOC, n_split=4, wait=()):
    marks = []
    per = DC // n_split
    for i in range(n_split):
        marks.append(p.dma("gpsimd", lambda g, i=i: g.dma_start(
            out=xbuf[:, i * per:(i + 1) * per, :],
            in_=x_in.rearrange("(kc p) t -> p kc t", p=128)[
                :, i * per:(i + 1) * per, t0:t0 + T]),
            wait=wait if i == 0 else ()))
    return marks[-1]


def build_ffn0(nc, p, x_in, x_out, wflat_ext, h128_ext, tag="s1",
               staged=None):
    n_w1 = 8 * 16 * 128 * 128
    n_w2 = 2 * 64 * 128 * 128
    per_rank = 2 * n_w1 + n_w2
    if staged is None:
        wfull, wmark = stage_weights_ag(p, nc, wflat_ext, per_rank, tag + "w")
    else:
        wfull, wmark = staged
    wf = wfull.ap()
    wmk = (lambda r: wmark[r]) if isinstance(wmark, list) else (
        lambda r: wmark)
    w2mk = (lambda r: wmark[NCORE + r]) if isinstance(wmark, list) else (
        lambda r: wmark)

    def w1_chunk(hc, which):
        r, i = hc // 8, hc % 8
        off = r * per_rank + which * n_w1 + i * (16 * 128 * 128)
        return wf[off:off + 16 * 128 * 128].rearrange(
            "(p kc m) -> p kc m", p=128, kc=16)

    def w2_chunk(dc):
        r, i = dc // 2, dc % 2
        off = r * per_rank + 2 * n_w1 + i * (64 * 128 * 128)
        return wf[off:off + 64 * 128 * 128].rearrange(
            "(p cc m) -> p cc m", p=128, cc=64)

    uprime = nc.dram_tensor(tag + "_uprime", [4, 2, 128, HC, 256], BF16,
                            kind="Internal")

    with ExitStack() as ctx:
        sb = lambda n, s, d: ctx.enter_context(nc.sbuf_tensor(tag + n, s, d))
        xbuf = sb("_xbuf", [128, DC, TLOC], BF16)
        ln = LNCtx(nc, ctx, tag + "_ln", TLOC)
        h128 = sb("_h128", [128, 128], BF16)
        wb1 = [sb(f"_w1_{i}", [128, 16, 128], BF16) for i in range(2)]
        wb3 = [sb(f"_w3_{i}", [128, 16, 128], BF16) for i in range(2)]
        silu_t = [sb(f"_silu{i}", [128, 512], F32) for i in range(2)]
        ubf = [sb(f"_ubf{i}", [128, 512], BF16) for i in range(2)]
        upr = [sb(f"_upr_{i}", [128, 512], BF16) for i in range(2)]

        hm = p.dma("gpsimd", lambda g: g.dma_start(out=h128[:], in_=h128_ext),
                   stream="s1h")
        xm = load_xbuf(p, x_in, xbuf)
        lnm = ln_compute(p, nc, ctx, ln, xbuf, DC, xm)
        nm = ln_normalize(p, ln, xbuf, DC, lnm)

        with ExitStack() as pctx:
            ps_g = [pctx.enter_context(
                nc.psum_tensor(tag + f"psg{i}", [128, 512], F32))
                for i in (0, 1)]
            ps_l = [pctx.enter_context(
                nc.psum_tensor(tag + f"psl{i}", [128, 512], F32))
                for i in (0, 1)]
            ps_h = [pctx.enter_context(
                nc.psum_tensor(tag + f"psh{i}", [128, 512], F32))
                for i in (0, 1)]

            g_idx = 0
            silu_m = {}
            mul_m = {}
            evac_m = {}
            h128_m = {}
            stm_m = {}
            grp_last_pe = {}
            pending_h128 = None

            def emit_h128(g):
                h128_m[g] = p.op(
                    "tensor", lambda t, g=g: t.matmul(
                        ps_h[g % 2][:], h128[:], ubf[g % 2][:],
                        start=True, stop=True),
                    wait=[mul_m[g], evac_m.get(g - 2)]
                    + ([hm] if g == 0 else []),
                    inc="pe")
                ub = g % 2
                hc_, tsp_ = g // 4, g % 4
                evac_m[g] = p.op(
                    "vector", lambda v, ub=ub: v.tensor_scalar_mul(
                        upr[ub][:], ps_h[ub][:], FWHT_SCALE),
                    wait=[h128_m[g], stm_m.get(g - 2)], inc="dve")
                p.dma("gpsimd", lambda g_, hc_=hc_, tsp_=tsp_, ub=ub:
                      g_.dma_start(
                          out=uprime.ap()[tsp_, 0, :, hc_, :],
                          in_=upr[ub][:, 0:256]),
                      wait=[evac_m[g]],
                      stream="s1upst", slot=g % 2)
                stm_m[g] = p.dma(
                    "gpsimd", lambda g_, hc_=hc_, tsp_=tsp_, ub=ub:
                    g_.dma_start(
                        out=uprime.ap()[tsp_, 1, :, hc_, :],
                        in_=upr[ub][:, 256:512]),
                    wait=(), stream="s1upst", slot=g % 2)

            for hc in range(HC):
                wbs = hc % 2
                guard = ([wmk(hc // 8)] if hc < 2 else
                         [grp_last_pe.get((hc - 2, 3)), wmk(hc // 8)])
                m1 = p.dma("sync", lambda s, hc=hc, wbs=wbs: s.dma_start(
                    out=wb1[wbs][:], in_=w1_chunk(hc, 0)), wait=guard,
                    stream="s1w1", slot=hc % 2)
                m3 = p.dma("scalar", lambda s, hc=hc, wbs=wbs: s.dma_start(
                    out=wb3[wbs][:], in_=w1_chunk(hc, 1)), wait=guard,
                    stream="s1w3", slot=hc % 2)
                for tsp in range(4):
                    tl = slice(tsp * 512, (tsp + 1) * 512)
                    gb = g_idx % 2
                    w0 = [m1, nm] if g_idx == 0                         else [m1, silu_m.get(g_idx - 2)]
                    for kc in range(16):
                        p.op("tensor",
                             lambda t, kc=kc, wbs=wbs, tl=tl, gb=gb: t.matmul(
                                 ps_g[gb][:], wb1[wbs][:, kc, :],
                                 xbuf[:, kc, tl],
                                 start=(kc == 0), stop=(kc == 15)),
                             wait=w0 if kc == 0 else ())
                    w0 = [m3] if g_idx == 0 else [m3, mul_m.get(g_idx - 2)]
                    for kc in range(16):
                        lastmm = p.op(
                            "tensor",
                            lambda t, kc=kc, wbs=wbs, tl=tl, gb=gb: t.matmul(
                                ps_l[gb][:], wb3[wbs][:, kc, :],
                                xbuf[:, kc, tl],
                                start=(kc == 0), stop=(kc == 15)),
                            wait=w0 if kc == 0 else (),
                            inc="pe" if kc == 15 else None)
                    grp_last_pe[(hc, tsp)] = lastmm
                    silu_m[g_idx] = p.op(
                        "scalar", lambda s, gb=gb: s.activation(
                            out=silu_t[gb][:], in_=ps_g[gb][:],
                            func=AF.Silu),
                        wait=[lastmm, mul_m.get(g_idx - 2)], inc="act")
                    mul_m[g_idx] = p.op(
                        "vector", lambda v, gb=gb: v.tensor_mul(
                            ubf[gb][:], silu_t[gb][:], ps_l[gb][:]),
                        wait=[silu_m[g_idx], h128_m.get(g_idx - 2)],
                        inc="dve")
                    if pending_h128 is not None:
                        emit_h128(pending_h128)
                    pending_h128 = g_idx
                    g_idx += 1
            emit_h128(pending_h128)
            p.barrier()

    with ExitStack() as ctx:
        sb = lambda n, s, d: ctx.enter_context(nc.sbuf_tensor(tag + n, s, d))
        bfly = sb("_bfly", [128, 64, 256], F32)
        bflyb = sb("_bflyb", [128, 64, 256], BF16)
        w2b = [sb(f"_w2_{i}", [128, 64, 128], BF16) for i in range(2)]
        xres = sb("_xres", [128, 16, 256], F32)
        outb = sb("_outb", [128, 16, 256], F32)
        with ExitStack() as pctx:
            ps_o = [pctx.enter_context(
                nc.psum_tensor(tag + f"pso{i}", [128, 256], F32))
                for i in range(2)]

            cast_m = None
            store_m = None
            pb_pe = {}
            wload_m = {}
            evac_v = {}
            for ts2 in range(8):
                tsp, half = ts2 // 2, ts2 % 2
                lm = p.dma("gpsimd", lambda g, tsp=tsp, half=half: g.dma_start(
                    out=bfly[:], in_=uprime.ap()[tsp, half]),
                    wait=[cast_m] if cast_m else (), stream="s1upld")
                for st_ in range(6):
                    h = 1 << st_
                    nb = 64 // (2 * h)

                    def apair(v, h=h, nb=nb):
                        a = bfly[:].rearrange(
                            "p (b two w) t -> p b two w t", b=nb, two=2, w=h)
                        return v.tensor_add(a[:, :, 0], a[:, :, 0], a[:, :, 1])

                    def bpair(v, h=h, nb=nb):
                        a = bfly[:].rearrange(
                            "p (b two w) t -> p b two w t", b=nb, two=2, w=h)
                        return v.scalar_tensor_tensor(
                            a[:, :, 1], a[:, :, 1], -2.0, a[:, :, 0],
                            ALU.mult, ALU.add)

                    p.op("vector", apair,
                         wait=[lm] if st_ == 0 else (), inc="dve")
                    p.op("vector", bpair, inc="dve")
                cast_m = p.op("vector", lambda v: v.tensor_copy(
                    bflyb[:], bfly[:]),
                    wait=[pb_pe.get((ts2 - 1, 15))], inc="dve")
                xm2 = p.dma("sync", lambda s, ts2=ts2: s.dma_start(
                    out=xres[:],
                    in_=x_in.rearrange("(kc p) t -> p kc t", p=128)[
                        :, :, ts2 * 256:(ts2 + 1) * 256]),
                    wait=[evac_v.get((ts2 - 1, 15))], stream="s1xres")
                for dc in range(16):
                    wbs = dc % 2
                    prevkey = (ts2, dc - 2) if dc >= 2 else (ts2 - 1, 14 + dc)
                    wload_m[(ts2, dc)] = p.dma(
                        "scalar" if dc % 2 else "sync",
                        lambda s, dc=dc, wbs=wbs: s.dma_start(
                            out=w2b[wbs][:], in_=w2_chunk(dc)),
                        wait=[pb_pe.get(prevkey) or wmk(0), w2mk(dc // 2)],
                        stream="s1w2", slot=dc % 2)
                    w0 = [wload_m[(ts2, dc)], cast_m, evac_v.get(prevkey)]
                    for cc in range(64):
                        mmark = p.op(
                            "tensor",
                            lambda t, cc=cc, wbs=wbs, dc=dc: t.matmul(
                                ps_o[dc % 2][:], w2b[wbs][:, cc, :],
                                bflyb[:, cc, :],
                                start=(cc == 0), stop=(cc == 63)),
                            wait=w0 if cc == 0 else (),
                            inc="pe" if cc == 63 else None)
                    pb_pe[(ts2, dc)] = mmark
                    evac_v[(ts2, dc)] = p.op(
                        "vector", lambda v, dc=dc: v.scalar_tensor_tensor(
                            outb[:, dc, :], ps_o[dc % 2][:], 0.5,
                            xres[:, dc, :], ALU.mult, ALU.add),
                        wait=[mmark, xm2, store_m], inc="dve")
                store_m = p.dma("gpsimd", lambda g, ts2=ts2: g.dma_start(
                    out=x_out.rearrange("(kc p) t -> p kc t", p=128)[
                        :, :, ts2 * 256:(ts2 + 1) * 256],
                    in_=outb[:]), wait=[evac_v[(ts2, 15)]], stream="s1xst")
            p.barrier()
    return (wfull, wmark)


def prep_ffn0_weights(W1, W2, W3):
    t1 = tile_w_out_in(np.asarray(W1), HC, DC)
    t3 = tile_w_out_in(np.asarray(W3), HC, DC)
    t2 = tile_w_out_in(np.asarray(W2), DC, HC)
    slices = []
    for r in range(NCORE):
        s = np.concatenate([
            t1[8 * r:8 * r + 8].ravel(),
            t3[8 * r:8 * r + 8].ravel(),
            t2[2 * r:2 * r + 2].ravel()])
        slices.append(np.ascontiguousarray(s))
    return slices


def make_ffn0_launch():
    nc = bass.Bass("TRN2", target_bir_lowering=False, debug=False)
    ctx = ExitStack()
    x_in = nc.dram_tensor("x", [D, TLOC], F32, kind="ExternalInput").ap()
    per_rank = 2 * (8 * 16 * 128 * 128) + 2 * 64 * 128 * 128
    wflat = nc.dram_tensor("wflat", [per_rank], F32, kind="ExternalInput").ap()
    h128c = nc.dram_tensor("h128", [128, 128], BF16, kind="ExternalInput").ap()
    x_out = nc.dram_tensor("xout", [D, TLOC], F32, kind="ExternalOutput").ap()
    with ctx:
        p = Prog(nc, ctx)
        build_ffn0(nc, p, x_in, x_out, wflat, h128c)
        p.finalize([p.mark("d_gpsimd")])
    return nc


def run_ffn0(x_slices, W1, W2, W3):
    nc = make_ffn0_launch()
    wsl = prep_ffn0_weights(W1, W2, W3)
    h = hadamard128().astype(ml_dtypes.bfloat16)
    in_maps = [{"x": x_slices[c], "wflat": wsl[c], "h128": h}
               for c in range(NCORE)]
    res = run_bass_kernel_spmd(nc, in_maps, core_ids=list(range(NCORE)))
    return [res.results[c]["xout"] for c in range(NCORE)]


def build_attn0(nc, p, x_in, x_out, wflat_ext, consts, tag="s2",
                staged=None):
    NW1 = 2 * 128 * 16 * 128
    per_rank = 4 * NW1
    if staged is None:
        wfull, wmark = stage_weights_ag(p, nc, wflat_ext, per_rank, tag + "w")
    else:
        wfull, wmark = staged
    wf = wfull.ap()
    wmk = (lambda r: wmark[r]) if isinstance(wmark, list) else (
        lambda r: wmark)

    def w_chunk(widx, oc):
        r, i = oc // 2, oc % 2
        off = r * per_rank + widx * NW1 + i * (128 * 16 * 128)
        return wf[off:off + 128 * 16 * 128].rearrange(
            "(p kc m) -> p kc m", p=128, kc=16)

    sq_d = nc.dram_tensor(tag + "_sq", [16, 128, TLOC], BF16, kind="Internal")
    sk_d = nc.dram_tensor(tag + "_sk", [16, 128, TLOC], BF16, kind="Internal")
    sktm_d = nc.dram_tensor(tag + "_sktm", [16, 16, 128, 128], BF16,
                            kind="Internal")
    vvtm_d = nc.dram_tensor(tag + "_vvtm", [16, 16, 128, 128], BF16,
                            kind="Internal")
    adot_d = nc.dram_tensor(tag + "_adot", [16, 16, 128, 128], BF16,
                            kind="Internal")
    aq_d = nc.dram_tensor(tag + "_aq", [16, 4, 2, 128, 128], BF16,
                          kind="Internal")
    cc1_in = nc.dram_tensor(tag + "_cc1i", [128, 64], F32, kind="Internal")
    cc1_out = nc.dram_tensor(tag + "_cc1o", [NCORE * 128, 64], F32,
                             kind="Internal")
    cc2_in = nc.dram_tensor(tag + "_cc2i", [128, 4096], BF16, kind="Internal")
    cc2_out = nc.dram_tensor(tag + "_cc2o", [NCORE * 128, 4096], BF16,
                             kind="Internal")

    octx = ExitStack()
    osb = lambda n, s, d: octx.enter_context(nc.sbuf_tensor(tag + n, s, d))
    ident = osb("_id", [128, 128], BF16)
    colsum = osb("_colsum", [128, 64], F32)
    zall = osb("_zall", [128, 64], F32)
    zallbf = osb("_zallbf", [128, 64], BF16)

    idm = p.dma("gpsimd", lambda g: g.dma_start(
        out=ident[:], in_=consts["ident"]), stream="s2id")

    with ExitStack() as ctx:
        sb = lambda n, s, d: ctx.enter_context(nc.sbuf_tensor(tag + n, s, d))
        xbuf = sb("_xbuf", [128, DC, TLOC], BF16)
        masks = sb("_masks", [128, 4, 512], F32)
        mkm = p.dma("gpsimd", lambda g: g.dma_start(
            out=masks[:], in_=consts["masks"].rearrange("j p t -> p j t")),
            stream="s2mk")
        with ExitStack() as lctx:
            ln = LNCtx(nc, lctx, tag + "_ln", TLOC)
            xm = load_xbuf(p, x_in, xbuf)
            lnm = ln_compute(p, nc, lctx, ln, xbuf, DC, xm)
            nm = ln_normalize(p, ln, xbuf, DC, lnm)
        qh = [sb(f"_qh{i}", [128, TLOC], BF16) for i in range(2)]
        kh = [sb(f"_kh{i}", [128, TLOC], BF16) for i in range(2)]
        vh = [sb(f"_vh{i}", [128, TLOC], BF16) for i in range(2)]
        sqh = [sb(f"_sqh{i}", [128, TLOC], BF16) for i in range(2)]
        skh = [sb(f"_skh{i}", [128, TLOC], BF16) for i in range(2)]
        vvtm = [sb(f"_vvtmsb{i}", [128, 16, 128], BF16) for i in range(2)]
        t32a = sb("_t32a", [128, TLOC], F32)
        t32b = sb("_t32b", [128, TLOC], F32)
        scob = sb("_scob", [128, 512], F32)
        nmax = sb("_nmax", [128, 1], F32)
        nmaxs = sb("_nmaxs", [128, 1], F32)
        rsum = sb("_rsum", [128, 1], F32)
        rrec = sb("_rrec", [128, 1], F32)
        pnb = sb("_pnb", [128, 512], BF16)
        ptb = sb("_ptb", [128, 4, 128], BF16)
        tpe = sb("_tpe", [128, 128], BF16)
        ade = sb("_ade", [128, 128], BF16)
        wqb = [sb(f"_wq{i}", [128, 16, 128], BF16) for i in range(2)]
        wkb = [sb(f"_wk{i}", [128, 16, 128], BF16) for i in range(2)]
        wvb = [sb(f"_wv{i}", [128, 16, 128], BF16) for i in range(2)]

        with ExitStack() as pctx:
            ps_p = [pctx.enter_context(
                nc.psum_tensor(tag + f"psp{i}", [128, 512], F32))
                for i in (0, 1)]
            ps_s = pctx.enter_context(nc.psum_tensor(tag + "pss", [128, 512], F32))
            ps_t = pctx.enter_context(nc.psum_tensor(tag + "pst", [128, 128], BF16))
            ps_d = pctx.enter_context(nc.psum_tensor(tag + "psd", [128, 128], F32))

            pe_end = {}
            proj_evac = {}
            prev_proj_evac = [None, None]
            sq_stash_m = [None, None]
            sk_stash_m = [None, None]
            vvst_last = {}
            st = {"last_tp_evac": None, "last_adot_store": None,
                  "last_score_read": None}

            def stage1(h):
                hb = h % 2
                guard = ([wmk(h // 2), nm] if h < 2 else
                         [pe_end[h - 2], wmk(h // 2)])
                mwq = p.dma("sync", lambda s, h=h, hb=hb: s.dma_start(
                    out=wqb[hb][:], in_=w_chunk(0, h)), wait=guard,
                    stream="s2wq", slot=hb)
                mwk = p.dma("sync", lambda s, h=h, hb=hb: s.dma_start(
                    out=wkb[hb][:], in_=w_chunk(1, h)), wait=guard,
                    stream="s2wk", slot=hb)
                mwv = p.dma("sync", lambda s, h=h, hb=hb: s.dma_start(
                    out=wvb[hb][:], in_=w_chunk(2, h)), wait=guard,
                    stream="s2wv", slot=hb)
                for widx, (wb_, dst, wm_) in enumerate(
                        ((wqb, qh, mwq), (wkb, kh, mwk), (wvb, vh, mwv))):
                    for ts in range(4):
                        tl = slice(ts * 512, (ts + 1) * 512)
                        for kc in range(16):
                            mm = p.op(
                                "tensor",
                                lambda t, kc=kc, hb=hb, tl=tl, wb_=wb_:
                                t.matmul(ps_p[hb][:], wb_[hb][:, kc, :],
                                         xbuf[:, kc, tl],
                                         start=(kc == 0), stop=(kc == 15)),
                                wait=([wm_, nm, prev_proj_evac[hb]]
                                      if kc == 0 else ()),
                                inc="pe" if kc == 15 else None)
                        proj_evac[(h, widx, ts)] = prev_proj_evac[hb] = p.op(
                            "vector",
                            lambda v, dst=dst, tl=tl, hb=hb: v.tensor_copy(
                                dst[hb][:, tl], ps_p[hb][:]),
                            wait=[mm] + ([pe_end[h - 2]]
                                         if widx == 0 and ts == 0 and h >= 2
                                         else []),
                            inc="dve")

            def stage23(h):
                hb = h % 2
                qm = proj_evac[(h, 0, 3)]
                km = proj_evac[(h, 1, 3)]
                vm = proj_evac[(h, 2, 3)]

                def sigma(src, dst, wmk, stash_guard):
                    p.op("vector", lambda v, src=src: v.tensor_single_scalar(
                        t32a[:], src[hb][:], 0.0, ALU.min),
                        wait=[wmk], inc="dve")
                    am = p.op("scalar", lambda s: s.activation(
                        out=t32a[:], in_=t32a[:], func=AF.Exp),
                        wait=[p.mark("dve")], inc="act")
                    p.op("vector", lambda v, src=src: v.tensor_single_scalar(
                        t32b[:], src[hb][:], 0.0, ALU.max), inc="dve")
                    return p.op("vector", lambda v, dst=dst: v.tensor_add(
                        dst[hb][:], t32a[:], t32b[:]),
                        wait=[am, stash_guard], inc="dve")

                sqm = sigma(qh, sqh, qm, sq_stash_m[hb])
                skm = sigma(kh, skh, km, sk_stash_m[hb])
                sq_stash_m[hb] = p.dma("gpsimd", lambda g, h=h, hb=hb:
                                       g.dma_start(
                                           out=sq_d.ap()[h], in_=sqh[hb][:]),
                                       wait=[sqm], stream="s2sqst", slot=hb)
                sk_stash_m[hb] = p.dma("gpsimd", lambda g, h=h, hb=hb:
                                       g.dma_start(
                                           out=sk_d.ap()[h], in_=skh[hb][:]),
                                       wait=[skm], stream="s2skst", slot=hb)
                for sg in range(4):
                    p.op("vector", lambda v, h=h, sg=sg, hb=hb: v.reduce_sum(
                        colsum[:, h * 4 + sg:h * 4 + sg + 1],
                        skh[hb][:, sg * 512:(sg + 1) * 512], AXL.X),
                        wait=[skm] if sg == 0 else (), inc="dve")
                stm1 = None
                vvst_prev = vvst_last.get(h - 2)
                for tc in range(16):
                    t1 = p.op("tensor", lambda t, tc=tc, hb=hb: t.transpose(
                        ps_t[:], skh[hb][:, tc * 128:(tc + 1) * 128],
                        ident[:]),
                        wait=[skm, idm, st["last_tp_evac"]], inc="pe")
                    e1 = p.op("vector", lambda v: v.tensor_copy(
                        tpe[:], ps_t[:]), wait=[t1, stm1], inc="dve")
                    stm1 = p.dma("gpsimd", lambda g, h=h, tc=tc: g.dma_start(
                        out=sktm_d.ap()[h, tc], in_=tpe[:]), wait=[e1],
                        stream="s2tpst")
                    t2 = p.op("tensor", lambda t, tc=tc, hb=hb: t.transpose(
                        ps_t[:], vh[hb][:, tc * 128:(tc + 1) * 128],
                        ident[:]),
                        wait=[vm, e1], inc="pe")
                    e2 = p.op("vector", lambda v, tc=tc, hb=hb: v.tensor_copy(
                        vvtm[hb][:, tc, :], ps_t[:]),
                        wait=[t2, stm1, vvst_prev], inc="dve")
                    vvst_last[h] = p.dma(
                        "gpsimd", lambda g, h=h, tc=tc, hb=hb: g.dma_start(
                            out=vvtm_d.ap()[h, tc], in_=vvtm[hb][:, tc, :]),
                        wait=[e2], stream="s2vvst", slot=hb)
                    st["last_tp_evac"] = e2
                for sg in range(4):
                    sgl = slice(sg * 512, (sg + 1) * 512)
                    for sc in range(4):
                        scm = p.op(
                            "tensor", lambda t, sc=sc, sg=sg, sgl=sgl, hb=hb:
                            t.matmul(
                                ps_s[:],
                                qh[hb][:, sg * 512 + sc * 128:
                                       sg * 512 + (sc + 1) * 128],
                                kh[hb][:, sgl], start=True, stop=True),
                            wait=[qm, km, st["last_score_read"]], inc="pe")
                        p.op("vector", lambda v, sc=sc: v.tensor_add(
                            scob[:], ps_s[:], masks[:, sc, :]),
                            wait=[scm, mkm], inc="dve")
                        st["last_score_read"] = p.mark("dve")
                        p.op("vector", lambda v: v.tensor_reduce(
                            nmax[:], scob[:], AXL.X, ALU.max, negate=True),
                            inc="dve")
                        nsm = p.op("vector", lambda v: v.tensor_scalar_mul(
                            nmaxs[:], nmax[:], INV_SCALE_ATT), inc="dve")
                        em = p.op("scalar", lambda s: s.activation(
                            out=scob[:], in_=scob[:], func=AF.Exp,
                            bias=nmaxs[:], scale=INV_SCALE_ATT),
                            wait=[nsm], inc="act")
                        p.op("vector", lambda v: v.reduce_sum(
                            rsum[:], scob[:], AXL.X), wait=[em], inc="dve")
                        p.op("vector", lambda v: v.reciprocal(
                            rrec[:], rsum[:]), inc="dve")
                        pnm = p.op("vector", lambda v: v.tensor_scalar_mul(
                            pnb[:], scob[:], rrec[:]),
                            wait=[st["last_tp_evac"]], inc="dve")
                        for tb in range(4):
                            t3 = p.op(
                                "tensor", lambda t, tb=tb: t.transpose(
                                    ps_t[:], pnb[:, tb * 128:(tb + 1) * 128],
                                    ident[:]),
                                wait=[pnm, st["last_tp_evac"]], inc="pe")
                            st["last_tp_evac"] = p.op(
                                "vector", lambda v, tb=tb: v.tensor_copy(
                                    ptb[:, tb, :], ps_t[:]),
                                wait=[t3], inc="dve")
                        for tb in range(4):
                            admm = p.op(
                                "tensor",
                                lambda t, tb=tb, sg=sg, hb=hb: t.matmul(
                                    ps_d[:], ptb[:, tb, :],
                                    vvtm[hb][:, sg * 4 + tb, :],
                                    start=(tb == 0), stop=(tb == 3)),
                                wait=[st["last_tp_evac"],
                                      st["last_adot_store"]]
                                if tb == 0 else (),
                                inc="pe" if tb == 3 else None)
                        adm = p.op("vector", lambda v: v.tensor_copy(
                            ade[:], ps_d[:]), wait=[admm], inc="dve")
                        st["last_adot_store"] = p.dma(
                            "gpsimd", lambda g, h=h, sg=sg, sc=sc:
                            g.dma_start(out=adot_d.ap()[h, sg * 4 + sc],
                                        in_=ade[:]), wait=[adm],
                            stream="s2adst")
                pe_end[h] = p.milestone("tensor")

            stage1(0)
            for h in range(16):
                if h + 1 < 16:
                    stage1(h + 1)
                stage23(h)
            csm = p.dma("gpsimd", lambda g: g.dma_start(
                out=cc1_in.ap(), in_=colsum[:]), wait=[p.mark("dve")])
            ag1 = p.op("gpsimd", lambda g: g.collective_compute(
                "AllGather", ALU.bypass, replica_groups=[list(range(NCORE))],
                ins=[cc1_in.ap().opt()], outs=[cc1_out.ap().opt()]),
                wait=[csm], inc="cc")
        p.barrier()

    if _S2_STOP is not None and _S2_STOP != "b":
        return {}
    with ExitStack() as ctx:
        sb = lambda n, s, d: ctx.enter_context(nc.sbuf_tensor(tag + n, s, d))
        csall = sb("_csall", [128, 8, 64], F32)
        zmask = sb("_zmasksb", [128, 32], F32)
        onesr = sb("_onesr", [1, 128], F32)
        zmrow = sb("_zmrow", [1, 32], F32)
        skfm = sb("_skfm", [128, TLOC], BF16)
        sktm = sb("_sktmsb", [128, 16, 128], BF16)
        vvtm2 = sb("_vvtm2", [128, 16, 128], BF16)
        skdt = sb("_skdt", [128, 4, 128], BF16)
        dcols = sb("_dcols", [128, 16], F32)
        zt = sb("_zt", [128, 32], F32)
        zred = sb("_zred", [128, 1], F32)
        ab = sb("_ab", [128, 128], BF16)
        atb = sb("_atb", [128, 128], BF16)
        qb = sb("_qb", [128, 128], BF16)
        asg = sb("_asg", [128, 4, 128], BF16)
        qsg = sb("_qsg", [128, 4, 128], BF16)
        cmp32 = sb("_cmp32", [128, 128], F32)
        cmp32b = sb("_cmp32b", [128, 128], F32)

        om = p.op("vector", lambda v: v.memset(onesr[:], 1.0), inc="dve")
        zrm = p.dma("sync", lambda s: s.dma_start(
            out=zmrow[:], in_=consts["zmask"]), stream="s2zm")
        csm2 = p.dma("sync", lambda s: s.dma_start(
            out=csall[:], in_=cc1_out.ap().rearrange(
                "(r p) c -> p r c", p=128)), wait=[ag1], stream="s2cs")
        with ExitStack() as pctx:
            ps_b = pctx.enter_context(
                nc.psum_tensor(tag + "psb", [128, 32], F32))
            bm = p.op("tensor", lambda t: t.matmul(
                ps_b[:], onesr[:], zmrow[:], start=True, stop=True),
                wait=[om, zrm, csm2], inc="pe")
            zmm = p.op("vector", lambda v: v.tensor_copy(zmask[:], ps_b[:]),
                       wait=[bm], inc="dve")

        with ExitStack() as pctx:
            ps_z = pctx.enter_context(
                nc.psum_tensor(tag + "psz", [128, 16], F32))
            ps_a = pctx.enter_context(
                nc.psum_tensor(tag + "psa", [128, 128], F32))
            ps_q = pctx.enter_context(
                nc.psum_tensor(tag + "psq", [128, 128], F32))
            ps_c = pctx.enter_context(
                nc.psum_tensor(tag + "psc", [128, 128], F32))

            last_mm_2b = None
            last_dve_2b = None
            aqst_last = None
            pub_at_m = None
            pub_qb_m = None
            for h in range(16):
                lguard = [last_mm_2b, last_dve_2b] if h > 0 else []
                lm1 = p.dma("sync", lambda s, h=h: s.dma_start(
                    out=skfm[:], in_=sk_d.ap()[h]), wait=lguard,
                    stream="s2skfm")
                lm2 = p.dma("sync", lambda s, h=h: s.dma_start(
                    out=sktm[:], in_=sktm_d.ap()[h].rearrange(
                        "tc p m -> p tc m")), stream="s2sktmld")
                lm3 = p.dma("sync", lambda s, h=h: s.dma_start(
                    out=vvtm2[:], in_=vvtm_d.ap()[h].rearrange(
                        "tc p m -> p tc m")), stream="s2vvtmld")
                p.op("vector", lambda v, h=h: v.tensor_mul(
                    zt[:].rearrange("p (r s) -> p r s", r=8),
                    csall[:, :, h * 4:h * 4 + 4], zmask[:].rearrange(
                        "p (r s) -> p r s", r=8)),
                    wait=[csm2, zmm], inc="dve")
                p.op("vector", lambda v: v.reduce_sum(
                    zred[:], zt[:], AXL.X), inc="dve")
                p.op("vector", lambda v, h=h: v.tensor_scalar_add(
                    zall[:, h * 4:h * 4 + 1], zred[:], 1.0 / 128.0),
                    inc="dve")
                for s2 in range(1, 4):
                    p.op("vector", lambda v, h=h, s2=s2: v.tensor_add(
                        zall[:, h * 4 + s2:h * 4 + s2 + 1],
                        zall[:, h * 4 + s2 - 1:h * 4 + s2],
                        colsum[:, h * 4 + s2 - 1:h * 4 + s2]), inc="dve")
                p.op("vector", lambda v, h=h: v.tensor_copy(
                    zallbf[:, h * 4:h * 4 + 4], zall[:, h * 4:h * 4 + 4]),
                    inc="dve")
                zm = p.mark("dve")
                for j in range(16):
                    sg2 = j // 4
                    zc = p.op("tensor", lambda t, j=j, sg2=sg2, h=h: t.matmul(
                        ps_z[:, j:j + 1],
                        skfm[:, j * 128:(j + 1) * 128],
                        zallbf[:, h * 4 + sg2:h * 4 + sg2 + 1],
                        start=(j == 0), stop=(j == 15),
                        skip_group_check=True),
                        wait=[lm1, zm] if j == 0 else (),
                        inc="pe" if j == 15 else None)
                dcm = p.op("vector", lambda v: v.reciprocal(
                    dcols[:], ps_z[:]), wait=[zc], inc="dve")
                evacA = {}
                eq_prev = None
                aqst_prev = aqst_last
                for sg2 in range(4):
                    for tcw in range(4):
                        j = sg2 * 4 + tcw
                        p.op("vector", lambda v, j=j, tcw=tcw: v.tensor_scalar(
                            skdt[:, tcw, :], sktm[:, j, :],
                            dcols[:, j:j + 1], None, ALU.mult, ALU.bypass),
                            wait=[dcm, lm2] if j == 0 else (), inc="dve")
                    sdm = p.mark("dve")
                    for tcw in range(4):
                        j = sg2 * 4 + tcw
                        am = p.op("tensor", lambda t, j=j, tcw=tcw: t.matmul(
                            ps_a[:], sktm[:, j, :], skdt[:, tcw, :],
                            start=(tcw == 0), stop=(tcw == 3)),
                            wait=[sdm] if tcw == 0 else (),
                            inc="pe" if tcw == 3 else None)
                    for tcw in range(4):
                        j = sg2 * 4 + tcw
                        qmm = p.op("tensor", lambda t, j=j: t.matmul(
                            ps_q[:], sktm[:, j, :], vvtm2[:, j, :],
                            start=(j % 4 == 0), stop=(j % 4 == 3)),
                            wait=([lm3] if sg2 == 0 else [eq_prev])
                            if tcw == 0 else (),
                            inc="pe" if j % 4 == 3 else None)
                    ea = p.op("vector", lambda v, sg2=sg2: v.tensor_copy(
                        asg[:, sg2, :], ps_a[:]),
                        wait=[am, aqst_prev], inc="dve")
                    eq = p.op("vector", lambda v, sg2=sg2: v.tensor_copy(
                        qsg[:, sg2, :], ps_q[:]),
                        wait=[qmm, aqst_prev], inc="dve")
                    evacA[sg2] = (ea, eq)
                    eq_prev = eq
                    p.dma(
                        "gpsimd", lambda g, h=h, sg2=sg2: g.dma_start(
                            out=aq_d.ap()[h, sg2, 0], in_=asg[:, sg2, :]),
                        wait=[ea], stream="s2aqst")
                    aqst_last = p.dma(
                        "gpsimd", lambda g, h=h, sg2=sg2: g.dma_start(
                            out=aq_d.ap()[h, sg2, 1], in_=qsg[:, sg2, :]),
                        wait=[eq], stream="s2aqst")
                c0 = p.op("vector", lambda v: v.tensor_copy(
                    ab[:], asg[:, 0, :]), wait=[evacA[0][0]], inc="dve")
                p.op("vector", lambda v: v.tensor_copy(
                    atb[:], asg[:, 0, :]), wait=[pub_at_m], inc="dve")
                cm = p.op("vector", lambda v: v.tensor_copy(
                    qb[:], qsg[:, 0, :]),
                    wait=[evacA[0][1], pub_qb_m], inc="dve")
                for s2 in range(1, 4):
                    m1 = p.op("tensor", lambda t, s2=s2: t.matmul(
                        ps_c[:], asg[:, s2, :], ab[:],
                        start=True, stop=True),
                        wait=[cm, evacA[s2][0], evacA[s2][1]], inc="pe")
                    m2 = p.op("tensor", lambda t, s2=s2: t.matmul(
                        ps_a[:], ab[:], asg[:, s2, :],
                        start=True, stop=True), inc="pe")
                    m3 = p.op("tensor", lambda t, s2=s2: t.matmul(
                        ps_q[:], asg[:, s2, :], qb[:],
                        start=True, stop=True), inc="pe")
                    p.op("vector", lambda v, s2=s2: v.tensor_add(
                        cmp32[:], asg[:, s2, :], ab[:]), wait=[m3], inc="dve")
                    p.op("vector", lambda v: v.tensor_sub(
                        ab[:], cmp32[:], ps_c[:]), inc="dve")
                    p.op("vector", lambda v, s2=s2: v.tensor_add(
                        cmp32b[:], asg[:, s2, :], atb[:]), inc="dve")
                    p.op("vector", lambda v: v.tensor_sub(
                        atb[:], cmp32b[:], ps_a[:]), inc="dve")
                    p.op("vector", lambda v, s2=s2: v.tensor_add(
                        cmp32[:], qsg[:, s2, :], qb[:]), inc="dve")
                    cm = p.op("vector", lambda v: v.tensor_sub(
                        qb[:], cmp32[:], ps_q[:]), inc="dve")
                    last_mm_2b = m3
                pub_at_m = p.dma("gpsimd", lambda g, h=h: g.dma_start(
                    out=cc2_in.ap()[:, h * 256:h * 256 + 128], in_=atb[:]),
                    wait=[cm], stream="s2pub", slot=0)
                last_dve_2b = cm
                pub_qb_m = p.dma("gpsimd", lambda g, h=h: g.dma_start(
                    out=cc2_in.ap()[:, h * 256 + 128:h * 256 + 256],
                    in_=qb[:]), wait=(), stream="s2pub", slot=1)
            ag2 = p.op("gpsimd", lambda g: g.collective_compute(
                "AllGather", ALU.bypass, replica_groups=[list(range(NCORE))],
                ins=[cc2_in.ap().opt()], outs=[cc2_out.ap().opt()]),
                wait=[pub_at_m, pub_qb_m], inc="cc")
        p.barrier()

    if _S2_STOP == "b":
        return {}
    with ExitStack() as ctx:
        sb = lambda n, s, d: ctx.enter_context(nc.sbuf_tensor(tag + n, s, d))
        wob = sb("_wob", [128, 16, 16, 128], BF16)
        attout = sb("_attout", [128, 16, TLOC], BF16)
        smask = sb("_smasksb", [128, 8], F32)
        smrow = sb("_smrow", [1, 8], F32)
        onesr2 = sb("_onesr2", [1, 128], F32)
        betar = sb("_betar", [1, 2048], F32)
        betab = sb("_betab", [128, 16, 128], F32)
        agb = sb("_agb", [128, 8, 256], BF16)
        sqh2 = sb("_sqh2", [128, TLOC], BF16)
        adb = sb("_adb", [128, 16, 128], BF16)
        asg2 = sb("_asg2", [128, 4, 128], BF16)
        qsg2 = sb("_qsg2", [128, 4, 128], BF16)
        m32 = sb("_m32", [128, 128], F32)
        mbf = sb("_mbf", [128, 128], BF16)
        qd = sb("_qd", [128, 16], F32)
        am32 = sb("_am32", [128, 128], F32)
        t3b = sb("_t3b", [128, 128], F32)
        attb = sb("_attb", [128, 128], BF16)
        attfm = sb("_attfm", [128, TLOC], BF16)
        xres2 = sb("_xres2", [128, 512], F32)
        outb2 = sb("_outb2", [128, 512], F32)

        om2 = p.op("vector", lambda v: v.memset(onesr2[:], 1.0), inc="dve")
        smm = p.dma("sync", lambda s: s.dma_start(
            out=smrow[:], in_=consts["smask"]), stream="s2sm")
        btm = p.dma("sync", lambda s: s.dma_start(
            out=betar[:], in_=consts["betas"]), stream="s2bt")
        btsig = p.op("scalar", lambda s: s.activation(
            out=betar[:], in_=betar[:], func=AF.Sigmoid),
            wait=[btm], inc="act")
        wo_marks = []
        for oc in range(16):
            wo_marks.append(p.dma(
                "scalar", lambda s, oc=oc: s.dma_start(
                    out=wob[:, oc, :, :], in_=w_chunk(3, oc)),
                wait=[wmk(oc // 2)]))
        wom = wo_marks[-1]
        with ExitStack() as pctx:
            ps_bc = pctx.enter_context(
                nc.psum_tensor(tag + "psbc", [128, 128], F32))
            bmk = None
            for hh in range(16):
                bmk = p.op("tensor", lambda t, hh=hh: t.matmul(
                    ps_bc[:], onesr2[:], betar[:, hh * 128:(hh + 1) * 128],
                    start=True, stop=True),
                    wait=[om2, btsig] if hh == 0 else [p.mark("dve")],
                    inc="pe")
                p.op("vector", lambda v, hh=hh: v.tensor_copy(
                    betab[:, hh, :], ps_bc[:]), wait=[bmk], inc="dve")
            smb = p.op("tensor", lambda t: t.matmul(
                ps_bc[:, 0:8], onesr2[:], smrow[:], start=True, stop=True),
                wait=[smm, p.mark("dve")], inc="pe")
            p.op("vector", lambda v: v.tensor_copy(smask[:], ps_bc[:, 0:8]),
                 wait=[smb], inc="dve")

        with ExitStack() as pctx:
            ps_f = pctx.enter_context(
                nc.psum_tensor(tag + "psf", [128, 128], F32))
            ps_zq = pctx.enter_context(
                nc.psum_tensor(tag + "pszq", [128, 16], F32))
            ps_m = pctx.enter_context(
                nc.psum_tensor(tag + "psm", [128, 128], F32))
            ps_t2 = pctx.enter_context(
                nc.psum_tensor(tag + "pst2", [128, 128], BF16))
            ps_w = [pctx.enter_context(
                nc.psum_tensor(tag + f"psw{i}", [128, 128], F32))
                for i in range(2)]

            pe_prev = None
            dve_prev = None
            for h in range(16):
                lg = [pe_prev, dve_prev] if h > 0 else [ag2]
                lag = p.dma("sync", lambda s, h=h: s.dma_start(
                    out=agb[:], in_=cc2_out.ap().rearrange(
                        "(r p) c -> p r c", p=128)[
                        :, :, h * 256:(h + 1) * 256]),
                    wait=lg + [ag2] if h == 0 else lg)
                lsq = p.dma("sync", lambda s, h=h: s.dma_start(
                    out=sqh2[:], in_=sq_d.ap()[h]))
                lad = p.dma("sync", lambda s, h=h: s.dma_start(
                    out=adb[:], in_=adot_d.ap()[h].rearrange(
                        "tc p m -> p tc m")))
                laq = p.dma("sync", lambda s, h=h: s.dma_start(
                    out=asg2[:], in_=aq_d.ap()[h, :, 0].rearrange(
                        "sg p m -> p sg m")))
                laq2 = p.dma("sync", lambda s, h=h: s.dma_start(
                    out=qsg2[:], in_=aq_d.ap()[h, :, 1].rearrange(
                        "sg p m -> p sg m")))
                hlm = p.mark("d_sync")
                mm0 = p.op("vector", lambda v: v.memset(m32[:], 0.0),
                           wait=[pe_prev] if h > 0 else [], inc="dve")
                mmb = p.op("vector", lambda v: v.memset(mbf[:], 0.0),
                           inc="dve")
                for j in range(NCORE):
                    fm = p.op("tensor", lambda t, j=j: t.matmul(
                        ps_f[:], agb[:, j, 0:128], mbf[:],
                        start=True, stop=True),
                        wait=[hlm, mmb] if j == 0 else [mmb], inc="pe")
                    p.op("vector", lambda v, j=j: v.tensor_sub(
                        t3b[:], agb[:, j, 128:256], ps_f[:]),
                        wait=[fm], inc="dve")
                    p.op("vector", lambda v, j=j: v.scalar_tensor_tensor(
                        m32[:], t3b[:], smask[:, j:j + 1], m32[:],
                        ALU.mult, ALU.add), inc="dve")
                    mmb = p.op("vector", lambda v: v.tensor_copy(
                        mbf[:], m32[:]), inc="dve")
                for j in range(16):
                    sg2 = j // 4
                    zq = p.op("tensor", lambda t, j=j, sg2=sg2, h=h: t.matmul(
                        ps_zq[:, j:j + 1],
                        sqh2[:, j * 128:(j + 1) * 128],
                        zallbf[:, h * 4 + sg2:h * 4 + sg2 + 1],
                        start=(j == 0), stop=(j == 15),
                        skip_group_check=True),
                        wait=[hlm] if j == 0 else (),
                        inc="pe" if j == 15 else None)
                qdm = p.op("vector", lambda v: v.reciprocal(
                    qd[:], ps_zq[:]), wait=[zq], inc="dve")
                tpev_m = None
                amev = None
                for sg2 in range(4):
                    for tcw in range(4):
                        j = sg2 * 4 + tcw
                        amm = p.op("tensor", lambda t, j=j: t.matmul(
                            ps_m[:], sqh2[:, j * 128:(j + 1) * 128], mbf[:],
                            start=True, stop=True),
                            wait=[mmb, qdm, amev] if j == 0
                            else [mmb, amev], inc="pe")
                        amev = p.op("vector", lambda v, j=j: v.tensor_scalar(
                            am32[:], ps_m[:], qd[:, j:j + 1], None,
                            ALU.mult, ALU.bypass), wait=[amm, hlm],
                            inc="dve")
                        p.op("vector", lambda v, j=j: v.tensor_sub(
                            t3b[:], am32[:], adb[:, j, :]), inc="dve")
                        p.op("vector", lambda v, h=h: v.tensor_mul(
                            t3b[:], t3b[:], betab[:, h, :]), inc="dve")
                        abm = p.op("vector", lambda v, j=j: v.tensor_add(
                            attb[:], t3b[:], adb[:, j, :]),
                            wait=[tpev_m], inc="dve")
                        tpm = p.op("tensor", lambda t, j=j: t.transpose(
                            ps_t2[:], attb[:], ident[:]),
                            wait=[abm], inc="pe")
                        tpev_m = p.op("vector", lambda v, j=j: v.tensor_copy(
                            attfm[:, j * 128:(j + 1) * 128], ps_t2[:]),
                            wait=[tpm], inc="dve")
                    if sg2 < 3:
                        um = p.op("tensor", lambda t, sg2=sg2: t.matmul(
                            ps_f[:], asg2[:, sg2, :], mbf[:],
                            start=True, stop=True),
                            wait=[hlm] if sg2 == 0 else (), inc="pe")
                        p.op("vector", lambda v, sg2=sg2: v.tensor_sub(
                            t3b[:], qsg2[:, sg2, :], ps_f[:]),
                            wait=[um], inc="dve")
                        p.op("vector", lambda v: v.tensor_add(
                            m32[:], m32[:], t3b[:]), inc="dve")
                        mmb = p.op("vector", lambda v: v.tensor_copy(
                            mbf[:], m32[:]), inc="dve")
                att_r = attfm[:].rearrange(
                    "p (sg sb k) -> p k sg sb", sg=4, sb=32, k=16)
                out_r = attout[:].rearrange(
                    "p dc (sg q sb) -> p dc sg q sb", sg=4, q=16, sb=32)
                wo_ev = None
                for dc in range(16):
                    for kc in range(16):
                        wmm = p.op(
                            "tensor", lambda t, dc=dc, kc=kc: t.matmul(
                                ps_w[dc % 2][:].rearrange(
                                    "p (sg sb) -> p sg sb", sg=4),
                                wob[:, dc, kc, :], att_r[:, kc],
                                start=(kc == 0), stop=(kc == 15)),
                            wait=([tpev_m, wom] if dc == 0 else
                                  [p.mark("dve")]) if kc == 0 else (),
                            inc="pe" if kc == 15 else None)
                    wo_ev = p.op("vector", lambda v, dc=dc, h=h: v.tensor_copy(
                        out_r[:, dc, :, h, :],
                        ps_w[dc % 2][:].rearrange("p (sg sb) -> p sg sb",
                                                  sg=4)),
                        wait=[wmm], inc="dve")
                pe_prev = p.milestone("tensor")
                dve_prev = wo_ev
            som = None
            for dc in range(16):
                for ts in range(4):
                    tl = slice(ts * 512, (ts + 1) * 512)
                    xm3 = p.dma("sync", lambda s, dc=dc, tl=tl: s.dma_start(
                        out=xres2[:],
                        in_=x_in.rearrange("(kc p) t -> p kc t", p=128)[
                            :, dc, tl]),
                        wait=[som], stream="s2xres")
                    aom = p.op("vector", lambda v, dc=dc, tl=tl: v.tensor_add(
                        outb2[:], attout[:, dc, tl], xres2[:]),
                        wait=[xm3, dve_prev], inc="dve")
                    som = p.dma("gpsimd", lambda g, dc=dc, tl=tl: g.dma_start(
                        out=x_out.rearrange("(kc p) t -> p kc t", p=128)[
                            :, dc, tl],
                        in_=outb2[:]), wait=[aom], stream="s2xst")
        p.barrier()
    octx.close()
    return {}


def prep_attn0_weights(Wq, Wk, Wv, Wo):
    ts_ = [tile_w_in_out(np.asarray(W), 16, 16) for W in (Wq, Wk, Wv, Wo)]
    slices = []
    for r in range(NCORE):
        s = np.concatenate([t[2 * r:2 * r + 2].ravel() for t in ts_])
        slices.append(np.ascontiguousarray(s))
    return slices


def make_attn0_launch():
    nc = bass.Bass("TRN2", target_bir_lowering=False, debug=False)
    ctx = ExitStack()
    x_in = nc.dram_tensor("x", [D, TLOC], F32, kind="ExternalInput").ap()
    per_rank = 4 * 2 * 128 * 16 * 128
    wflat = nc.dram_tensor("wflat", [per_rank], F32, kind="ExternalInput").ap()
    ident = nc.dram_tensor("ident", [128, 128], BF16, kind="ExternalInput").ap()
    masks = nc.dram_tensor("masks", [4, 128, 512], F32, kind="ExternalInput").ap()
    zmask = nc.dram_tensor("zmask", [1, 32], F32, kind="ExternalInput").ap()
    smask = nc.dram_tensor("smask", [1, 8], F32, kind="ExternalInput").ap()
    betas = nc.dram_tensor("betas", [1, 2048], F32, kind="ExternalInput").ap()
    x_out = nc.dram_tensor("xout", [D, TLOC], F32, kind="ExternalOutput").ap()
    consts = {"ident": ident, "masks": masks, "zmask": zmask, "smask": smask,
              "betas": betas}
    with ctx:
        p = Prog(nc, ctx)
        build_attn0(nc, p, x_in, x_out, wflat, consts)
        p.finalize([p.mark("d_gpsimd")])
    return nc


def core_masks():
    zs, ss = [], []
    for c in range(NCORE):
        b, qt = c // 4, c % 4
        zm = np.zeros((1, 32), np.float32)
        sm = np.zeros((1, 8), np.float32)
        for r in range(NCORE):
            if r // 4 != b:
                continue
            if r % 4 < qt:
                sm[0, r] = 1.0
            for sg in range(4):
                if 4 * (r % 4) + sg < 4 * qt:
                    zm[0, r * 4 + sg] = 1.0
        zs.append(zm)
        ss.append(sm)
    return zs, ss


def run_attn0(x_slices, Wq, Wk, Wv, Wo, betas):
    nc = make_attn0_launch()
    wsl = prep_attn0_weights(Wq, Wk, Wv, Wo)
    idb = np.eye(128, dtype=np.float32).astype(ml_dtypes.bfloat16)
    msk = causal_masks()
    zs, ss = core_masks()
    brow = np.ascontiguousarray(
        np.asarray(betas).reshape(16, 128).reshape(1, 2048).astype(np.float32))
    in_maps = [{"x": x_slices[c], "wflat": wsl[c], "ident": idb,
                "masks": msk, "zmask": zs[c], "smask": ss[c], "betas": brow}
               for c in range(NCORE)]
    res = run_bass_kernel_spmd(nc, in_maps, core_ids=list(range(NCORE)))
    return [res.results[c]["xout"] for c in range(NCORE)]


N_GU_OT = 11
N_DN_OT = 2
GU_ELEMS = N_GU_OT * 16 * 128 * 128
DN_ELEMS = N_DN_OT * 44 * 128 * 128
FFN1_PER_RANK = GU_ELEMS + DN_ELEMS


def dequant_block(nc, p, ctx, tag, pk_ext, sc_ext, n_ot, n_kc, wc_out,
                  wc_base, ident, idm):
    W = n_kc * 16
    sb = lambda n, s, d: ctx.enter_context(nc.sbuf_tensor(tag + n, s, d))
    pk = sb("_pk", [128, W], I32)
    m3i = sb("_m3i", [128, W], I32)
    sfi = sb("_sfi", [128, W], I32)
    m3f = sb("_m3f", [128, W], F32)
    sff = sb("_sff", [128, W], F32)
    t1b = sb("_t1", [128, W], F32)
    t2b = sb("_t2", [128, W], F32)
    vb = sb("_v", [128, W], F32)
    fb = sb("_f", [128, W], F32)
    dq = sb("_dq", [128, n_ot, n_kc * 128], F32)
    scb = sb("_scb", [128, n_ot * 128], F32)
    cvt = sb("_cvt", [128, 128], BF16)
    wt = sb("_wt", [128, n_ot, 128], BF16)
    with ExitStack() as pctx:
        ps_t = pctx.enter_context(nc.psum_tensor(tag + "pst", [128, 128], BF16))
        pk_m = None
        dq_m = None
        for ot in range(n_ot):
            pk_m = p.dma("sync", lambda s, ot=ot: s.dma_start(
                out=pk[:], in_=pk_ext[ot]),
                wait=[dq_m] if dq_m else [], stream=tag + "pk")
            for k in range(8):
                p.op("vector", lambda v, k=k: v.tensor_scalar(
                    m3i[:], pk[:], 4 * k, 7,
                    ALU.logical_shift_right, ALU.bitwise_and),
                    wait=[pk_m], inc="dve")
                p.op("vector", lambda v, k=k: v.tensor_scalar(
                    sfi[:], pk[:], 4 * k + 3, 1,
                    ALU.logical_shift_right, ALU.bitwise_and), inc="dve")
                p.op("vector", lambda v: v.tensor_copy(m3f[:], m3i[:]),
                     inc="dve")
                p.op("vector", lambda v: v.tensor_copy(sff[:], sfi[:]),
                     inc="dve")
                p.op("vector", lambda v: v.tensor_scalar(
                    t1b[:], m3f[:], -4.0, 0.0, ALU.add, ALU.max), inc="dve")
                p.op("vector", lambda v: v.tensor_scalar(
                    t2b[:], m3f[:], -6.0, 0.0, ALU.add, ALU.max), inc="dve")
                p.op("vector", lambda v: v.tensor_add(
                    vb[:], m3f[:], t1b[:]), inc="dve")
                p.op("vector", lambda v: v.scalar_tensor_tensor(
                    vb[:], t2b[:], 2.0, vb[:], ALU.mult, ALU.add), inc="dve")
                p.op("vector", lambda v: v.tensor_scalar(
                    fb[:], sff[:], -0.5, 0.25, ALU.mult, ALU.add), inc="dve")
                dq_m = p.op(
                    "vector", lambda v, ot=ot, k=k: v.tensor_mul(
                        dq[:, ot, :].rearrange("p (w e) -> p w e", e=8)
                        [:, :, k], vb[:], fb[:]), inc="dve")
        st_m = None
        last_ev = None
        for kc in range(n_kc):
            sc_m = p.dma("gpsimd", lambda g, kc=kc: g.dma_start(
                out=scb[:],
                in_=bass.AP(sc_ext.tensor,
                            sc_ext.offset + kc * (n_ot * 128),
                            [[0, 128], [1, n_ot * 128]])),
                wait=[st_m] if st_m else [dq_m],
                stream=tag + "sc", slot=kc % 2)
            for ot in range(n_ot):
                cm = p.op("vector", lambda v, kc=kc, ot=ot: v.tensor_copy(
                    cvt[:], dq[:, ot, kc * 128:(kc + 1) * 128]),
                    wait=[dq_m, last_ev], inc="dve")
                tm = p.op("tensor", lambda t: t.transpose(
                    ps_t[:], cvt[:], ident[:]),
                    wait=[cm, idm, last_ev], inc="pe")
                last_ev = p.op(
                    "vector", lambda v, kc=kc, ot=ot: v.tensor_tensor(
                        wt[:, ot, :], ps_t[:],
                        scb[:, ot * 128:(ot + 1) * 128], ALU.mult),
                    wait=[tm, sc_m], inc="dve")
            st_m = p.dma("gpsimd", lambda g, kc=kc: g.dma_start(
                out=bass.AP(wc_out.tensor,
                            wc_out.offset + wc_base + kc * 128,
                            [[n_kc * 128, 128], [n_kc * 16384, n_ot],
                             [1, 128]]),
                in_=wt[:]), wait=[last_ev], stream=tag + "st", slot=kc % 2)
        return st_m


def stage_ffn1_weights(nc, p, ins, tag="s4d"):
    cc_in = nc.dram_tensor(tag + "_cci", [FFN1_PER_RANK], BF16,
                           kind="Internal")
    cc_out = nc.dram_tensor(tag + "_cco", [NCORE * FFN1_PER_RANK], BF16,
                            kind="Internal")
    with ExitStack() as octx:
        ident = octx.enter_context(
            nc.sbuf_tensor(tag + "_id", [128, 128], BF16))
        idm = p.dma("gpsimd", lambda g: g.dma_start(
            out=ident[:], in_=ins["ident"]), stream=tag + "id")
        with ExitStack() as ctx:
            m1 = dequant_block(nc, p, ctx, tag + "g", ins["gu_pk"],
                               ins["gu_sc"], N_GU_OT, 16, cc_in.ap(), 0,
                               ident, idm)
        p.barrier(exclude=("cc",))
        with ExitStack() as ctx:
            m2 = dequant_block(nc, p, ctx, tag + "d", ins["dn_pk"],
                               ins["dn_sc"], N_DN_OT, 44, cc_in.ap(),
                               GU_ELEMS, ident, idm)
        agm = p.op("gpsimd", lambda g: g.collective_compute(
            "AllGather", ALU.bypass,
            replica_groups=[list(range(NCORE))],
            ins=[cc_in.ap().opt()], outs=[cc_out.ap().opt()]),
            wait=[m1, m2], inc="cc")
        p.barrier(exclude=("cc",))
    return (cc_out, agm)


def build_ffn1(nc, p, x_in, x_out, ins, tag="s4", staged=None):
    if staged is None:
        staged = stage_ffn1_weights(nc, p, ins, tag + "d")
    cc_out = staged[0]
    cco = cc_out.ap()

    def gu_chunkset(oc):
        r, loc = oc // N_GU_OT, oc % N_GU_OT
        off = r * FFN1_PER_RANK + loc * (16 * 16384)
        return bass.AP(cco.tensor, cco.offset + off,
                       [[16 * 128, 128], [128, 16], [1, 128]])

    def dn_chunkset(oc):
        r, loc = oc // N_DN_OT, oc % N_DN_OT
        off = r * FFN1_PER_RANK + GU_ELEMS + loc * (44 * 16384)
        return bass.AP(cco.tensor, cco.offset + off,
                       [[44 * 128, 128], [128, 44], [1, 128]])

    agm = staged[1]
    with ExitStack() as octx:
        p.barrier()

        with ExitStack() as ctx:
            sb = lambda n, s, d: ctx.enter_context(
                nc.sbuf_tensor(tag + n, s, d))
            xs_ = sb("_xs", [128, DC, 1024], BF16)
            hbuf = sb("_hb", [128, 44, 1024], BF16)
            ybf = xs_
            ln = LNCtx(nc, ctx, tag + "_ln", 1024)
            sqb = ln.sqbf
            silu_t = sb("_silu", [128, 512], F32)
            rmsrow = ln.m2row
            rb = ln.mbc
            tmp32 = ln.tmp32
            xres = ln.rbc
            ones1f = sb("_o1f", [1, 128], F32)
            onesb = sb("_o1b", [128, 1], BF16)
            wg_ = [sb(f"_wg{i}", [128, 16, 128], BF16) for i in range(2)]
            wu_ = [sb(f"_wu{i}", [128, 16, 128], BF16) for i in range(2)]
            wdb = [sb("_wd0", [128, 44, 128], BF16)] * 2
            p.op("vector", lambda v: v.memset(ones1f[:], 1.0), inc="dve")
            om = p.op("vector", lambda v: v.memset(onesb[:], 1.0), inc="dve")

            def bcast_row(row_ap, dst, ps_c, wait):
                b1 = p.op("tensor", lambda t: t.matmul(
                    ps_c[:, 0:512], ones1f[:], row_ap[:, 0:512],
                    start=True, stop=True), wait=wait, inc="pe")
                b2 = p.op("tensor", lambda t: t.matmul(
                    ps_c[:, 512:1024], ones1f[:], row_ap[:, 512:1024],
                    start=True, stop=True), inc="pe")
                return p.op("vector", lambda v: v.tensor_copy(
                    dst[:], ps_c[:]), wait=[b2], inc="dve")

            if True:
                st_m = None
                hm_prev = None
                for tsp in range(2):
                    t0 = tsp * 1024
                    xm2 = load_xbuf(p, x_in, xs_, t0=t0, T=1024, n_split=2,
                                    wait=[p.mark("pe"), p.mark("dve")]
                                    if tsp else ())
                    lnm = ln_compute(p, nc, ctx, ln, xs_, DC, xm2)
                    nmk = ln_normalize(p, ln, xs_, DC, lnm,
                                       extra=[st_m] if st_m else [])
                    pctx = ExitStack()
                    psum = lambda n, s, d: pctx.enter_context(
                        nc.psum_tensor(tag + n + str(tsp), s, d))
                    ps_g = [psum(f"psg{i}", [128, 512], F32) for i in (0, 1)]
                    ps_u = [psum(f"psu{i}", [128, 512], F32) for i in (0, 1)]
                    ps_c = psum("psc", [128, 1024], F32)
                    ps_s = psum("pss", [1, 1024], F32)
                    hmj = {}
                    slj = {}
                    for j in range(44):
                        jb = j % 2
                        gl = p.dma("scalar", lambda s, j=j, jb=jb: s.dma_start(
                            out=wg_[jb][:], in_=gu_chunkset(j)),
                            wait=[agm] if tsp == 0 and j == 0
                            else [hmj.get(j - 2) or hm_prev],
                            stream="f1wg", slot=jb)
                        ul = p.dma("sync", lambda s, j=j, jb=jb: s.dma_start(
                            out=wu_[jb][:], in_=gu_chunkset(44 + j)),
                            wait=[agm] if tsp == 0 and j == 0
                            else [hmj.get(j - 2) or hm_prev],
                            stream="f1wu", slot=jb)
                        for n2 in range(2):
                            nl = slice(n2 * 512, (n2 + 1) * 512)
                            for kc in range(16):
                                p.op("tensor",
                                     lambda t, kc=kc, nl=nl, jb=jb: t.matmul(
                                         ps_g[jb][:], wg_[jb][:, kc, :],
                                         xs_[:, kc, nl],
                                         start=(kc == 0), stop=(kc == 15)),
                                     wait=([gl, nmk, slj.get(j - 2)]
                                           if n2 == 0 else
                                           [slj.get((j, 0))])
                                     if kc == 0 else ())
                            for kc in range(16):
                                ua = p.op(
                                    "tensor",
                                    lambda t, kc=kc, nl=nl, jb=jb: t.matmul(
                                        ps_u[jb][:], wu_[jb][:, kc, :],
                                        xs_[:, kc, nl],
                                        start=(kc == 0), stop=(kc == 15)),
                                    wait=[ul, hmj.get((j - 2, 1) if n2 == 0
                                                      else (j, 0))]
                                    if kc == 0 else (),
                                    inc="pe" if kc == 15 else None)
                            slj[(j, n2)] = p.op(
                                "scalar", lambda s, jb=jb: s.activation(
                                    out=silu_t[:], in_=ps_g[jb][:],
                                    func=AF.Silu),
                                wait=[ua, hm_prev], inc="act")
                            hmj[(j, n2)] = hm_prev = p.op(
                                "vector",
                                lambda v, j=j, nl=nl, jb=jb: v.tensor_mul(
                                    hbuf[:, j, nl], silu_t[:], ps_u[jb][:]),
                                wait=[slj[(j, n2)]], inc="dve")
                        slj[j - 1] = slj.get((j - 1, 1))
                        hmj[j - 1] = hmj.get((j - 1, 1))
                    hmj[43] = hmj[(43, 1)]
                    slj[43] = slj[(43, 1)]
                    hmk = hm_prev
                    yev = {}
                    sm_ = None
                    for dc in range(16):
                        db = dc % 2
                        dl = p.dma("scalar", lambda s, dc=dc, db=db:
                                   s.dma_start(
                                       out=wdb[db][:], in_=dn_chunkset(dc)),
                                   wait=[yev.get(dc - 1) or hmk],
                                   stream="f1wd", slot=db)
                        for n2 in range(2):
                            nl = slice(n2 * 512, (n2 + 1) * 512)
                            for kc in range(44):
                                da = p.op(
                                    "tensor",
                                    lambda t, kc=kc, nl=nl, dc=dc, n2=n2,
                                    db=db: t.matmul(
                                        ps_g[db][:] if n2 == 0
                                        else ps_u[db][:],
                                        wdb[db][:, kc, :],
                                        hbuf[:, kc, nl],
                                        start=(kc == 0), stop=(kc == 43)),
                                    wait=[dl, hmk, yev.get(dc - 2)]
                                    if kc == 0 and n2 == 0 else (),
                                    inc="pe" if kc == 43 else None)
                        p.op("vector", lambda v, dc=dc, db=db: v.tensor_copy(
                            ybf[:, dc, 0:512], ps_g[db][:]), wait=[da],
                            inc="dve")
                        ye = p.op("vector", lambda v, dc=dc, db=db:
                                  v.tensor_copy(
                                      ybf[:, dc, 512:1024], ps_u[db][:]),
                                  inc="dve")
                        sq = p.op("scalar", lambda s, dc=dc: s.activation(
                            out=sqb[:], in_=ybf[:, dc, :], func=AF.Square),
                            wait=[ye], inc="act")
                        for n2 in range(2):
                            nl = slice(n2 * 512, (n2 + 1) * 512)
                            sm_ = p.op(
                                "tensor", lambda t, nl=nl, dc=dc: t.matmul(
                                    ps_s[:, nl], onesb[:], sqb[:, nl],
                                    start=(dc == 0), stop=(dc == 15),
                                    skip_group_check=True),
                                wait=[sq, om] if n2 == 0 else (),
                                inc="pe" if dc == 15 and n2 == 1 else None)
                        yev[dc] = ye
                    p.op("vector", lambda v: v.tensor_scalar(
                        rmsrow[:], ps_s[:], 1.0 / 2048.0, float(EPS_RMS),
                        ALU.mult, ALU.add), wait=[sm_], inc="dve")
                    lnr = p.op("scalar", lambda s: s.activation(
                        out=rmsrow[:], in_=rmsrow[:], func=AF.Ln),
                        wait=[p.mark("dve")], inc="act")
                    p.op("scalar", lambda s: s.activation(
                        out=rmsrow[:], in_=rmsrow[:], func=AF.Exp,
                        scale=-0.5), inc="act")
                    rbm = bcast_row(rmsrow, rb, ps_c,
                                    [p.mark("act"), p.mark("dve")])
                    xr_m = None
                    for dc in range(16):
                        xr_m = p.dma("sync", lambda s, dc=dc, t0=t0:
                                     s.dma_start(
                                         out=xres[:],
                                         in_=x_in.rearrange(
                                             "(kc p) t -> p kc t", p=128)[
                                             :, dc, t0:t0 + 1024]),
                                     wait=[st_m, rbm], stream="f1xres")
                        p.op("vector", lambda v, dc=dc: v.tensor_mul(
                            tmp32[:], ybf[:, dc, :], rb[:]),
                            wait=[rbm, st_m], inc="dve")
                        ob = p.op("vector",
                                  lambda v: v.scalar_tensor_tensor(
                                      tmp32[:], tmp32[:], 0.5, xres[:],
                                      ALU.mult, ALU.add),
                                  wait=[xr_m], inc="dve")
                        st_m = p.dma("gpsimd", lambda g, dc=dc, t0=t0:
                                     g.dma_start(
                                         out=x_out.rearrange(
                                             "(kc p) t -> p kc t", p=128)[
                                             :, dc, t0:t0 + 1024],
                                         in_=tmp32[:]),
                                     wait=[ob], stream="f1xst")
                    pctx.close()
        p.barrier()
    return (cc_out, agm)




def prep_ffn1_inputs(gu_pk, gu_sc, dn_pk, dn_sc):
    out = []
    for r in range(NCORE):
        g = np.ascontiguousarray(
            gu_pk[1408 * r:1408 * (r + 1)].reshape(N_GU_OT, 128, 256))
        gs = np.ascontiguousarray(gu_sc[:, 1408 * r:1408 * (r + 1)])
        d = np.ascontiguousarray(
            dn_pk[256 * r:256 * (r + 1)].reshape(N_DN_OT, 128, 704))
        ds = np.ascontiguousarray(dn_sc[:, 256 * r:256 * (r + 1)])
        out.append((g, gs, d, ds))
    return out


def make_ffn1_launch():
    nc = bass.Bass("TRN2", target_bir_lowering=False, debug=False)
    ctx = ExitStack()
    x_in = nc.dram_tensor("x", [D, TLOC], F32, kind="ExternalInput").ap()
    gu_pk = nc.dram_tensor("gu_pk", [N_GU_OT, 128, 256], I32,
                           kind="ExternalInput").ap()
    gu_sc = nc.dram_tensor("gu_sc", [16, 1408], F16, kind="ExternalInput").ap()
    dn_pk = nc.dram_tensor("dn_pk", [N_DN_OT, 128, 704], I32,
                           kind="ExternalInput").ap()
    dn_sc = nc.dram_tensor("dn_sc", [44, 256], F16, kind="ExternalInput").ap()
    ident = nc.dram_tensor("ident", [128, 128], BF16, kind="ExternalInput").ap()
    x_out = nc.dram_tensor("xout", [D, TLOC], F32, kind="ExternalOutput").ap()
    ins = {"gu_pk": gu_pk, "gu_sc": gu_sc, "dn_pk": dn_pk, "dn_sc": dn_sc,
           "ident": ident}
    with ctx:
        p = Prog(nc, ctx)
        build_ffn1(nc, p, x_in, x_out, ins)
        p.finalize([p.mark("ds_f1xst_0")])
    return nc


def run_ffn1(x_slices, gu_pk, gu_sc, dn_pk, dn_sc):
    nc = make_ffn1_launch()
    sl = prep_ffn1_inputs(gu_pk, gu_sc, dn_pk, dn_sc)
    idb = np.eye(128, dtype=np.float32).astype(ml_dtypes.bfloat16)
    in_maps = [{"x": x_slices[c], "gu_pk": sl[c][0], "gu_sc": sl[c][1],
                "dn_pk": sl[c][2], "dn_sc": sl[c][3], "ident": idb}
               for c in range(NCORE)]
    res = run_bass_kernel_spmd(nc, in_maps, core_ids=list(range(NCORE)))
    return [res.results[c]["xout"] for c in range(NCORE)]


def build_attn1(nc, p, x_in, x_out, wflat_ext, ins, tag="s5",
                staged=None):
    NW1 = 2 * 128 * 16 * 128
    per_rank = 4 * NW1
    if staged is None:
        wfull, wmark = stage_weights_ag(p, nc, wflat_ext, per_rank, tag + "w")
    else:
        wfull, wmark = staged
    wf = wfull.ap()
    wmk = (lambda r: wmark[r]) if isinstance(wmark, list) else (
        lambda r: wmark)

    def w_chunkset(widx, oc):
        r, i = oc // 2, oc % 2
        off = r * per_rank + widx * NW1 + i * (128 * 16 * 128)
        return wf[off:off + 128 * 16 * 128].rearrange(
            "(p kc m) -> p kc m", p=128, kc=16)

    ek_d = nc.dram_tensor(tag + "_ek", [16, 128, TLOC], F32, kind="Internal")
    ekv_d = nc.dram_tensor(tag + "_ekv", [16, 128, TLOC], F32, kind="Internal")
    r_d = nc.dram_tensor(tag + "_r", [16, 128, TLOC], BF16, kind="Internal")
    ccb_in = nc.dram_tensor(tag + "_ccbi", [128, 16], F32, kind="Internal")
    ccb_out = nc.dram_tensor(tag + "_ccbo", [NCORE * 128, 16], F32,
                             kind="Internal")
    cct_in = nc.dram_tensor(tag + "_ccti", [128, 32], F32, kind="Internal")
    cct_out = nc.dram_tensor(tag + "_ccto", [NCORE * 128, 32], F32,
                             kind="Internal")

    octx = ExitStack()
    osb = lambda n, s, d: octx.enter_context(nc.sbuf_tensor(tag + n, s, d))
    m1row = osb("_m1r", [1, TLOC], F32)
    r1row = osb("_r1r", [1, TLOC], F32)
    vecs = osb("_vecs", [128, 7, 16], F32)
    maskbc = osb("_mkb", [128, 2, 8], F32)
    xbuf2 = osb("_xa", [128, DC, TLOC], BF16)
    sxcol = osb("_sx", [128, 16], F32)
    ones1f = osb("_o1f", [1, 128], F32)

    vm_ = p.dma("sync", lambda s: s.dma_start(
        out=vecs[:], in_=ins["vecs"]), stream=tag + "vec")
    p.op("vector", lambda v: v.memset(ones1f[:], 1.0), inc="dve")

    with ExitStack() as ctx:
        xbuf = ctx.enter_context(
            nc.sbuf_tensor(tag + "_u", [128, DC, TLOC], BF16))
        ln = LNCtx(nc, ctx, tag + "_ln", TLOC)
        pubcol = ctx.enter_context(
            nc.sbuf_tensor(tag + "_pub", [128, 16], F32))
        xm = load_xbuf(p, x_in, xbuf)
        lnm = ln_compute(p, nc, ctx, ln, xbuf, DC, xm)
        p.op("vector", lambda v: v.tensor_copy(m1row[:], ln.mrow[:]),
             wait=[lnm], inc="dve")
        rows_m = p.op("vector", lambda v: v.tensor_copy(
            r1row[:], ln.rrow[:]), inc="dve")
        nm1 = ln_normalize(p, ln, xbuf, DC, rows_m)
        lnm2 = ln_compute(p, nc, ctx, ln, xbuf, DC, nm1)
        am = None
        for kc in range(DC):
            p.op("vector", lambda v, kc=kc: v.tensor_sub(
                ln.tmp32[:], xbuf[:, kc, :], ln.mbc[:]),
                wait=[lnm2] if kc == 0 else (), inc="dve")
            p.op("vector", lambda v: v.tensor_mul(
                ln.tmp32[:], ln.tmp32[:], ln.rbc[:]), inc="dve")
            am = p.op("vector", lambda v, kc=kc: v.tensor_scalar(
                xbuf2[:, kc, :], ln.tmp32[:],
                vecs[:, 0, kc:kc + 1], vecs[:, 1, kc:kc + 1],
                ALU.mult, ALU.add), wait=[vm_] if kc == 0 else (),
                inc="dve")
        pm = None
        for kc in range(DC):
            pm = p.op("vector", lambda v, kc=kc: v.tensor_copy(
                pubcol[:, kc:kc + 1], xbuf2[:, kc, TLOC - 1:TLOC]),
                wait=[am] if kc == 0 else (), inc="dve")
        pbm = p.dma("gpsimd", lambda g: g.dma_start(
            out=ccb_in.ap(), in_=pubcol[:]), wait=[pm], stream=tag + "pb")
        agb = p.op("gpsimd", lambda g: g.collective_compute(
            "AllGather", ALU.bypass, replica_groups=[list(range(NCORE))],
            ins=[ccb_in.ap().opt()], outs=[ccb_out.ap().opt()]),
            wait=[pbm], inc="cc")
        with ExitStack() as pctx:
            ps_b = pctx.enter_context(
                nc.psum_tensor(tag + "psmk", [128, 8], F32))
            mkrow = pctx.enter_context(
                nc.sbuf_tensor(tag + "_mkr", [1, 16], F32))
            bm1 = p.dma("sync", lambda s: s.dma_start(
                out=mkrow[0:1, 0:8], in_=ins["bmask"]), stream=tag + "bm")
            bm2 = p.dma("sync", lambda s: s.dma_start(
                out=mkrow[0:1, 8:16], in_=ins["smask"]), stream=tag + "bm2")
            k1 = p.op("tensor", lambda t: t.matmul(
                ps_b[:], ones1f[:], mkrow[0:1, 0:8],
                start=True, stop=True), wait=[bm1, bm2, am], inc="pe")
            p.op("vector", lambda v: v.tensor_copy(
                maskbc[:, 0, :], ps_b[:]), wait=[k1], inc="dve")
            k2 = p.op("tensor", lambda t: t.matmul(
                ps_b[:], ones1f[:], mkrow[0:1, 8:16],
                start=True, stop=True), wait=[p.mark("dve")], inc="pe")
            mkm = p.op("vector", lambda v: v.tensor_copy(
                maskbc[:, 1, :], ps_b[:]), wait=[k2], inc="dve")
        with ExitStack() as bctx:
            bndl = bctx.enter_context(
                nc.sbuf_tensor(tag + "_bnd", [128, 8, 16], F32))
            blm = p.dma("sync", lambda s: s.dma_start(
                out=bndl[:], in_=ccb_out.ap().rearrange(
                    "(r p) c -> p r c", p=128)), wait=[agb, mkm],
                stream=tag + "bnl")
            fm = p.op("vector", lambda v: v.memset(sxcol[:], 0.0), inc="dve")
            for j in range(NCORE):
                fm = p.op("vector", lambda v, j=j: v.scalar_tensor_tensor(
                    sxcol[:], bndl[:, j, :], maskbc[:, 0, j:j + 1],
                    sxcol[:], ALU.mult, ALU.add),
                    wait=[blm, mkm] if j == 0 else (), inc="dve")
        p.barrier()

    with ExitStack() as ctx:
        sb = lambda n, s, d: ctx.enter_context(nc.sbuf_tensor(tag + n, s, d))
        xkb = sb("_xk", [128, DC, 512], BF16)
        xvb = sb("_xv", [128, DC, 512], BF16)
        xrb = sb("_xrm", [128, DC, 512], BF16)
        tmp32 = sb("_t32b", [128, 512], F32)
        kf32 = sb("_kf", [128, 512], F32)
        vf32 = sb("_vf", [128, 512], F32)
        ekS = [sb(f"_ekS{i}", [128, 512], F32) for i in range(2)]
        ekvS = [sb(f"_ekvS{i}", [128, 512], F32) for i in range(2)]
        rS = [sb(f"_rS{i}", [128, 512], BF16) for i in range(2)]
        wkb = [sb(f"_wk{i}", [128, 16, 128], BF16) for i in range(2)]
        wvb = [sb(f"_wv{i}", [128, 16, 128], BF16) for i in range(2)]
        wrb = [sb(f"_wr{i}", [128, 16, 128], BF16) for i in range(2)]
        with ExitStack() as pctx:
            ps_k = pctx.enter_context(
                nc.psum_tensor(tag + "psk", [128, 512], F32))
            ps_v = pctx.enter_context(
                nc.psum_tensor(tag + "psv", [128, 512], F32))
            ps_r = pctx.enter_context(
                nc.psum_tensor(tag + "psr", [128, 512], F32))
            ekst = {}
            ekvst = {}
            rst = {}
            kev = {}
            vev = {}
            rev = {}
            mixm = None
            for ts in range(4):
                t0 = ts * 512
                for kc in range(DC):
                    for mi, dst in ((2, xkb), (3, xvb), (4, xrb)):
                        if ts == 0:
                            p.op("vector", lambda v, kc=kc: v.tensor_sub(
                                tmp32[:, 0:1], xbuf2[:, kc, 0:1],
                                sxcol[:, kc:kc + 1]),
                                wait=[p.mark("pe")] if kc == 0 and mi == 2
                                else (), inc="dve")
                            p.op("vector",
                                 lambda v, kc=kc, mi=mi, dst=dst:
                                 v.scalar_tensor_tensor(
                                     dst[:, kc, 0:1], tmp32[:, 0:1],
                                     vecs[:, mi, kc:kc + 1],
                                     sxcol[:, kc:kc + 1],
                                     ALU.mult, ALU.add), inc="dve")
                            p.op("vector", lambda v, kc=kc: v.tensor_sub(
                                tmp32[:, 1:512], xbuf2[:, kc, 1:512],
                                xbuf2[:, kc, 0:511]), inc="dve")
                            mixm = p.op(
                                "vector", lambda v, kc=kc, mi=mi, dst=dst:
                                v.scalar_tensor_tensor(
                                    dst[:, kc, 1:512], tmp32[:, 1:512],
                                    vecs[:, mi, kc:kc + 1],
                                    xbuf2[:, kc, 0:511],
                                    ALU.mult, ALU.add), inc="dve")
                        else:
                            p.op("vector", lambda v, kc=kc, t0=t0:
                                 v.tensor_sub(
                                     tmp32[:], xbuf2[:, kc, t0:t0 + 512],
                                     xbuf2[:, kc, t0 - 1:t0 + 511]),
                                 wait=[p.mark("pe")] if kc == 0 and mi == 2
                                 else (), inc="dve")
                            mixm = p.op(
                                "vector", lambda v, kc=kc, mi=mi, dst=dst,
                                t0=t0: v.scalar_tensor_tensor(
                                    dst[:, kc, :], tmp32[:],
                                    vecs[:, mi, kc:kc + 1],
                                    xbuf2[:, kc, t0 - 1:t0 + 511],
                                    ALU.mult, ALU.add), inc="dve")
                for oc in range(16):
                    ob = oc % 2
                    lk = p.dma("sync", lambda s, oc=oc, ob=ob: s.dma_start(
                        out=wkb[ob][:], in_=w_chunkset(0, oc)),
                        wait=[wmk(oc // 2)] if ts == 0 and oc < 2
                        else [kev.get((ts, oc - 2))
                              or kev.get((ts - 1, 14 + oc)), wmk(oc // 2)],
                        stream=tag + "wk", slot=ob)
                    lv = p.dma("sync", lambda s, oc=oc, ob=ob: s.dma_start(
                        out=wvb[ob][:], in_=w_chunkset(1, oc)),
                        stream=tag + "wv", slot=ob)
                    lr = p.dma("sync", lambda s, oc=oc, ob=ob: s.dma_start(
                        out=wrb[ob][:], in_=w_chunkset(2, oc)),
                        stream=tag + "wr", slot=ob)
                    pk_g = kev.get((ts, oc - 1)) or kev.get((ts - 1, 15))
                    pv_g = vev.get((ts, oc - 1)) or vev.get((ts - 1, 15))
                    pr_g = rev.get((ts, oc - 1)) or rev.get((ts - 1, 15))
                    for kc in range(16):
                        p.op("tensor", lambda t, kc=kc, ob=ob: t.matmul(
                            ps_k[:], wkb[ob][:, kc, :], xkb[:, kc, :],
                            start=(kc == 0), stop=(kc == 15)),
                            wait=[lk, mixm, pk_g] if kc == 0 else ())
                    for kc in range(16):
                        km = p.op("tensor", lambda t, kc=kc, ob=ob: t.matmul(
                            ps_v[:], wvb[ob][:, kc, :], xvb[:, kc, :],
                            start=(kc == 0), stop=(kc == 15)),
                            wait=[lv, pv_g] if kc == 0 else ())
                    for kc in range(16):
                        rm_ = p.op("tensor", lambda t, kc=kc, ob=ob: t.matmul(
                            ps_r[:], wrb[ob][:, kc, :], xrb[:, kc, :],
                            start=(kc == 0), stop=(kc == 15)),
                            wait=[lr, pr_g] if kc == 0 else (),
                            inc="pe" if kc == 15 else None)
                    ke = p.op("vector", lambda v: v.tensor_copy(
                        kf32[:], ps_k[:]), wait=[rm_], inc="dve")
                    kev[(ts, oc)] = ke
                    ve = p.op("vector", lambda v: v.tensor_copy(
                        vf32[:], ps_v[:]), inc="dve")
                    vev[(ts, oc)] = ve
                    eb = oc % 2
                    ekm = p.op("scalar", lambda s, eb=eb: s.activation(
                        out=ekS[eb][:], in_=kf32[:], func=AF.Exp),
                        wait=[ke, ekst.get((ts, oc - 2)) or
                              ekst.get((ts - 1, 14 + oc))], inc="act")
                    ekvm = p.op("vector", lambda v, eb=eb: v.tensor_mul(
                        ekvS[eb][:], ekS[eb][:], vf32[:]),
                        wait=[ekm, ekvst.get((ts, oc - 2)) or
                              ekvst.get((ts - 1, 14 + oc))], inc="dve")
                    rsm = p.op("scalar", lambda s, eb=eb: s.activation(
                        out=rS[eb][:], in_=ps_r[:], func=AF.Sigmoid),
                        wait=[rm_, rst.get((ts, oc - 2)) or
                              rst.get((ts - 1, 14 + oc))], inc="act")
                    rev[(ts, oc)] = rsm
                    ekst[(ts, oc)] = p.dma(
                        "gpsimd", lambda g, oc=oc, t0=t0, eb=eb: g.dma_start(
                            out=ek_d.ap()[oc, :, t0:t0 + 512],
                            in_=ekS[eb][:]), wait=[ekm],
                        stream=tag + "eks", slot=eb)
                    ekvst[(ts, oc)] = p.dma(
                        "gpsimd", lambda g, oc=oc, t0=t0, eb=eb: g.dma_start(
                            out=ekv_d.ap()[oc, :, t0:t0 + 512],
                            in_=ekvS[eb][:]), wait=[ekvm],
                        stream=tag + "ekvs", slot=eb)
                    rst[(ts, oc)] = p.dma(
                        "gpsimd", lambda g, oc=oc, t0=t0, eb=eb: g.dma_start(
                            out=r_d.ap()[oc, :, t0:t0 + 512],
                            in_=rS[eb][:]), wait=[rsm],
                        stream=tag + "rs", slot=eb)
        p.barrier()

    with ExitStack() as ctx:
        sb = lambda n, s, d: ctx.enter_context(nc.sbuf_tensor(tag + n, s, d))
        ekb = sb("_ekb", [128, TLOC], F32)
        ekvb = sb("_ekvb", [128, TLOC], F32)
        numo = sb("_no", [128, TLOC], F32)
        deno = sb("_do", [128, TLOC], F32)
        onest = sb("_ones", [128, TLOC], F32)
        ewt = sb("_ewt", [128, TLOC], F32)
        dcol = sb("_dcol", [128, 1], F32)
        ewcol = sb("_ewc", [128, 1], F32)
        tailb = sb("_tail", [128, 32], F32)
        p.op("vector", lambda v: v.memset(onest[:], 1.0), inc="dve")
        for ch in range(16):
            lek = p.dma("sync", lambda s, ch=ch: s.dma_start(
                out=ekb[:], in_=ek_d.ap()[ch]),
                wait=[p.mark("dve")] if ch > 0 else [],
                stream=tag + "ekl")
            lekv = p.dma("sync", lambda s, ch=ch: s.dma_start(
                out=ekvb[:], in_=ekv_d.ap()[ch]), stream=tag + "ekvl")
            p.op("scalar", lambda s, ch=ch: s.activation(
                out=dcol[:], in_=vecs[:, 5, ch:ch + 1], func=AF.Exp),
                wait=[vm_] if ch == 0 else (), inc="act")
            p.op("vector", lambda v: v.tensor_scalar_mul(
                dcol[:], dcol[:], -1.0), wait=[p.mark("act")], inc="dve")
            ewm = p.op("scalar", lambda s: s.activation(
                out=ewcol[:], in_=dcol[:], func=AF.Exp),
                wait=[p.mark("dve")], inc="act")
            etm = p.op("vector", lambda v: v.tensor_scalar(
                ewt[:], onest[:], ewcol[:], None, ALU.mult, ALU.bypass),
                wait=[ewm], inc="dve")
            sn = p.op("vector", lambda v: v.tensor_tensor_scan(
                numo[:], ewt[:], ekvb[:], 0.0, ALU.mult, ALU.add),
                wait=[lekv, etm], inc="dve")
            sd = p.op("vector", lambda v: v.tensor_tensor_scan(
                deno[:], ewt[:], ekb[:], 0.0, ALU.mult, ALU.add),
                wait=[lek], inc="dve")
            p.op("vector", lambda v, ch=ch: v.tensor_copy(
                tailb[:, 2 * ch:2 * ch + 1],
                numo[:, TLOC - 1:TLOC]), inc="dve")
            tm = p.op("vector", lambda v, ch=ch: v.tensor_copy(
                tailb[:, 2 * ch + 1:2 * ch + 2],
                deno[:, TLOC - 1:TLOC]), inc="dve")
        ptm = p.dma("gpsimd", lambda g: g.dma_start(
            out=cct_in.ap(), in_=tailb[:]), wait=[tm], stream=tag + "tp")
        agt = p.op("gpsimd", lambda g: g.collective_compute(
            "AllGather", ALU.bypass, replica_groups=[list(range(NCORE))],
            ins=[cct_in.ap().opt()], outs=[cct_out.ap().opt()]),
            wait=[ptm], inc="cc")
        p.barrier()

    rwkvb = osb("_rwkv", [128, DC, TLOC], BF16)
    with ExitStack() as ctx:
        sb = lambda n, s, d: ctx.enter_context(nc.sbuf_tensor(tag + n, s, d))
        ekb = sb("_ekb2", [128, TLOC], F32)
        ekvb = sb("_ekvb2", [128, TLOC], F32)
        numb = sb("_numb", [128, TLOC + 1], F32)
        denb = sb("_denb", [128, TLOC + 1], F32)
        wkvb = sb("_wkvb", [128, TLOC], F32)
        rl = sb("_rl", [128, TLOC], BF16)
        onest = sb("_ones2", [128, TLOC], F32)
        ewt = sb("_ewt2", [128, TLOC], F32)
        dcol = sb("_dcol2", [128, 1], F32)
        ewcol = sb("_ewc2", [128, 1], F32)
        ewCc = sb("_ewC", [128, 1], F32)
        eucol = sb("_euc", [128, 1], F32)
        t1c = sb("_t1c", [128, 1], F32)
        snc = sb("_snc", [128, 1], F32)
        sdc = sb("_sdc", [128, 1], F32)
        tlb = sb("_tlb", [128, 8, 32], F32)
        p.op("vector", lambda v: v.memset(onest[:], 1.0), inc="dve")
        tlm = p.dma("sync", lambda s: s.dma_start(
            out=tlb[:], in_=cct_out.ap().rearrange("(r p) c -> p r c",
                                                   p=128)),
            wait=[agt], stream=tag + "tll")
        for ch in range(16):
            lek = p.dma("sync", lambda s, ch=ch: s.dma_start(
                out=ekb[:], in_=ek_d.ap()[ch]),
                wait=[p.mark("dve")] if ch > 0 else [],
                stream=tag + "ekl2")
            lekv = p.dma("sync", lambda s, ch=ch: s.dma_start(
                out=ekvb[:], in_=ekv_d.ap()[ch]), stream=tag + "ekvl2")
            lrc = p.dma("sync", lambda s, ch=ch: s.dma_start(
                out=rl[:], in_=r_d.ap()[ch]), stream=tag + "rl")
            p.op("scalar", lambda s, ch=ch: s.activation(
                out=dcol[:], in_=vecs[:, 5, ch:ch + 1], func=AF.Exp),
                inc="act")
            p.op("vector", lambda v: v.tensor_scalar_mul(
                t1c[:], dcol[:], -float(TLOC)), wait=[p.mark("act")],
                inc="dve")
            p.op("scalar", lambda s: s.activation(
                out=ewCc[:], in_=t1c[:], func=AF.Exp),
                wait=[p.mark("dve")], inc="act")
            p.op("vector", lambda v: v.tensor_scalar_mul(
                dcol[:], dcol[:], -1.0), inc="dve")
            ewm = p.op("scalar", lambda s: s.activation(
                out=ewcol[:], in_=dcol[:], func=AF.Exp),
                wait=[p.mark("dve")], inc="act")
            eum = p.op("scalar", lambda s, ch=ch: s.activation(
                out=eucol[:], in_=vecs[:, 6, ch:ch + 1], func=AF.Exp),
                inc="act")
            etm = p.op("vector", lambda v: v.tensor_scalar(
                ewt[:], onest[:], ewcol[:], None, ALU.mult, ALU.bypass),
                wait=[ewm], inc="dve")
            p.op("vector", lambda v: v.memset(snc[:], 0.0), inc="dve")
            p.op("vector", lambda v: v.memset(sdc[:], 0.0), inc="dve")
            fmm = p.mark("dve")
            for j in range(NCORE):
                p.op("vector", lambda v: v.tensor_scalar_add(
                    t1c[:], ewCc[:], -1.0),
                    wait=[tlm, fmm] if j == 0 else (), inc="dve")
                p.op("vector", lambda v, j=j: v.tensor_scalar(
                    t1c[:], t1c[:], maskbc[:, 1, j:j + 1], 1.0,
                    ALU.mult, ALU.add), inc="dve")
                p.op("vector", lambda v: v.tensor_mul(
                    snc[:], snc[:], t1c[:]), inc="dve")
                p.op("vector", lambda v, j=j, ch=ch:
                     v.scalar_tensor_tensor(
                         snc[:], tlb[:, j, 2 * ch:2 * ch + 1],
                         maskbc[:, 1, j:j + 1], snc[:],
                         ALU.mult, ALU.add), inc="dve")
                p.op("vector", lambda v: v.tensor_mul(
                    sdc[:], sdc[:], t1c[:]), inc="dve")
                p.op("vector", lambda v, j=j, ch=ch:
                     v.scalar_tensor_tensor(
                         sdc[:], tlb[:, j, 2 * ch + 1:2 * ch + 2],
                         maskbc[:, 1, j:j + 1], sdc[:],
                         ALU.mult, ALU.add), inc="dve")
            p.op("vector", lambda v: v.tensor_copy(
                numb[:, 0:1], snc[:]), inc="dve")
            p.op("vector", lambda v: v.tensor_copy(
                denb[:, 0:1], sdc[:]), inc="dve")
            p.op("vector", lambda v: v.tensor_tensor_scan(
                numb[:, 1:TLOC + 1], ewt[:], ekvb[:], snc[:],
                ALU.mult, ALU.add), wait=[lekv, etm], inc="dve")
            p.op("vector", lambda v: v.tensor_tensor_scan(
                denb[:, 1:TLOC + 1], ewt[:], ekb[:], sdc[:],
                ALU.mult, ALU.add), wait=[lek], inc="dve")
            p.op("vector", lambda v: v.tensor_scalar(
                wkvb[:], ekvb[:], eucol[:], None, ALU.mult, ALU.bypass),
                wait=[eum], inc="dve")
            p.op("vector", lambda v: v.tensor_add(
                wkvb[:], wkvb[:], numb[:, 0:TLOC]), inc="dve")
            p.op("vector", lambda v: v.tensor_scalar(
                ewt[:], ekb[:], eucol[:], None, ALU.mult, ALU.bypass),
                inc="dve")
            dnm = p.op("vector", lambda v: v.tensor_add(
                ewt[:], ewt[:], denb[:, 0:TLOC]), inc="dve")
            p.op("scalar", lambda s: s.activation(
                out=ewt[:], in_=ewt[:], func=AF.Ln), wait=[dnm], inc="act")
            rcm = p.op("scalar", lambda s: s.activation(
                out=ewt[:], in_=ewt[:], func=AF.Exp, scale=-1.0),
                inc="act")
            p.op("vector", lambda v: v.tensor_mul(
                wkvb[:], wkvb[:], ewt[:]), wait=[rcm], inc="dve")
            p.op("vector", lambda v, ch=ch: v.tensor_mul(
                rwkvb[:, ch, :], wkvb[:], rl[:]), wait=[lrc], inc="dve")
        p.barrier()

    with ExitStack() as ctx:
        sb = lambda n, s, d: ctx.enter_context(nc.sbuf_tensor(tag + n, s, d))
        wob = [sb(f"_wo{i}", [128, 16, 128], BF16) for i in range(2)]
        m1bc = sb("_m1bc", [128, 512], F32)
        r1bc = sb("_r1bc", [128, 512], F32)
        xres = sb("_xr2", [128, 512], F32)
        u32 = sb("_u32", [128, 512], F32)
        o32 = sb("_o32", [128, 512], F32)
        with ExitStack() as pctx:
            ps_o = [pctx.enter_context(
                nc.psum_tensor(tag + f"pso{i}", [128, 512], F32))
                for i in range(2)]
            ps_c = pctx.enter_context(
                nc.psum_tensor(tag + "psbc", [128, 512], F32))
            st_m = None
            oev = {}
            wo_m = {}
            for ts in range(4):
                t0 = ts * 512
                b1 = p.op("tensor", lambda t, t0=t0: t.matmul(
                    ps_c[:], ones1f[:], m1row[:, t0:t0 + 512],
                    start=True, stop=True),
                    wait=[rows_m, p.mark("dve")], inc="pe")
                p.op("vector", lambda v: v.tensor_copy(m1bc[:], ps_c[:]),
                     wait=[b1], inc="dve")
                b2 = p.op("tensor", lambda t, t0=t0: t.matmul(
                    ps_c[:], ones1f[:], r1row[:, t0:t0 + 512],
                    start=True, stop=True),
                    wait=[p.mark("dve")], inc="pe")
                bcm = p.op("vector", lambda v: v.tensor_copy(
                    r1bc[:], ps_c[:]), wait=[b2], inc="dve")
                for dc in range(16):
                    ob = dc % 2
                    prevkey = (ts, dc - 2) if dc >= 2 else (ts - 1, 14 + dc)
                    wo_m[(ts, dc)] = p.dma(
                        "sync", lambda s, dc=dc, ob=ob: s.dma_start(
                            out=wob[ob][:], in_=w_chunkset(3, dc)),
                        wait=[oev.get(prevkey) or wmk(0), wmk(dc // 2)],
                        stream=tag + "wo", slot=ob)
                    for kc in range(16):
                        om_ = p.op("tensor",
                                   lambda t, kc=kc, ob=ob, t0=t0: t.matmul(
                                       ps_o[ob][:], wob[ob][:, kc, :],
                                       rwkvb[:, kc, t0:t0 + 512],
                                       start=(kc == 0), stop=(kc == 15)),
                                   wait=[wo_m[(ts, dc)],
                                         oev.get(prevkey)]
                                   if kc == 0 else (),
                                   inc="pe" if kc == 15 else None)
                    xr_m = p.dma("sync", lambda s, dc=dc, t0=t0: s.dma_start(
                        out=xres[:],
                        in_=x_in.rearrange("(kc p) t -> p kc t", p=128)[
                            :, dc, t0:t0 + 512]),
                        wait=[st_m], stream="f1xres")
                    p.op("vector", lambda v: v.tensor_sub(
                        u32[:], xres[:], m1bc[:]),
                        wait=[xr_m, bcm], inc="dve")
                    p.op("vector", lambda v: v.tensor_mul(
                        u32[:], u32[:], r1bc[:]), inc="dve")
                    p.op("vector", lambda v: v.tensor_add(
                        u32[:], u32[:], xres[:]), inc="dve")
                    oev[(ts, dc)] = p.op("vector", lambda v, ob=ob:
                                         v.tensor_add(
                                             o32[:], u32[:], ps_o[ob][:]),
                                         wait=[om_, st_m], inc="dve")
                    st_m = p.dma(
                        "gpsimd", lambda g, dc=dc, t0=t0: g.dma_start(
                            out=x_out.rearrange("(kc p) t -> p kc t",
                                                p=128)[:, dc, t0:t0 + 512],
                            in_=o32[:]),
                        wait=[oev[(ts, dc)]], stream="f1xst")
        p.barrier()
    octx.close()


def prep_attn1_inputs(Wk, Wv, Wr, Wo, lnw, lnb, mixk, mixv, mixr, td, ctw):
    ts_ = [tile_w_out_in(np.asarray(W), 16, 16) for W in (Wk, Wv, Wr, Wo)]
    wslices = []
    for r in range(NCORE):
        s = np.concatenate([t[2 * r:2 * r + 2].ravel() for t in ts_])
        wslices.append(np.ascontiguousarray(s))
    vecs = np.stack([np.asarray(v).reshape(16, 128).T.astype(np.float32)
                     for v in (lnw, lnb, mixk, mixv, mixr, td, ctw)],
                    axis=1)
    vecs = np.ascontiguousarray(vecs)
    bmasks, smasks = [], []
    for c in range(NCORE):
        b, qt = c // 4, c % 4
        bm = np.zeros((1, 8), np.float32)
        if qt > 0:
            bm[0, c - 1] = 1.0
        sm = np.zeros((1, 8), np.float32)
        for r in range(NCORE):
            if r // 4 == b and r % 4 < qt:
                sm[0, r] = 1.0
        bmasks.append(bm)
        smasks.append(sm)
    return wslices, vecs, bmasks, smasks


def make_attn1_launch():
    nc = bass.Bass("TRN2", target_bir_lowering=False, debug=False)
    ctx = ExitStack()
    x_in = nc.dram_tensor("x", [D, TLOC], F32, kind="ExternalInput").ap()
    per_rank = 4 * 2 * 128 * 16 * 128
    wflat = nc.dram_tensor("wflat", [per_rank], F32, kind="ExternalInput").ap()
    vecs = nc.dram_tensor("vecs", [128, 7, 16], F32, kind="ExternalInput").ap()
    bmask = nc.dram_tensor("bmask", [1, 8], F32, kind="ExternalInput").ap()
    smask = nc.dram_tensor("smask", [1, 8], F32, kind="ExternalInput").ap()
    x_out = nc.dram_tensor("xout", [D, TLOC], F32, kind="ExternalOutput").ap()
    ins = {"vecs": vecs, "bmask": bmask, "smask": smask}
    with ctx:
        p = Prog(nc, ctx)
        build_attn1(nc, p, x_in, x_out, wflat, ins)
        p.finalize([p.mark("ds_f1xst_0")])
    return nc


def run_attn1(x_slices, Wk, Wv, Wr, Wo, lnw, lnb, mixk, mixv, mixr, td, ctw):
    nc = make_attn1_launch()
    wsl, vecs, bms, sms = prep_attn1_inputs(Wk, Wv, Wr, Wo, lnw, lnb,
                                            mixk, mixv, mixr, td, ctw)
    in_maps = [{"x": x_slices[c], "wflat": wsl[c], "vecs": vecs,
                "bmask": bms[c], "smask": sms[c]} for c in range(NCORE)]
    res = run_bass_kernel_spmd(nc, in_maps, core_ids=list(range(NCORE)))
    return [res.results[c]["xout"] for c in range(NCORE)]


def make_fused_launch():
    nc = bass.Bass("TRN2", target_bir_lowering=False, debug=False)
    ctx = ExitStack()
    x_in = nc.dram_tensor("x", [D, TLOC], F32, kind="ExternalInput").ap()
    pr_f0 = 2 * (8 * 16 * 128 * 128) + 2 * 64 * 128 * 128
    pr_a = 4 * 2 * 128 * 16 * 128
    w_f0_t = nc.dram_tensor("w_f0", [NCORE * pr_f0], F32,
                            kind="ExternalInput")
    w_f0 = w_f0_t.ap()
    h128c = nc.dram_tensor("h128", [128, 128], BF16, kind="ExternalInput").ap()
    w_a0_t = nc.dram_tensor("w_a0", [NCORE * pr_a], F32, kind="ExternalInput")
    w_a0 = w_a0_t.ap()
    ident = nc.dram_tensor("ident", [128, 128], BF16, kind="ExternalInput").ap()
    masks = nc.dram_tensor("masks", [4, 128, 512], F32,
                           kind="ExternalInput").ap()
    zmask = nc.dram_tensor("zmask", [1, 32], F32, kind="ExternalInput").ap()
    smask = nc.dram_tensor("smask", [1, 8], F32, kind="ExternalInput").ap()
    betas = nc.dram_tensor("betas", [1, 2048], F32, kind="ExternalInput").ap()
    gu_pk = nc.dram_tensor("gu_pk", [N_GU_OT, 128, 256], I32,
                           kind="ExternalInput").ap()
    gu_sc = nc.dram_tensor("gu_sc", [16, 1408], F16, kind="ExternalInput").ap()
    dn_pk = nc.dram_tensor("dn_pk", [N_DN_OT, 128, 704], I32,
                           kind="ExternalInput").ap()
    dn_sc = nc.dram_tensor("dn_sc", [44, 256], F16, kind="ExternalInput").ap()
    w_a1_t = nc.dram_tensor("w_a1", [NCORE * pr_a], F32, kind="ExternalInput")
    w_a1 = w_a1_t.ap()
    vecs = nc.dram_tensor("vecs", [128, 7, 16], F32, kind="ExternalInput").ap()
    bmask = nc.dram_tensor("bmask", [1, 8], F32, kind="ExternalInput").ap()
    x_out = nc.dram_tensor("xout", [D, TLOC], F32, kind="ExternalOutput").ap()
    xm_ = [nc.dram_tensor(f"xmid{i}", [D, TLOC], F32, kind="Internal").ap()
           for i in range(5)]
    a0c = {"ident": ident, "masks": masks, "zmask": zmask, "smask": smask,
           "betas": betas}
    f1c = {"gu_pk": gu_pk, "gu_sc": gu_sc, "dn_pk": dn_pk, "dn_sc": dn_sc,
           "ident": ident}
    a1c = {"vecs": vecs, "bmask": bmask, "smask": smask}
    with ctx:
        p = Prog(nc, ctx)
        n_w1 = 8 * 16 * 128 * 128
        st_f0 = stage_weights_cast(p, nc, w_f0, pr_f0, "s1w",
                                   split=2 * n_w1)
        st_a0 = stage_weights_cast(p, nc, w_a0, pr_a, "s2w")
        st_a1 = stage_weights_cast(p, nc, w_a1, pr_a, "s5w")
        st_f1 = stage_ffn1_weights(nc, p, f1c, "s4d")
        build_ffn0(nc, p, x_in, xm_[0], w_f0, h128c, tag="s1", staged=st_f0)
        build_attn0(nc, p, xm_[0], xm_[1], w_a0, a0c, tag="s2", staged=st_a0)
        build_ffn0(nc, p, xm_[1], xm_[2], w_f0, h128c, tag="s3",
                   staged=st_f0)
        build_ffn1(nc, p, xm_[2], xm_[3], f1c, tag="s4", staged=st_f1)
        build_attn1(nc, p, xm_[3], xm_[4], w_a1, a1c, tag="s5",
                    staged=st_a1)
        build_ffn1(nc, p, xm_[4], x_out, f1c, tag="s6", staged=st_f1)
        p.finalize(list(p.pend["gpsimd"]))
    return nc


_FUSED = True


def kernel(**inputs):
    inp = {k: np.asarray(v) for k, v in inputs.items()}
    x = inp["L0_attn_x"].astype(np.float32, copy=False)
    xs = core_slices(x)

    ffn0_w = prep_ffn0_weights(inp["L0_ffn_W1"], inp["L0_ffn_W2"],
                               inp["L0_ffn_W3"])
    h128b = hadamard128().astype(ml_dtypes.bfloat16)
    attn0_w = prep_attn0_weights(inp["L0_attn_Wq"], inp["L0_attn_Wk"],
                                 inp["L0_attn_Wv"], inp["L0_attn_Wo"])
    idb = np.eye(128, dtype=np.float32).astype(ml_dtypes.bfloat16)
    msk = causal_masks()
    zs, ss = core_masks()
    brow = np.ascontiguousarray(
        inp["L0_attn_betas"].reshape(1, 2048).astype(np.float32))
    ffn1_in = prep_ffn1_inputs(inp["L1_ffn_gate_up_packed"],
                               inp["L1_ffn_gate_up_scales"],
                               inp["L1_ffn_down_packed"],
                               inp["L1_ffn_down_scales"])
    a1_w, a1_vecs, a1_bm, a1_sm = prep_attn1_inputs(
        inp["L1_attn_Wk"], inp["L1_attn_Wv"], inp["L1_attn_Wr"],
        inp["L1_attn_Wo"], inp["L1_attn_ln_w"], inp["L1_attn_ln_b"],
        inp["L1_attn_mix_k"], inp["L1_attn_mix_v"], inp["L1_attn_mix_r"],
        inp["L1_attn_time_decay"], inp["L1_attn_current_token_weight"])

    if _FUSED:
        nc = make_fused_launch()
        f0full = np.ascontiguousarray(np.concatenate(ffn0_w))
        a0full = np.ascontiguousarray(np.concatenate(attn0_w))
        a1full = np.ascontiguousarray(np.concatenate(a1_w))
        in_maps = [{
            "x": xs[c], "w_f0": f0full, "h128": h128b,
            "w_a0": a0full, "ident": idb, "masks": msk,
            "zmask": zs[c], "smask": ss[c], "betas": brow,
            "gu_pk": ffn1_in[c][0], "gu_sc": ffn1_in[c][1],
            "dn_pk": ffn1_in[c][2], "dn_sc": ffn1_in[c][3],
            "w_a1": a1full, "vecs": a1_vecs, "bmask": a1_bm[c],
        } for c in range(NCORE)]
        res = run_bass_kernel_spmd(nc, in_maps, core_ids=list(range(NCORE)))
        cur = [np.asarray(res.results[c]["xout"]) for c in range(NCORE)]
        out = np.empty((B, S, D), dtype=np.float32)
        for c in range(NCORE):
            b, q = c // 4, c % 4
            out[b, q * TLOC:(q + 1) * TLOC, :] = cur[c].T
        return out

    nc_ffn0 = make_ffn0_launch()
    nc_attn0 = make_attn0_launch()
    nc_ffn1 = make_ffn1_launch()
    nc_attn1 = make_attn1_launch()

    def run(nc, maps):
        res = run_bass_kernel_spmd(nc, maps, core_ids=list(range(NCORE)))
        return [np.asarray(res.results[c]["xout"]) for c in range(NCORE)]

    def ffn0_maps(cur):
        return [{"x": cur[c], "wflat": ffn0_w[c], "h128": h128b}
                for c in range(NCORE)]

    def attn0_maps(cur):
        return [{"x": cur[c], "wflat": attn0_w[c], "ident": idb,
                 "masks": msk, "zmask": zs[c], "smask": ss[c],
                 "betas": brow} for c in range(NCORE)]

    def ffn1_maps(cur):
        return [{"x": cur[c], "gu_pk": ffn1_in[c][0], "gu_sc": ffn1_in[c][1],
                 "dn_pk": ffn1_in[c][2], "dn_sc": ffn1_in[c][3],
                 "ident": idb} for c in range(NCORE)]

    def attn1_maps(cur):
        return [{"x": cur[c], "wflat": a1_w[c], "vecs": a1_vecs,
                 "bmask": a1_bm[c], "smask": a1_sm[c]}
                for c in range(NCORE)]

    cur = xs
    cur = run(nc_ffn0, ffn0_maps(cur))
    cur = run(nc_attn0, attn0_maps(cur))
    cur = run(nc_ffn0, ffn0_maps(cur))
    cur = run(nc_ffn1, ffn1_maps(cur))
    cur = run(nc_attn1, attn1_maps(cur))
    cur = run(nc_ffn1, ffn1_maps(cur))

    out = np.empty((B, S, D), dtype=np.float32)
    for c in range(NCORE):
        b, q = c // 4, c % 4
        out[b, q * TLOC:(q + 1) * TLOC, :] = cur[c].T
    return out


# version 07
# speedup vs baseline: 1.0224x; 1.0224x over previous
from contextlib import ExitStack

import numpy as np
import ml_dtypes

import concourse.bass as bass
import concourse.mybir as mybir
from concourse.bass_utils import run_bass_kernel_spmd

F32 = mybir.dt.float32
BF16 = mybir.dt.bfloat16
I32 = mybir.dt.int32
F16 = mybir.dt.float16
AF = mybir.ActivationFunctionType
ALU = mybir.AluOpType
AXL = mybir.AxisListType

NCORE = 8
B, S, D = 2, 8192, 2048
TLOC = 2048
DC = D // 128
HFF = 8192
HC = HFF // 128
NHEAD, DKH, SEG = 16, 128, 512
NSEG_LOC = TLOC // SEG
I1 = 5632
EPS_LN = 1e-5
EPS_RMS = 1e-6
INV_SCALE_ATT = float(1.0 / np.float32(np.sqrt(np.float32(128.0))))
FWHT_SCALE = float(1.0 / np.sqrt(8192.0))

ENGS = ("sync", "scalar", "vector", "tensor", "gpsimd")
_S2_STOP = None
_S2_NHEADS = 16
_S2_NPROJ = 12


class Prog:
    def __init__(self, nc, ctx):
        self.nc = nc
        self.ctx = ctx
        self.ops = {e: [] for e in ENGS}
        self.cnt = {}
        self.sems = {}
        self.pend = {e: [] for e in ENGS}
        self.dirty = {e: False for e in ENGS}

    def sem(self, name):
        if name not in self.sems:
            self.sems[name] = self.ctx.enter_context(self.nc.semaphore(name))
            self.cnt[name] = 0
        return name

    AUTOCHAIN = ("vector", "scalar")

    def op(self, eng, fn, wait=(), inc=None, is_dma=False):
        waits = [w for w in (list(self.pend[eng]) + list(wait))
                 if w is not None and w[1] > 0]
        self.pend[eng] = []
        chain = eng in self.AUTOCHAIN and not is_dma
        if chain:
            prev = getattr(self, "chain_mark", {}).get(eng)
            if prev is not None:
                waits.append(prev)
            if inc is None:
                inc = eng + "_ch"
        incs = []
        mark = None
        if inc is not None:
            self.sem(inc)
            k = 16 if is_dma else 1
            self.cnt[inc] += k
            mark = (inc, self.cnt[inc])
            incs.append((inc, k))
        if chain:
            if not hasattr(self, "chain_mark"):
                self.chain_mark = {}
            self.chain_mark[eng] = mark
        for (s, _) in waits:
            self.sem(s)
        self.ops[eng].append([fn, waits, incs, is_dma])
        self.dirty[eng] = True
        return mark

    def dma(self, q, fn, wait=(), stream=None, slot=0):
        sem = f"ds_{stream}_{slot}" if stream else "d_" + q
        return self.op(q, fn, wait=wait, inc=sem, is_dma=True)

    def val(self, s):
        return self.cnt.get(s, 0)

    def mark(self, s):
        return (s, self.cnt.get(s, 0))

    def milestone(self, eng):
        assert self.ops[eng], f"milestone on empty engine {eng}"
        last = self.ops[eng][-1]
        if last[2]:
            s, k = last[2][-1]
            return (s, self.cnt[s])
        s = eng + "_ms"
        self.sem(s)
        k = 16 if last[3] else 1
        self.cnt[s] += k
        last[2].append((s, k))
        return (s, self.cnt[s])

    def barrier(self, exclude=()):
        marks = []
        for e in ENGS:
            if not self.dirty[e] or not self.ops[e]:
                continue
            last = self.ops[e][-1]
            if last[2]:
                sname, k = last[2][-1]
                marks.append((sname, self.cnt[sname]))
            else:
                sname = "bm_" + e
                self.sem(sname)
                k = 16 if last[3] else 1
                self.cnt[sname] += k
                last[2].append((sname, k))
                marks.append((sname, self.cnt[sname]))
            self.dirty[e] = False
        for s, v in self.cnt.items():
            if (s.startswith("d_") or s.startswith("ds_")) and v > 0:
                marks.append((s, v))
        marks = [(s, v) for (s, v) in marks if s not in exclude]
        marks = list({s: (s, v) for (s, v) in marks}.values())
        for e in ENGS:
            self.pend[e] = self.pend[e] + marks
        return marks

    def finalize(self, final_waits):
        for (s, v) in final_waits:
            if v > 0:
                self.sem(s)
                self.ops["gpsimd"].append([None, [(s, v)], [], False])
        blk = self.ctx.enter_context(self.nc.Block())
        prog = self

        def runner(ename):
            def run(eng):
                for fn, waits, incs, _ in prog.ops[ename]:
                    for (s, v) in waits:
                        eng.wait_ge(prog.sems[s], v)
                    if fn is None:
                        continue
                    inst = fn(eng)
                    for (s, k) in incs:
                        inst.then_inc(prog.sems[s], k)
            return run

        blk.sync(runner("sync"))
        blk.scalar(runner("scalar"))
        blk.vector(runner("vector"))
        blk.tensor(runner("tensor"))
        blk.gpsimd(runner("gpsimd"))


def tile_w_out_in(W, n_out_chunks, n_k_chunks):
    O, K = W.shape
    t = W.reshape(n_out_chunks, 128, n_k_chunks, 128).transpose(0, 2, 3, 1)
    return np.ascontiguousarray(t.transpose(0, 2, 1, 3))


def tile_w_in_out(W, n_out_chunks, n_k_chunks):
    t = W.reshape(n_k_chunks, 128, n_out_chunks, 128).transpose(2, 0, 1, 3)
    return np.ascontiguousarray(t.transpose(0, 2, 1, 3))


def hadamard128():
    H = np.array([[1.0]], dtype=np.float32)
    for _ in range(7):
        H = np.block([[H, H], [H, -H]])
    return np.ascontiguousarray(H.astype(np.float32))


def causal_masks():
    m = np.zeros((4, 128, 512), dtype=np.float32)
    t = np.arange(512)[None, :]
    for j in range(4):
        i = np.arange(128)[:, None] + 128 * j
        m[j] = np.where(t <= i, 0.0, np.float32(-1e38)).astype(np.float32)
    return m


def core_slices(x):
    out = []
    for c in range(NCORE):
        b, q = c // 4, c % 4
        out.append(np.ascontiguousarray(x[b, q * TLOC:(q + 1) * TLOC, :].T))
    return out


def stage_weights_ag(p, nc, ext_flat, n_per_rank, tag, base=0):
    wbf = nc.dram_tensor(f"{tag}_bf", [n_per_rank], BF16, kind="Internal")
    wfull = nc.dram_tensor(f"{tag}_full", [NCORE * n_per_rank], BF16,
                           kind="Internal")
    CH = 4 * 1024 * 1024
    last = None
    for off in range(0, n_per_rank, CH):
        n = min(CH, n_per_rank - off)
        last = p.dma("gpsimd", lambda g, off=off, n=n: g.dma_start(
            out=wbf.ap()[off:off + n],
            in_=ext_flat[base + off:base + off + n]))
    mark = p.op("gpsimd", lambda g: g.collective_compute(
        "AllGather", ALU.bypass, replica_groups=[list(range(NCORE))],
        ins=[wbf.ap().opt()], outs=[wfull.ap().opt()]),
        wait=[last], inc="cc")
    return wfull, mark


class LNCtx:

    def __init__(self, nc, ctx, tag, T):
        sb = lambda n, s, d: ctx.enter_context(nc.sbuf_tensor(tag + n, s, d))
        self.T = T
        self.sqbf = sb("_sqbf", [128, T], BF16)
        self.tmp32 = sb("_tmp32", [128, T], F32)
        self.mbc = sb("_mbc", [128, T], F32)
        self.rbc = sb("_rbc", [128, T], F32)
        self.mrow = sb("_mrow", [1, T], F32)
        self.vrow = sb("_vrow", [1, T], F32)
        self.m2row = sb("_m2row", [1, T], F32)
        self.rrow = sb("_rrow", [1, T], F32)
        self.onesb = sb("_onesb", [128, 1], BF16)
        self.ones1f = sb("_ones1f", [1, 128], F32)
        self.init_done = False


_LN_UID = [0]


def ln_compute(p, nc, ctx, ln, xbuf, n_chunks, xload_mark, eps=EPS_LN):
    T = ln.T
    nsp = T // 512
    _LN_UID[0] += 1
    uid = _LN_UID[0]
    if not ln.init_done:
        p.op("vector", lambda v: v.memset(ln.onesb[:], 1.0), inc="dve")
        p.op("vector", lambda v: v.memset(ln.ones1f[:], 1.0), inc="dve")
        ln.init_done = True
    ones_mark = p.mark("dve")
    with ExitStack() as pctx:
        ps_sx = pctx.enter_context(
            nc.psum_tensor(f"lnps_sx{uid}", [1, T], F32))
        ps_sxx = pctx.enter_context(
            nc.psum_tensor(f"lnps_sxx{uid}", [1, T], F32))
        sq_marks = {}
        pe_kc = {}
        for kc in range(n_chunks):
            w = [xload_mark, ones_mark] if kc == 0 else [pe_kc.get(kc - 1)]
            sq_marks[kc] = p.op(
                "scalar", lambda s, kc=kc: s.activation(
                    out=ln.sqbf[:], in_=xbuf[:, kc, :], func=AF.Square),
                wait=[w_ for w_ in w if w_], inc="act")
            for tsp in range(nsp):
                tl = slice(tsp * 512, (tsp + 1) * 512)
                p.op("tensor", lambda t, kc=kc, tl=tl: t.matmul(
                    ps_sx[:, tl], ln.onesb[:], xbuf[:, kc, tl],
                    start=(kc == 0), stop=(kc == n_chunks - 1),
                    skip_group_check=True),
                    wait=[xload_mark, ones_mark] if kc == 0 and tsp == 0
                    else ())
                pe_kc[kc] = p.op("tensor", lambda t, kc=kc, tl=tl: t.matmul(
                    ps_sxx[:, tl], ln.onesb[:], ln.sqbf[:, tl],
                    start=(kc == 0), stop=(kc == n_chunks - 1),
                    skip_group_check=True),
                    wait=[sq_marks[kc]] if tsp == 0 else (),
                    inc="pe" if tsp == nsp - 1 else None)
        stats_pe = pe_kc[n_chunks - 1]
        n_el = 128 * n_chunks
        p.op("vector", lambda v: v.tensor_scalar_mul(
            ln.mrow[:], ps_sx[:], 1.0 / n_el), wait=[stats_pe], inc="dve")
        p.op("vector", lambda v: v.tensor_scalar_mul(
            ln.vrow[:], ps_sxx[:], 1.0 / n_el), inc="dve")
        p.op("vector", lambda v: v.tensor_mul(
            ln.m2row[:], ln.mrow[:], ln.mrow[:]), inc="dve")
        p.op("vector", lambda v: v.tensor_sub(
            ln.vrow[:], ln.vrow[:], ln.m2row[:]), inc="dve")
        rows_dve = p.op("vector", lambda v: v.tensor_scalar_add(
            ln.vrow[:], ln.vrow[:], float(eps)), inc="dve")
    p.op("scalar", lambda s: s.activation(
        out=ln.vrow[:], in_=ln.vrow[:], func=AF.Ln),
        wait=[rows_dve], inc="act")
    rstd_act = p.op("scalar", lambda s: s.activation(
        out=ln.rrow[:], in_=ln.vrow[:], func=AF.Exp, scale=-0.5), inc="act")
    with ExitStack() as pctx:
        ps_bc = pctx.enter_context(
            nc.psum_tensor(f"lnps_bc{uid}", [128, T], F32))
        ps_bc2 = pctx.enter_context(
            nc.psum_tensor(f"lnps_bc2{uid}", [128, T], F32))
        for tsp in range(nsp):
            tl = slice(tsp * 512, (tsp + 1) * 512)
            p.op("tensor", lambda t, tl=tl: t.matmul(
                ps_bc[:, tl], ln.ones1f[:], ln.mrow[:, tl],
                start=True, stop=True),
                wait=[rstd_act] if tsp == 0 else ())
            bc_pe = p.op("tensor", lambda t, tl=tl: t.matmul(
                ps_bc2[:, tl], ln.ones1f[:], ln.rrow[:, tl],
                start=True, stop=True),
                inc="pe" if tsp == nsp - 1 else None)
        p.op("vector", lambda v: v.tensor_copy(ln.mbc[:], ps_bc[:]),
             wait=[bc_pe], inc="dve")
        bc_dve = p.op("vector", lambda v: v.tensor_copy(ln.rbc[:], ps_bc2[:]),
                      inc="dve")
    return bc_dve


def ln_normalize(p, ln, xbuf, n_chunks, stats_mark, affine=None, extra=()):
    m = None
    for kc in range(n_chunks):
        p.op("vector", lambda v, kc=kc: v.tensor_sub(
            ln.tmp32[:], xbuf[:, kc, :], ln.mbc[:]),
            wait=[stats_mark] + list(extra) if kc == 0 else (), inc="dve")
        if affine is None:
            m = p.op("vector", lambda v, kc=kc: v.tensor_mul(
                xbuf[:, kc, :], ln.tmp32[:], ln.rbc[:]), inc="dve")
        else:
            wcol, bcol = affine
            p.op("vector", lambda v, kc=kc: v.tensor_mul(
                ln.tmp32[:], ln.tmp32[:], ln.rbc[:]), inc="dve")
            m = p.op("vector", lambda v, kc=kc: v.tensor_scalar(
                xbuf[:, kc, :], ln.tmp32[:],
                wcol[:, kc:kc + 1], bcol[:, kc:kc + 1],
                ALU.mult, ALU.add), inc="dve")
    return m


def load_xbuf(p, x_in, xbuf, t0=0, T=TLOC, n_split=4, wait=()):
    marks = []
    per = DC // n_split
    for i in range(n_split):
        marks.append(p.dma("gpsimd", lambda g, i=i: g.dma_start(
            out=xbuf[:, i * per:(i + 1) * per, :],
            in_=x_in.rearrange("(kc p) t -> p kc t", p=128)[
                :, i * per:(i + 1) * per, t0:t0 + T]),
            wait=wait if i == 0 else ()))
    return marks[-1]


def build_ffn0(nc, p, x_in, x_out, wflat_ext, h128_ext, tag="s1",
               staged=None):
    n_w1 = 8 * 16 * 128 * 128
    n_w2 = 2 * 64 * 128 * 128
    per_rank = 2 * n_w1 + n_w2
    w2full = None
    if staged is None:
        wfull, wmark = stage_weights_ag(p, nc, wflat_ext, per_rank, tag + "w")
    elif len(staged) == 2:
        wfull, wmark = staged
    else:
        wfull, wmark, w2full, w2mark = staged
    wf = wfull.ap()
    split = w2full is not None
    w2f = w2full.ap() if split else wf
    w2_mark = w2mark if split else wmark

    def w1_chunk(hc, which):
        r, i = hc // 8, hc % 8
        if split:
            off = r * (2 * n_w1) + which * n_w1 + i * (16 * 128 * 128)
        else:
            off = r * per_rank + which * n_w1 + i * (16 * 128 * 128)
        return wf[off:off + 16 * 128 * 128].rearrange(
            "(p kc m) -> p kc m", p=128, kc=16)

    def w2_chunk(dc):
        r, i = dc // 2, dc % 2
        if split:
            off = r * n_w2 + i * (64 * 128 * 128)
        else:
            off = r * per_rank + 2 * n_w1 + i * (64 * 128 * 128)
        return w2f[off:off + 64 * 128 * 128].rearrange(
            "(p cc m) -> p cc m", p=128, cc=64)

    uprime = nc.dram_tensor(tag + "_uprime", [4, 2, 128, HC, 256], BF16,
                            kind="Internal")

    with ExitStack() as ctx:
        sb = lambda n, s, d: ctx.enter_context(nc.sbuf_tensor(tag + n, s, d))
        xbuf = sb("_xbuf", [128, DC, TLOC], BF16)
        ln = LNCtx(nc, ctx, tag + "_ln", TLOC)
        h128 = sb("_h128", [128, 128], BF16)
        wb1 = [sb(f"_w1_{i}", [128, 16, 128], BF16) for i in range(2)]
        wb3 = [sb(f"_w3_{i}", [128, 16, 128], BF16) for i in range(2)]
        silu_t = [sb(f"_silu{i}", [128, 512], F32) for i in range(2)]
        ubf = [sb(f"_ubf{i}", [128, 512], BF16) for i in range(2)]
        upr = [sb(f"_upr_{i}", [128, 512], BF16) for i in range(2)]

        hm = p.dma("gpsimd", lambda g: g.dma_start(out=h128[:], in_=h128_ext),
                   stream="s1h")
        xm = load_xbuf(p, x_in, xbuf)
        lnm = ln_compute(p, nc, ctx, ln, xbuf, DC, xm)
        nm = ln_normalize(p, ln, xbuf, DC, lnm)

        with ExitStack() as pctx:
            ps_g = [pctx.enter_context(
                nc.psum_tensor(tag + f"psg{i}", [128, 512], F32))
                for i in (0, 1)]
            ps_l = [pctx.enter_context(
                nc.psum_tensor(tag + f"psl{i}", [128, 512], F32))
                for i in (0, 1)]
            ps_h = [pctx.enter_context(
                nc.psum_tensor(tag + f"psh{i}", [128, 512], F32))
                for i in (0, 1)]

            g_idx = 0
            silu_m = {}
            mul_m = {}
            evac_m = {}
            h128_m = {}
            stm_m = {}
            grp_last_pe = {}
            pending_h128 = None

            def emit_h128(g):
                h128_m[g] = p.op(
                    "tensor", lambda t, g=g: t.matmul(
                        ps_h[g % 2][:], h128[:], ubf[g % 2][:],
                        start=True, stop=True),
                    wait=[mul_m[g], evac_m.get(g - 2)]
                    + ([hm] if g == 0 else []),
                    inc="pe")
                ub = g % 2
                hc_, tsp_ = g // 4, g % 4
                evac_m[g] = p.op(
                    "vector", lambda v, ub=ub: v.tensor_scalar_mul(
                        upr[ub][:], ps_h[ub][:], FWHT_SCALE),
                    wait=[h128_m[g], stm_m.get(g - 2)], inc="dve")
                p.dma("gpsimd", lambda g_, hc_=hc_, tsp_=tsp_, ub=ub:
                      g_.dma_start(
                          out=uprime.ap()[tsp_, 0, :, hc_, :],
                          in_=upr[ub][:, 0:256]),
                      wait=[evac_m[g]],
                      stream="s1upst", slot=g % 2)
                stm_m[g] = p.dma(
                    "gpsimd", lambda g_, hc_=hc_, tsp_=tsp_, ub=ub:
                    g_.dma_start(
                        out=uprime.ap()[tsp_, 1, :, hc_, :],
                        in_=upr[ub][:, 256:512]),
                    wait=(), stream="s1upst", slot=g % 2)

            for hc in range(HC):
                wbs = hc % 2
                guard = [wmark] if hc < 2 else [grp_last_pe.get((hc - 2, 3))]
                m1 = p.dma("sync", lambda s, hc=hc, wbs=wbs: s.dma_start(
                    out=wb1[wbs][:], in_=w1_chunk(hc, 0)), wait=guard,
                    stream="s1w1", slot=hc % 2)
                m3 = p.dma("scalar", lambda s, hc=hc, wbs=wbs: s.dma_start(
                    out=wb3[wbs][:], in_=w1_chunk(hc, 1)), wait=guard,
                    stream="s1w3", slot=hc % 2)
                for tsp in range(4):
                    tl = slice(tsp * 512, (tsp + 1) * 512)
                    gb = g_idx % 2
                    w0 = [m1, nm] if g_idx == 0                         else [m1, silu_m.get(g_idx - 2)]
                    for kc in range(16):
                        p.op("tensor",
                             lambda t, kc=kc, wbs=wbs, tl=tl, gb=gb: t.matmul(
                                 ps_g[gb][:], wb1[wbs][:, kc, :],
                                 xbuf[:, kc, tl],
                                 start=(kc == 0), stop=(kc == 15)),
                             wait=w0 if kc == 0 else ())
                    w0 = [m3] if g_idx == 0 else [m3, mul_m.get(g_idx - 2)]
                    for kc in range(16):
                        lastmm = p.op(
                            "tensor",
                            lambda t, kc=kc, wbs=wbs, tl=tl, gb=gb: t.matmul(
                                ps_l[gb][:], wb3[wbs][:, kc, :],
                                xbuf[:, kc, tl],
                                start=(kc == 0), stop=(kc == 15)),
                            wait=w0 if kc == 0 else (),
                            inc="pe" if kc == 15 else None)
                    grp_last_pe[(hc, tsp)] = lastmm
                    silu_m[g_idx] = p.op(
                        "scalar", lambda s, gb=gb: s.activation(
                            out=silu_t[gb][:], in_=ps_g[gb][:],
                            func=AF.Silu),
                        wait=[lastmm, mul_m.get(g_idx - 2)], inc="act")
                    mul_m[g_idx] = p.op(
                        "vector", lambda v, gb=gb: v.tensor_mul(
                            ubf[gb][:], silu_t[gb][:], ps_l[gb][:]),
                        wait=[silu_m[g_idx], h128_m.get(g_idx - 2)],
                        inc="dve")
                    if pending_h128 is not None:
                        emit_h128(pending_h128)
                    pending_h128 = g_idx
                    g_idx += 1
            emit_h128(pending_h128)
            p.barrier()

    with ExitStack() as ctx:
        sb = lambda n, s, d: ctx.enter_context(nc.sbuf_tensor(tag + n, s, d))
        bfly = sb("_bfly", [128, 64, 256], F32)
        bflyb = sb("_bflyb", [128, 64, 256], BF16)
        w2b = [sb(f"_w2_{i}", [128, 64, 128], BF16) for i in range(2)]
        xres = sb("_xres", [128, 16, 256], F32)
        outb = sb("_outb", [128, 16, 256], F32)
        with ExitStack() as pctx:
            ps_o = [pctx.enter_context(
                nc.psum_tensor(tag + f"pso{i}", [128, 256], F32))
                for i in range(2)]

            cast_m = None
            store_m = None
            pb_pe = {}
            wload_m = {}
            evac_v = {}
            for ts2 in range(8):
                tsp, half = ts2 // 2, ts2 % 2
                lm = p.dma("gpsimd", lambda g, tsp=tsp, half=half: g.dma_start(
                    out=bfly[:], in_=uprime.ap()[tsp, half]),
                    wait=[cast_m] if cast_m else (), stream="s1upld")
                for st_ in range(6):
                    h = 1 << st_
                    nb = 64 // (2 * h)

                    def apair(v, h=h, nb=nb):
                        a = bfly[:].rearrange(
                            "p (b two w) t -> p b two w t", b=nb, two=2, w=h)
                        return v.tensor_add(a[:, :, 0], a[:, :, 0], a[:, :, 1])

                    def bpair(v, h=h, nb=nb):
                        a = bfly[:].rearrange(
                            "p (b two w) t -> p b two w t", b=nb, two=2, w=h)
                        return v.scalar_tensor_tensor(
                            a[:, :, 1], a[:, :, 1], -2.0, a[:, :, 0],
                            ALU.mult, ALU.add)

                    p.op("vector", apair,
                         wait=[lm] if st_ == 0 else (), inc="dve")
                    p.op("vector", bpair, inc="dve")
                cast_m = p.op("vector", lambda v: v.tensor_copy(
                    bflyb[:], bfly[:]),
                    wait=[pb_pe.get((ts2 - 1, 15))], inc="dve")
                xm2 = p.dma("sync", lambda s, ts2=ts2: s.dma_start(
                    out=xres[:],
                    in_=x_in.rearrange("(kc p) t -> p kc t", p=128)[
                        :, :, ts2 * 256:(ts2 + 1) * 256]),
                    wait=[evac_v.get((ts2 - 1, 15))], stream="s1xres")
                for dc in range(16):
                    wbs = dc % 2
                    prevkey = (ts2, dc - 2) if dc >= 2 else (ts2 - 1, 14 + dc)
                    wload_m[(ts2, dc)] = p.dma(
                        "scalar" if dc % 2 else "sync",
                        lambda s, dc=dc, wbs=wbs: s.dma_start(
                            out=w2b[wbs][:], in_=w2_chunk(dc)),
                        wait=[pb_pe.get(prevkey) or w2_mark,
                              w2_mark if ts2 == 0 else None],
                        stream="s1w2", slot=dc % 2)
                    w0 = [wload_m[(ts2, dc)], cast_m, evac_v.get(prevkey)]
                    for cc in range(64):
                        mmark = p.op(
                            "tensor",
                            lambda t, cc=cc, wbs=wbs, dc=dc: t.matmul(
                                ps_o[dc % 2][:], w2b[wbs][:, cc, :],
                                bflyb[:, cc, :],
                                start=(cc == 0), stop=(cc == 63)),
                            wait=w0 if cc == 0 else (),
                            inc="pe" if cc == 63 else None)
                    pb_pe[(ts2, dc)] = mmark
                    evac_v[(ts2, dc)] = p.op(
                        "vector", lambda v, dc=dc: v.scalar_tensor_tensor(
                            outb[:, dc, :], ps_o[dc % 2][:], 0.5,
                            xres[:, dc, :], ALU.mult, ALU.add),
                        wait=[mmark, xm2, store_m], inc="dve")
                store_m = p.dma("gpsimd", lambda g, ts2=ts2: g.dma_start(
                    out=x_out.rearrange("(kc p) t -> p kc t", p=128)[
                        :, :, ts2 * 256:(ts2 + 1) * 256],
                    in_=outb[:]), wait=[evac_v[(ts2, 15)]], stream="s1xst")
            p.barrier()
    return (wfull, wmark)


def prep_ffn0_weights(W1, W2, W3):
    t1 = tile_w_out_in(np.asarray(W1), HC, DC)
    t3 = tile_w_out_in(np.asarray(W3), HC, DC)
    t2 = tile_w_out_in(np.asarray(W2), DC, HC)
    slices = []
    for r in range(NCORE):
        s = np.concatenate([
            t1[8 * r:8 * r + 8].ravel(),
            t3[8 * r:8 * r + 8].ravel(),
            t2[2 * r:2 * r + 2].ravel()])
        slices.append(np.ascontiguousarray(s))
    return slices


def make_ffn0_launch():
    nc = bass.Bass("TRN2", target_bir_lowering=False, debug=False)
    ctx = ExitStack()
    x_in = nc.dram_tensor("x", [D, TLOC], F32, kind="ExternalInput").ap()
    per_rank = 2 * (8 * 16 * 128 * 128) + 2 * 64 * 128 * 128
    wflat = nc.dram_tensor("wflat", [per_rank], F32, kind="ExternalInput").ap()
    h128c = nc.dram_tensor("h128", [128, 128], BF16, kind="ExternalInput").ap()
    x_out = nc.dram_tensor("xout", [D, TLOC], F32, kind="ExternalOutput").ap()
    with ctx:
        p = Prog(nc, ctx)
        build_ffn0(nc, p, x_in, x_out, wflat, h128c)
        p.finalize([p.mark("d_gpsimd")])
    return nc


def run_ffn0(x_slices, W1, W2, W3):
    nc = make_ffn0_launch()
    wsl = prep_ffn0_weights(W1, W2, W3)
    h = hadamard128().astype(ml_dtypes.bfloat16)
    in_maps = [{"x": x_slices[c], "wflat": wsl[c], "h128": h}
               for c in range(NCORE)]
    res = run_bass_kernel_spmd(nc, in_maps, core_ids=list(range(NCORE)))
    return [res.results[c]["xout"] for c in range(NCORE)]


def build_attn0(nc, p, x_in, x_out, wflat_ext, consts, tag="s2",
                staged=None):
    NW1 = 2 * 128 * 16 * 128
    per_rank = 4 * NW1
    if staged is None:
        wfull, wmark = stage_weights_ag(p, nc, wflat_ext, per_rank, tag + "w")
    else:
        wfull, wmark = staged
    wf = wfull.ap()

    def w_chunk(widx, oc):
        r, i = oc // 2, oc % 2
        off = r * per_rank + widx * NW1 + i * (128 * 16 * 128)
        return wf[off:off + 128 * 16 * 128].rearrange(
            "(p kc m) -> p kc m", p=128, kc=16)

    sq_d = nc.dram_tensor(tag + "_sq", [16, 128, TLOC], BF16, kind="Internal")
    sk_d = nc.dram_tensor(tag + "_sk", [16, 128, TLOC], BF16, kind="Internal")
    sktm_d = nc.dram_tensor(tag + "_sktm", [16, 16, 128, 128], BF16,
                            kind="Internal")
    vvtm_d = nc.dram_tensor(tag + "_vvtm", [16, 16, 128, 128], BF16,
                            kind="Internal")
    adot_d = nc.dram_tensor(tag + "_adot", [16, 16, 128, 128], BF16,
                            kind="Internal")
    aq_d = nc.dram_tensor(tag + "_aq", [16, 4, 2, 128, 128], BF16,
                          kind="Internal")
    cc1_in = nc.dram_tensor(tag + "_cc1i", [128, 64], F32, kind="Internal")
    cc1_out = nc.dram_tensor(tag + "_cc1o", [NCORE * 128, 64], F32,
                             kind="Internal")
    cc2_in = nc.dram_tensor(tag + "_cc2i", [128, 4096], BF16, kind="Internal")
    cc2_out = nc.dram_tensor(tag + "_cc2o", [NCORE * 128, 4096], BF16,
                             kind="Internal")

    octx = ExitStack()
    osb = lambda n, s, d: octx.enter_context(nc.sbuf_tensor(tag + n, s, d))
    ident = osb("_id", [128, 128], BF16)
    colsum = osb("_colsum", [128, 64], F32)
    zall = osb("_zall", [128, 64], F32)
    zallbf = osb("_zallbf", [128, 64], BF16)

    idm = p.dma("gpsimd", lambda g: g.dma_start(
        out=ident[:], in_=consts["ident"]), stream="s2id")

    with ExitStack() as ctx:
        sb = lambda n, s, d: ctx.enter_context(nc.sbuf_tensor(tag + n, s, d))
        xbuf = sb("_xbuf", [128, DC, TLOC], BF16)
        masks = sb("_masks", [128, 4, 512], F32)
        mkm = p.dma("gpsimd", lambda g: g.dma_start(
            out=masks[:], in_=consts["masks"].rearrange("j p t -> p j t")),
            stream="s2mk")
        with ExitStack() as lctx:
            ln = LNCtx(nc, lctx, tag + "_ln", TLOC)
            xm = load_xbuf(p, x_in, xbuf)
            lnm = ln_compute(p, nc, lctx, ln, xbuf, DC, xm)
            nm = ln_normalize(p, ln, xbuf, DC, lnm)
        qh = [sb(f"_qh{i}", [128, TLOC], BF16) for i in range(2)]
        kh = [sb(f"_kh{i}", [128, TLOC], BF16) for i in range(2)]
        vh = [sb(f"_vh{i}", [128, TLOC], BF16) for i in range(2)]
        sqh = [sb(f"_sqh{i}", [128, TLOC], BF16) for i in range(2)]
        skh = [sb(f"_skh{i}", [128, TLOC], BF16) for i in range(2)]
        vvtm = [sb(f"_vvtmsb{i}", [128, 16, 128], BF16) for i in range(2)]
        t32a = sb("_t32a", [128, TLOC], F32)
        t32b = sb("_t32b", [128, TLOC], F32)
        scob = sb("_scob", [128, 512], F32)
        nmax = sb("_nmax", [128, 1], F32)
        nmaxs = sb("_nmaxs", [128, 1], F32)
        rsum = sb("_rsum", [128, 1], F32)
        rrec = sb("_rrec", [128, 1], F32)
        pnb = sb("_pnb", [128, 512], BF16)
        ptb = sb("_ptb", [128, 4, 128], BF16)
        tpe = sb("_tpe", [128, 128], BF16)
        ade = sb("_ade", [128, 128], BF16)
        wqb = [sb(f"_wq{i}", [128, 16, 128], BF16) for i in range(2)]
        wkb = [sb(f"_wk{i}", [128, 16, 128], BF16) for i in range(2)]
        wvb = [sb(f"_wv{i}", [128, 16, 128], BF16) for i in range(2)]

        with ExitStack() as pctx:
            ps_p = [pctx.enter_context(
                nc.psum_tensor(tag + f"psp{i}", [128, 512], F32))
                for i in (0, 1)]
            ps_s = pctx.enter_context(nc.psum_tensor(tag + "pss", [128, 512], F32))
            ps_t = pctx.enter_context(nc.psum_tensor(tag + "pst", [128, 128], BF16))
            ps_d = pctx.enter_context(nc.psum_tensor(tag + "psd", [128, 128], F32))

            pe_end = {}
            proj_evac = {}
            prev_proj_evac = [None, None]
            sq_stash_m = [None, None]
            sk_stash_m = [None, None]
            vvst_last = {}
            st = {"last_tp_evac": None, "last_adot_store": None,
                  "last_score_read": None}

            def stage1(h):
                hb = h % 2
                guard = [wmark, nm] if h < 2 else [pe_end[h - 2]]
                mwq = p.dma("sync", lambda s, h=h, hb=hb: s.dma_start(
                    out=wqb[hb][:], in_=w_chunk(0, h)), wait=guard,
                    stream="s2wq", slot=hb)
                mwk = p.dma("sync", lambda s, h=h, hb=hb: s.dma_start(
                    out=wkb[hb][:], in_=w_chunk(1, h)), wait=guard,
                    stream="s2wk", slot=hb)
                mwv = p.dma("sync", lambda s, h=h, hb=hb: s.dma_start(
                    out=wvb[hb][:], in_=w_chunk(2, h)), wait=guard,
                    stream="s2wv", slot=hb)
                for widx, (wb_, dst, wm_) in enumerate(
                        ((wqb, qh, mwq), (wkb, kh, mwk), (wvb, vh, mwv))):
                    for ts in range(4):
                        tl = slice(ts * 512, (ts + 1) * 512)
                        for kc in range(16):
                            mm = p.op(
                                "tensor",
                                lambda t, kc=kc, hb=hb, tl=tl, wb_=wb_:
                                t.matmul(ps_p[hb][:], wb_[hb][:, kc, :],
                                         xbuf[:, kc, tl],
                                         start=(kc == 0), stop=(kc == 15)),
                                wait=([wm_, nm, prev_proj_evac[hb]]
                                      if kc == 0 else ()),
                                inc="pe" if kc == 15 else None)
                        proj_evac[(h, widx, ts)] = prev_proj_evac[hb] = p.op(
                            "vector",
                            lambda v, dst=dst, tl=tl, hb=hb: v.tensor_copy(
                                dst[hb][:, tl], ps_p[hb][:]),
                            wait=[mm] + ([pe_end[h - 2]]
                                         if widx == 0 and ts == 0 and h >= 2
                                         else []),
                            inc="dve")

            def stage23(h):
                hb = h % 2
                qm = proj_evac[(h, 0, 3)]
                km = proj_evac[(h, 1, 3)]
                vm = proj_evac[(h, 2, 3)]

                def sigma(src, dst, wmk, stash_guard):
                    p.op("vector", lambda v, src=src: v.tensor_single_scalar(
                        t32a[:], src[hb][:], 0.0, ALU.min),
                        wait=[wmk], inc="dve")
                    am = p.op("scalar", lambda s: s.activation(
                        out=t32a[:], in_=t32a[:], func=AF.Exp),
                        wait=[p.mark("dve")], inc="act")
                    p.op("vector", lambda v, src=src: v.tensor_single_scalar(
                        t32b[:], src[hb][:], 0.0, ALU.max), inc="dve")
                    return p.op("vector", lambda v, dst=dst: v.tensor_add(
                        dst[hb][:], t32a[:], t32b[:]),
                        wait=[am, stash_guard], inc="dve")

                sqm = sigma(qh, sqh, qm, sq_stash_m[hb])
                skm = sigma(kh, skh, km, sk_stash_m[hb])
                sq_stash_m[hb] = p.dma("gpsimd", lambda g, h=h, hb=hb:
                                       g.dma_start(
                                           out=sq_d.ap()[h], in_=sqh[hb][:]),
                                       wait=[sqm], stream="s2sqst", slot=hb)
                sk_stash_m[hb] = p.dma("gpsimd", lambda g, h=h, hb=hb:
                                       g.dma_start(
                                           out=sk_d.ap()[h], in_=skh[hb][:]),
                                       wait=[skm], stream="s2skst", slot=hb)
                for sg in range(4):
                    p.op("vector", lambda v, h=h, sg=sg, hb=hb: v.reduce_sum(
                        colsum[:, h * 4 + sg:h * 4 + sg + 1],
                        skh[hb][:, sg * 512:(sg + 1) * 512], AXL.X),
                        wait=[skm] if sg == 0 else (), inc="dve")
                stm1 = None
                vvst_prev = vvst_last.get(h - 2)
                for tc in range(16):
                    t1 = p.op("tensor", lambda t, tc=tc, hb=hb: t.transpose(
                        ps_t[:], skh[hb][:, tc * 128:(tc + 1) * 128],
                        ident[:]),
                        wait=[skm, idm, st["last_tp_evac"]], inc="pe")
                    e1 = p.op("vector", lambda v: v.tensor_copy(
                        tpe[:], ps_t[:]), wait=[t1, stm1], inc="dve")
                    stm1 = p.dma("gpsimd", lambda g, h=h, tc=tc: g.dma_start(
                        out=sktm_d.ap()[h, tc], in_=tpe[:]), wait=[e1],
                        stream="s2tpst")
                    t2 = p.op("tensor", lambda t, tc=tc, hb=hb: t.transpose(
                        ps_t[:], vh[hb][:, tc * 128:(tc + 1) * 128],
                        ident[:]),
                        wait=[vm, e1], inc="pe")
                    e2 = p.op("vector", lambda v, tc=tc, hb=hb: v.tensor_copy(
                        vvtm[hb][:, tc, :], ps_t[:]),
                        wait=[t2, stm1, vvst_prev], inc="dve")
                    vvst_last[h] = p.dma(
                        "gpsimd", lambda g, h=h, tc=tc, hb=hb: g.dma_start(
                            out=vvtm_d.ap()[h, tc], in_=vvtm[hb][:, tc, :]),
                        wait=[e2], stream="s2vvst", slot=hb)
                    st["last_tp_evac"] = e2
                for sg in range(4):
                    sgl = slice(sg * 512, (sg + 1) * 512)
                    for sc in range(4):
                        scm = p.op(
                            "tensor", lambda t, sc=sc, sg=sg, sgl=sgl, hb=hb:
                            t.matmul(
                                ps_s[:],
                                qh[hb][:, sg * 512 + sc * 128:
                                       sg * 512 + (sc + 1) * 128],
                                kh[hb][:, sgl], start=True, stop=True),
                            wait=[qm, km, st["last_score_read"]], inc="pe")
                        p.op("vector", lambda v, sc=sc: v.tensor_add(
                            scob[:], ps_s[:], masks[:, sc, :]),
                            wait=[scm, mkm], inc="dve")
                        st["last_score_read"] = p.mark("dve")
                        p.op("vector", lambda v: v.tensor_reduce(
                            nmax[:], scob[:], AXL.X, ALU.max, negate=True),
                            inc="dve")
                        nsm = p.op("vector", lambda v: v.tensor_scalar_mul(
                            nmaxs[:], nmax[:], INV_SCALE_ATT), inc="dve")
                        em = p.op("scalar", lambda s: s.activation(
                            out=scob[:], in_=scob[:], func=AF.Exp,
                            bias=nmaxs[:], scale=INV_SCALE_ATT),
                            wait=[nsm], inc="act")
                        p.op("vector", lambda v: v.reduce_sum(
                            rsum[:], scob[:], AXL.X), wait=[em], inc="dve")
                        p.op("vector", lambda v: v.reciprocal(
                            rrec[:], rsum[:]), inc="dve")
                        pnm = p.op("vector", lambda v: v.tensor_scalar_mul(
                            pnb[:], scob[:], rrec[:]),
                            wait=[st["last_tp_evac"]], inc="dve")
                        for tb in range(4):
                            t3 = p.op(
                                "tensor", lambda t, tb=tb: t.transpose(
                                    ps_t[:], pnb[:, tb * 128:(tb + 1) * 128],
                                    ident[:]),
                                wait=[pnm, st["last_tp_evac"]], inc="pe")
                            st["last_tp_evac"] = p.op(
                                "vector", lambda v, tb=tb: v.tensor_copy(
                                    ptb[:, tb, :], ps_t[:]),
                                wait=[t3], inc="dve")
                        for tb in range(4):
                            admm = p.op(
                                "tensor",
                                lambda t, tb=tb, sg=sg, hb=hb: t.matmul(
                                    ps_d[:], ptb[:, tb, :],
                                    vvtm[hb][:, sg * 4 + tb, :],
                                    start=(tb == 0), stop=(tb == 3)),
                                wait=[st["last_tp_evac"],
                                      st["last_adot_store"]]
                                if tb == 0 else (),
                                inc="pe" if tb == 3 else None)
                        adm = p.op("vector", lambda v: v.tensor_copy(
                            ade[:], ps_d[:]), wait=[admm], inc="dve")
                        st["last_adot_store"] = p.dma(
                            "gpsimd", lambda g, h=h, sg=sg, sc=sc:
                            g.dma_start(out=adot_d.ap()[h, sg * 4 + sc],
                                        in_=ade[:]), wait=[adm],
                            stream="s2adst")
                pe_end[h] = p.milestone("tensor")

            stage1(0)
            for h in range(16):
                if h + 1 < 16:
                    stage1(h + 1)
                stage23(h)
            csm = p.dma("gpsimd", lambda g: g.dma_start(
                out=cc1_in.ap(), in_=colsum[:]), wait=[p.mark("dve")])
            ag1 = p.op("gpsimd", lambda g: g.collective_compute(
                "AllGather", ALU.bypass, replica_groups=[list(range(NCORE))],
                ins=[cc1_in.ap().opt()], outs=[cc1_out.ap().opt()]),
                wait=[csm], inc="cc")
        p.barrier()

    if _S2_STOP is not None and _S2_STOP != "b":
        return {}
    with ExitStack() as ctx:
        sb = lambda n, s, d: ctx.enter_context(nc.sbuf_tensor(tag + n, s, d))
        csall = sb("_csall", [128, 8, 64], F32)
        zmask = sb("_zmasksb", [128, 32], F32)
        onesr = sb("_onesr", [1, 128], F32)
        zmrow = sb("_zmrow", [1, 32], F32)
        skfm = sb("_skfm", [128, TLOC], BF16)
        sktm = sb("_sktmsb", [128, 16, 128], BF16)
        vvtm2 = sb("_vvtm2", [128, 16, 128], BF16)
        skdt = sb("_skdt", [128, 4, 128], BF16)
        dcols = sb("_dcols", [128, 16], F32)
        zt = sb("_zt", [128, 32], F32)
        zred = sb("_zred", [128, 1], F32)
        ab = sb("_ab", [128, 128], BF16)
        atb = sb("_atb", [128, 128], BF16)
        qb = sb("_qb", [128, 128], BF16)
        asg = sb("_asg", [128, 4, 128], BF16)
        qsg = sb("_qsg", [128, 4, 128], BF16)
        cmp32 = sb("_cmp32", [128, 128], F32)
        cmp32b = sb("_cmp32b", [128, 128], F32)

        om = p.op("vector", lambda v: v.memset(onesr[:], 1.0), inc="dve")
        zrm = p.dma("sync", lambda s: s.dma_start(
            out=zmrow[:], in_=consts["zmask"]), stream="s2zm")
        csm2 = p.dma("sync", lambda s: s.dma_start(
            out=csall[:], in_=cc1_out.ap().rearrange(
                "(r p) c -> p r c", p=128)), wait=[ag1], stream="s2cs")
        with ExitStack() as pctx:
            ps_b = pctx.enter_context(
                nc.psum_tensor(tag + "psb", [128, 32], F32))
            bm = p.op("tensor", lambda t: t.matmul(
                ps_b[:], onesr[:], zmrow[:], start=True, stop=True),
                wait=[om, zrm, csm2], inc="pe")
            zmm = p.op("vector", lambda v: v.tensor_copy(zmask[:], ps_b[:]),
                       wait=[bm], inc="dve")

        with ExitStack() as pctx:
            ps_z = pctx.enter_context(
                nc.psum_tensor(tag + "psz", [128, 16], F32))
            ps_a = pctx.enter_context(
                nc.psum_tensor(tag + "psa", [128, 128], F32))
            ps_q = pctx.enter_context(
                nc.psum_tensor(tag + "psq", [128, 128], F32))
            ps_c = pctx.enter_context(
                nc.psum_tensor(tag + "psc", [128, 128], F32))

            last_mm_2b = None
            last_dve_2b = None
            aqst_last = None
            pub_at_m = None
            pub_qb_m = None
            for h in range(16):
                lguard = [last_mm_2b, last_dve_2b] if h > 0 else []
                lm1 = p.dma("sync", lambda s, h=h: s.dma_start(
                    out=skfm[:], in_=sk_d.ap()[h]), wait=lguard,
                    stream="s2skfm")
                lm2 = p.dma("sync", lambda s, h=h: s.dma_start(
                    out=sktm[:], in_=sktm_d.ap()[h].rearrange(
                        "tc p m -> p tc m")), stream="s2sktmld")
                lm3 = p.dma("sync", lambda s, h=h: s.dma_start(
                    out=vvtm2[:], in_=vvtm_d.ap()[h].rearrange(
                        "tc p m -> p tc m")), stream="s2vvtmld")
                p.op("vector", lambda v, h=h: v.tensor_mul(
                    zt[:].rearrange("p (r s) -> p r s", r=8),
                    csall[:, :, h * 4:h * 4 + 4], zmask[:].rearrange(
                        "p (r s) -> p r s", r=8)),
                    wait=[csm2, zmm], inc="dve")
                p.op("vector", lambda v: v.reduce_sum(
                    zred[:], zt[:], AXL.X), inc="dve")
                p.op("vector", lambda v, h=h: v.tensor_scalar_add(
                    zall[:, h * 4:h * 4 + 1], zred[:], 1.0 / 128.0),
                    inc="dve")
                for s2 in range(1, 4):
                    p.op("vector", lambda v, h=h, s2=s2: v.tensor_add(
                        zall[:, h * 4 + s2:h * 4 + s2 + 1],
                        zall[:, h * 4 + s2 - 1:h * 4 + s2],
                        colsum[:, h * 4 + s2 - 1:h * 4 + s2]), inc="dve")
                p.op("vector", lambda v, h=h: v.tensor_copy(
                    zallbf[:, h * 4:h * 4 + 4], zall[:, h * 4:h * 4 + 4]),
                    inc="dve")
                zm = p.mark("dve")
                for j in range(16):
                    sg2 = j // 4
                    zc = p.op("tensor", lambda t, j=j, sg2=sg2, h=h: t.matmul(
                        ps_z[:, j:j + 1],
                        skfm[:, j * 128:(j + 1) * 128],
                        zallbf[:, h * 4 + sg2:h * 4 + sg2 + 1],
                        start=(j == 0), stop=(j == 15),
                        skip_group_check=True),
                        wait=[lm1, zm] if j == 0 else (),
                        inc="pe" if j == 15 else None)
                dcm = p.op("vector", lambda v: v.reciprocal(
                    dcols[:], ps_z[:]), wait=[zc], inc="dve")
                evacA = {}
                eq_prev = None
                aqst_prev = aqst_last
                for sg2 in range(4):
                    for tcw in range(4):
                        j = sg2 * 4 + tcw
                        p.op("vector", lambda v, j=j, tcw=tcw: v.tensor_scalar(
                            skdt[:, tcw, :], sktm[:, j, :],
                            dcols[:, j:j + 1], None, ALU.mult, ALU.bypass),
                            wait=[dcm, lm2] if j == 0 else (), inc="dve")
                    sdm = p.mark("dve")
                    for tcw in range(4):
                        j = sg2 * 4 + tcw
                        am = p.op("tensor", lambda t, j=j, tcw=tcw: t.matmul(
                            ps_a[:], sktm[:, j, :], skdt[:, tcw, :],
                            start=(tcw == 0), stop=(tcw == 3)),
                            wait=[sdm] if tcw == 0 else (),
                            inc="pe" if tcw == 3 else None)
                    for tcw in range(4):
                        j = sg2 * 4 + tcw
                        qmm = p.op("tensor", lambda t, j=j: t.matmul(
                            ps_q[:], sktm[:, j, :], vvtm2[:, j, :],
                            start=(j % 4 == 0), stop=(j % 4 == 3)),
                            wait=([lm3] if sg2 == 0 else [eq_prev])
                            if tcw == 0 else (),
                            inc="pe" if j % 4 == 3 else None)
                    ea = p.op("vector", lambda v, sg2=sg2: v.tensor_copy(
                        asg[:, sg2, :], ps_a[:]),
                        wait=[am, aqst_prev], inc="dve")
                    eq = p.op("vector", lambda v, sg2=sg2: v.tensor_copy(
                        qsg[:, sg2, :], ps_q[:]),
                        wait=[qmm, aqst_prev], inc="dve")
                    evacA[sg2] = (ea, eq)
                    eq_prev = eq
                    p.dma(
                        "gpsimd", lambda g, h=h, sg2=sg2: g.dma_start(
                            out=aq_d.ap()[h, sg2, 0], in_=asg[:, sg2, :]),
                        wait=[ea], stream="s2aqst")
                    aqst_last = p.dma(
                        "gpsimd", lambda g, h=h, sg2=sg2: g.dma_start(
                            out=aq_d.ap()[h, sg2, 1], in_=qsg[:, sg2, :]),
                        wait=[eq], stream="s2aqst")
                c0 = p.op("vector", lambda v: v.tensor_copy(
                    ab[:], asg[:, 0, :]), wait=[evacA[0][0]], inc="dve")
                p.op("vector", lambda v: v.tensor_copy(
                    atb[:], asg[:, 0, :]), wait=[pub_at_m], inc="dve")
                cm = p.op("vector", lambda v: v.tensor_copy(
                    qb[:], qsg[:, 0, :]),
                    wait=[evacA[0][1], pub_qb_m], inc="dve")
                for s2 in range(1, 4):
                    m1 = p.op("tensor", lambda t, s2=s2: t.matmul(
                        ps_c[:], asg[:, s2, :], ab[:],
                        start=True, stop=True),
                        wait=[cm, evacA[s2][0], evacA[s2][1]], inc="pe")
                    m2 = p.op("tensor", lambda t, s2=s2: t.matmul(
                        ps_a[:], ab[:], asg[:, s2, :],
                        start=True, stop=True), inc="pe")
                    m3 = p.op("tensor", lambda t, s2=s2: t.matmul(
                        ps_q[:], asg[:, s2, :], qb[:],
                        start=True, stop=True), inc="pe")
                    p.op("vector", lambda v, s2=s2: v.tensor_add(
                        cmp32[:], asg[:, s2, :], ab[:]), wait=[m3], inc="dve")
                    p.op("vector", lambda v: v.tensor_sub(
                        ab[:], cmp32[:], ps_c[:]), inc="dve")
                    p.op("vector", lambda v, s2=s2: v.tensor_add(
                        cmp32b[:], asg[:, s2, :], atb[:]), inc="dve")
                    p.op("vector", lambda v: v.tensor_sub(
                        atb[:], cmp32b[:], ps_a[:]), inc="dve")
                    p.op("vector", lambda v, s2=s2: v.tensor_add(
                        cmp32[:], qsg[:, s2, :], qb[:]), inc="dve")
                    cm = p.op("vector", lambda v: v.tensor_sub(
                        qb[:], cmp32[:], ps_q[:]), inc="dve")
                    last_mm_2b = m3
                pub_at_m = p.dma("gpsimd", lambda g, h=h: g.dma_start(
                    out=cc2_in.ap()[:, h * 256:h * 256 + 128], in_=atb[:]),
                    wait=[cm], stream="s2pub", slot=0)
                last_dve_2b = cm
                pub_qb_m = p.dma("gpsimd", lambda g, h=h: g.dma_start(
                    out=cc2_in.ap()[:, h * 256 + 128:h * 256 + 256],
                    in_=qb[:]), wait=(), stream="s2pub", slot=1)
            ag2 = p.op("gpsimd", lambda g: g.collective_compute(
                "AllGather", ALU.bypass, replica_groups=[list(range(NCORE))],
                ins=[cc2_in.ap().opt()], outs=[cc2_out.ap().opt()]),
                wait=[pub_at_m, pub_qb_m], inc="cc")
        p.barrier()

    if _S2_STOP == "b":
        return {}
    with ExitStack() as ctx:
        sb = lambda n, s, d: ctx.enter_context(nc.sbuf_tensor(tag + n, s, d))
        wob = sb("_wob", [128, 16, 16, 128], BF16)
        attout = sb("_attout", [128, 16, TLOC], BF16)
        smask = sb("_smasksb", [128, 8], F32)
        smrow = sb("_smrow", [1, 8], F32)
        onesr2 = sb("_onesr2", [1, 128], F32)
        betar = sb("_betar", [1, 2048], F32)
        betab = sb("_betab", [128, 16, 128], F32)
        agb = sb("_agb", [128, 8, 256], BF16)
        sqh2 = sb("_sqh2", [128, TLOC], BF16)
        adb = sb("_adb", [128, 16, 128], BF16)
        asg2 = sb("_asg2", [128, 4, 128], BF16)
        qsg2 = sb("_qsg2", [128, 4, 128], BF16)
        m32 = sb("_m32", [128, 128], F32)
        mbf = sb("_mbf", [128, 128], BF16)
        qd = sb("_qd", [128, 16], F32)
        am32 = sb("_am32", [128, 128], F32)
        t3b = sb("_t3b", [128, 128], F32)
        attb = sb("_attb", [128, 128], BF16)
        attfm = sb("_attfm", [128, TLOC], BF16)
        xres2 = sb("_xres2", [128, 512], F32)
        outb2 = sb("_outb2", [128, 512], F32)

        om2 = p.op("vector", lambda v: v.memset(onesr2[:], 1.0), inc="dve")
        smm = p.dma("sync", lambda s: s.dma_start(
            out=smrow[:], in_=consts["smask"]), stream="s2sm")
        btm = p.dma("sync", lambda s: s.dma_start(
            out=betar[:], in_=consts["betas"]), stream="s2bt")
        btsig = p.op("scalar", lambda s: s.activation(
            out=betar[:], in_=betar[:], func=AF.Sigmoid),
            wait=[btm], inc="act")
        wo_marks = []
        for oc in range(16):
            wo_marks.append(p.dma(
                "scalar", lambda s, oc=oc: s.dma_start(
                    out=wob[:, oc, :, :], in_=w_chunk(3, oc)), wait=[wmark]))
        wom = wo_marks[-1]
        with ExitStack() as pctx:
            ps_bc = pctx.enter_context(
                nc.psum_tensor(tag + "psbc", [128, 128], F32))
            bmk = None
            for hh in range(16):
                bmk = p.op("tensor", lambda t, hh=hh: t.matmul(
                    ps_bc[:], onesr2[:], betar[:, hh * 128:(hh + 1) * 128],
                    start=True, stop=True),
                    wait=[om2, btsig] if hh == 0 else [p.mark("dve")],
                    inc="pe")
                p.op("vector", lambda v, hh=hh: v.tensor_copy(
                    betab[:, hh, :], ps_bc[:]), wait=[bmk], inc="dve")
            smb = p.op("tensor", lambda t: t.matmul(
                ps_bc[:, 0:8], onesr2[:], smrow[:], start=True, stop=True),
                wait=[smm, p.mark("dve")], inc="pe")
            p.op("vector", lambda v: v.tensor_copy(smask[:], ps_bc[:, 0:8]),
                 wait=[smb], inc="dve")

        with ExitStack() as pctx:
            ps_f = pctx.enter_context(
                nc.psum_tensor(tag + "psf", [128, 128], F32))
            ps_zq = pctx.enter_context(
                nc.psum_tensor(tag + "pszq", [128, 16], F32))
            ps_m = pctx.enter_context(
                nc.psum_tensor(tag + "psm", [128, 128], F32))
            ps_t2 = pctx.enter_context(
                nc.psum_tensor(tag + "pst2", [128, 128], BF16))
            ps_w = [pctx.enter_context(
                nc.psum_tensor(tag + f"psw{i}", [128, 128], F32))
                for i in range(2)]

            pe_prev = None
            dve_prev = None
            for h in range(16):
                lg = [pe_prev, dve_prev] if h > 0 else [ag2]
                lag = p.dma("sync", lambda s, h=h: s.dma_start(
                    out=agb[:], in_=cc2_out.ap().rearrange(
                        "(r p) c -> p r c", p=128)[
                        :, :, h * 256:(h + 1) * 256]),
                    wait=lg + [ag2] if h == 0 else lg)
                lsq = p.dma("sync", lambda s, h=h: s.dma_start(
                    out=sqh2[:], in_=sq_d.ap()[h]))
                lad = p.dma("sync", lambda s, h=h: s.dma_start(
                    out=adb[:], in_=adot_d.ap()[h].rearrange(
                        "tc p m -> p tc m")))
                laq = p.dma("sync", lambda s, h=h: s.dma_start(
                    out=asg2[:], in_=aq_d.ap()[h, :, 0].rearrange(
                        "sg p m -> p sg m")))
                laq2 = p.dma("sync", lambda s, h=h: s.dma_start(
                    out=qsg2[:], in_=aq_d.ap()[h, :, 1].rearrange(
                        "sg p m -> p sg m")))
                hlm = p.mark("d_sync")
                mm0 = p.op("vector", lambda v: v.memset(m32[:], 0.0),
                           wait=[pe_prev] if h > 0 else [], inc="dve")
                mmb = p.op("vector", lambda v: v.memset(mbf[:], 0.0),
                           inc="dve")
                for j in range(NCORE):
                    fm = p.op("tensor", lambda t, j=j: t.matmul(
                        ps_f[:], agb[:, j, 0:128], mbf[:],
                        start=True, stop=True),
                        wait=[hlm, mmb] if j == 0 else [mmb], inc="pe")
                    p.op("vector", lambda v, j=j: v.tensor_sub(
                        t3b[:], agb[:, j, 128:256], ps_f[:]),
                        wait=[fm], inc="dve")
                    p.op("vector", lambda v, j=j: v.scalar_tensor_tensor(
                        m32[:], t3b[:], smask[:, j:j + 1], m32[:],
                        ALU.mult, ALU.add), inc="dve")
                    mmb = p.op("vector", lambda v: v.tensor_copy(
                        mbf[:], m32[:]), inc="dve")
                for j in range(16):
                    sg2 = j // 4
                    zq = p.op("tensor", lambda t, j=j, sg2=sg2, h=h: t.matmul(
                        ps_zq[:, j:j + 1],
                        sqh2[:, j * 128:(j + 1) * 128],
                        zallbf[:, h * 4 + sg2:h * 4 + sg2 + 1],
                        start=(j == 0), stop=(j == 15),
                        skip_group_check=True),
                        wait=[hlm] if j == 0 else (),
                        inc="pe" if j == 15 else None)
                qdm = p.op("vector", lambda v: v.reciprocal(
                    qd[:], ps_zq[:]), wait=[zq], inc="dve")
                tpev_m = None
                amev = None
                for sg2 in range(4):
                    for tcw in range(4):
                        j = sg2 * 4 + tcw
                        amm = p.op("tensor", lambda t, j=j: t.matmul(
                            ps_m[:], sqh2[:, j * 128:(j + 1) * 128], mbf[:],
                            start=True, stop=True),
                            wait=[mmb, qdm, amev] if j == 0
                            else [mmb, amev], inc="pe")
                        amev = p.op("vector", lambda v, j=j: v.tensor_scalar(
                            am32[:], ps_m[:], qd[:, j:j + 1], None,
                            ALU.mult, ALU.bypass), wait=[amm, hlm],
                            inc="dve")
                        p.op("vector", lambda v, j=j: v.tensor_sub(
                            t3b[:], am32[:], adb[:, j, :]), inc="dve")
                        p.op("vector", lambda v, h=h: v.tensor_mul(
                            t3b[:], t3b[:], betab[:, h, :]), inc="dve")
                        abm = p.op("vector", lambda v, j=j: v.tensor_add(
                            attb[:], t3b[:], adb[:, j, :]),
                            wait=[tpev_m], inc="dve")
                        tpm = p.op("tensor", lambda t, j=j: t.transpose(
                            ps_t2[:], attb[:], ident[:]),
                            wait=[abm], inc="pe")
                        tpev_m = p.op("vector", lambda v, j=j: v.tensor_copy(
                            attfm[:, j * 128:(j + 1) * 128], ps_t2[:]),
                            wait=[tpm], inc="dve")
                    if sg2 < 3:
                        um = p.op("tensor", lambda t, sg2=sg2: t.matmul(
                            ps_f[:], asg2[:, sg2, :], mbf[:],
                            start=True, stop=True),
                            wait=[hlm] if sg2 == 0 else (), inc="pe")
                        p.op("vector", lambda v, sg2=sg2: v.tensor_sub(
                            t3b[:], qsg2[:, sg2, :], ps_f[:]),
                            wait=[um], inc="dve")
                        p.op("vector", lambda v: v.tensor_add(
                            m32[:], m32[:], t3b[:]), inc="dve")
                        mmb = p.op("vector", lambda v: v.tensor_copy(
                            mbf[:], m32[:]), inc="dve")
                att_r = attfm[:].rearrange(
                    "p (sg sb k) -> p k sg sb", sg=4, sb=32, k=16)
                out_r = attout[:].rearrange(
                    "p dc (sg q sb) -> p dc sg q sb", sg=4, q=16, sb=32)
                wo_ev = None
                for dc in range(16):
                    for kc in range(16):
                        wmm = p.op(
                            "tensor", lambda t, dc=dc, kc=kc: t.matmul(
                                ps_w[dc % 2][:].rearrange(
                                    "p (sg sb) -> p sg sb", sg=4),
                                wob[:, dc, kc, :], att_r[:, kc],
                                start=(kc == 0), stop=(kc == 15)),
                            wait=([tpev_m, wom] if dc == 0 else
                                  [p.mark("dve")]) if kc == 0 else (),
                            inc="pe" if kc == 15 else None)
                    wo_ev = p.op("vector", lambda v, dc=dc, h=h: v.tensor_copy(
                        out_r[:, dc, :, h, :],
                        ps_w[dc % 2][:].rearrange("p (sg sb) -> p sg sb",
                                                  sg=4)),
                        wait=[wmm], inc="dve")
                pe_prev = p.milestone("tensor")
                dve_prev = wo_ev
            som = None
            for dc in range(16):
                for ts in range(4):
                    tl = slice(ts * 512, (ts + 1) * 512)
                    xm3 = p.dma("sync", lambda s, dc=dc, tl=tl: s.dma_start(
                        out=xres2[:],
                        in_=x_in.rearrange("(kc p) t -> p kc t", p=128)[
                            :, dc, tl]),
                        wait=[som], stream="s2xres")
                    aom = p.op("vector", lambda v, dc=dc, tl=tl: v.tensor_add(
                        outb2[:], attout[:, dc, tl], xres2[:]),
                        wait=[xm3, dve_prev], inc="dve")
                    som = p.dma("gpsimd", lambda g, dc=dc, tl=tl: g.dma_start(
                        out=x_out.rearrange("(kc p) t -> p kc t", p=128)[
                            :, dc, tl],
                        in_=outb2[:]), wait=[aom], stream="s2xst")
        p.barrier()
    octx.close()
    return {}


def prep_attn0_weights(Wq, Wk, Wv, Wo):
    ts_ = [tile_w_in_out(np.asarray(W), 16, 16) for W in (Wq, Wk, Wv, Wo)]
    slices = []
    for r in range(NCORE):
        s = np.concatenate([t[2 * r:2 * r + 2].ravel() for t in ts_])
        slices.append(np.ascontiguousarray(s))
    return slices


def make_attn0_launch():
    nc = bass.Bass("TRN2", target_bir_lowering=False, debug=False)
    ctx = ExitStack()
    x_in = nc.dram_tensor("x", [D, TLOC], F32, kind="ExternalInput").ap()
    per_rank = 4 * 2 * 128 * 16 * 128
    wflat = nc.dram_tensor("wflat", [per_rank], F32, kind="ExternalInput").ap()
    ident = nc.dram_tensor("ident", [128, 128], BF16, kind="ExternalInput").ap()
    masks = nc.dram_tensor("masks", [4, 128, 512], F32, kind="ExternalInput").ap()
    zmask = nc.dram_tensor("zmask", [1, 32], F32, kind="ExternalInput").ap()
    smask = nc.dram_tensor("smask", [1, 8], F32, kind="ExternalInput").ap()
    betas = nc.dram_tensor("betas", [1, 2048], F32, kind="ExternalInput").ap()
    x_out = nc.dram_tensor("xout", [D, TLOC], F32, kind="ExternalOutput").ap()
    consts = {"ident": ident, "masks": masks, "zmask": zmask, "smask": smask,
              "betas": betas}
    with ctx:
        p = Prog(nc, ctx)
        build_attn0(nc, p, x_in, x_out, wflat, consts)
        p.finalize([p.mark("d_gpsimd")])
    return nc


def core_masks():
    zs, ss = [], []
    for c in range(NCORE):
        b, qt = c // 4, c % 4
        zm = np.zeros((1, 32), np.float32)
        sm = np.zeros((1, 8), np.float32)
        for r in range(NCORE):
            if r // 4 != b:
                continue
            if r % 4 < qt:
                sm[0, r] = 1.0
            for sg in range(4):
                if 4 * (r % 4) + sg < 4 * qt:
                    zm[0, r * 4 + sg] = 1.0
        zs.append(zm)
        ss.append(sm)
    return zs, ss


def run_attn0(x_slices, Wq, Wk, Wv, Wo, betas):
    nc = make_attn0_launch()
    wsl = prep_attn0_weights(Wq, Wk, Wv, Wo)
    idb = np.eye(128, dtype=np.float32).astype(ml_dtypes.bfloat16)
    msk = causal_masks()
    zs, ss = core_masks()
    brow = np.ascontiguousarray(
        np.asarray(betas).reshape(16, 128).reshape(1, 2048).astype(np.float32))
    in_maps = [{"x": x_slices[c], "wflat": wsl[c], "ident": idb,
                "masks": msk, "zmask": zs[c], "smask": ss[c], "betas": brow}
               for c in range(NCORE)]
    res = run_bass_kernel_spmd(nc, in_maps, core_ids=list(range(NCORE)))
    return [res.results[c]["xout"] for c in range(NCORE)]


N_GU_OT = 11
N_DN_OT = 2
GU_ELEMS = N_GU_OT * 16 * 128 * 128
DN_ELEMS = N_DN_OT * 44 * 128 * 128
FFN1_PER_RANK = GU_ELEMS + DN_ELEMS


def dequant_block(nc, p, ctx, tag, pk_ext, sc_ext, n_ot, n_kc, wc_out,
                  wc_base, ident, idm):
    W = n_kc * 16
    sb = lambda n, s, d: ctx.enter_context(nc.sbuf_tensor(tag + n, s, d))
    pk = sb("_pk", [128, W], I32)
    m3i = sb("_m3i", [128, W], I32)
    sfi = sb("_sfi", [128, W], I32)
    m3f = sb("_m3f", [128, W], F32)
    sff = sb("_sff", [128, W], F32)
    t1b = sb("_t1", [128, W], F32)
    t2b = sb("_t2", [128, W], F32)
    vb = sb("_v", [128, W], F32)
    fb = sb("_f", [128, W], F32)
    dq = sb("_dq", [128, n_ot, n_kc * 128], F32)
    scb = sb("_scb", [128, n_ot * 128], F32)
    cvt = sb("_cvt", [128, 128], BF16)
    wt = sb("_wt", [128, n_ot, 128], BF16)
    with ExitStack() as pctx:
        ps_t = pctx.enter_context(nc.psum_tensor(tag + "pst", [128, 128], BF16))
        pk_m = None
        dq_m = None
        for ot in range(n_ot):
            pk_m = p.dma("sync", lambda s, ot=ot: s.dma_start(
                out=pk[:], in_=pk_ext[ot]),
                wait=[dq_m] if dq_m else [], stream=tag + "pk")
            for k in range(8):
                p.op("vector", lambda v, k=k: v.tensor_scalar(
                    m3i[:], pk[:], 4 * k, 7,
                    ALU.logical_shift_right, ALU.bitwise_and),
                    wait=[pk_m], inc="dve")
                p.op("vector", lambda v, k=k: v.tensor_scalar(
                    sfi[:], pk[:], 4 * k + 3, 1,
                    ALU.logical_shift_right, ALU.bitwise_and), inc="dve")
                p.op("vector", lambda v: v.tensor_copy(m3f[:], m3i[:]),
                     inc="dve")
                p.op("vector", lambda v: v.tensor_copy(sff[:], sfi[:]),
                     inc="dve")
                p.op("vector", lambda v: v.tensor_scalar(
                    t1b[:], m3f[:], -4.0, 0.0, ALU.add, ALU.max), inc="dve")
                p.op("vector", lambda v: v.tensor_scalar(
                    t2b[:], m3f[:], -6.0, 0.0, ALU.add, ALU.max), inc="dve")
                p.op("vector", lambda v: v.tensor_add(
                    vb[:], m3f[:], t1b[:]), inc="dve")
                p.op("vector", lambda v: v.scalar_tensor_tensor(
                    vb[:], t2b[:], 2.0, vb[:], ALU.mult, ALU.add), inc="dve")
                p.op("vector", lambda v: v.tensor_scalar(
                    fb[:], sff[:], -0.5, 0.25, ALU.mult, ALU.add), inc="dve")
                dq_m = p.op(
                    "vector", lambda v, ot=ot, k=k: v.tensor_mul(
                        dq[:, ot, :].rearrange("p (w e) -> p w e", e=8)
                        [:, :, k], vb[:], fb[:]), inc="dve")
        st_m = None
        last_ev = None
        for kc in range(n_kc):
            sc_m = p.dma("gpsimd", lambda g, kc=kc: g.dma_start(
                out=scb[:],
                in_=bass.AP(sc_ext.tensor,
                            sc_ext.offset + kc * (n_ot * 128),
                            [[0, 128], [1, n_ot * 128]])),
                wait=[st_m] if st_m else [dq_m],
                stream=tag + "sc", slot=kc % 2)
            for ot in range(n_ot):
                cm = p.op("vector", lambda v, kc=kc, ot=ot: v.tensor_copy(
                    cvt[:], dq[:, ot, kc * 128:(kc + 1) * 128]),
                    wait=[dq_m, last_ev], inc="dve")
                tm = p.op("tensor", lambda t: t.transpose(
                    ps_t[:], cvt[:], ident[:]),
                    wait=[cm, idm, last_ev], inc="pe")
                last_ev = p.op(
                    "vector", lambda v, kc=kc, ot=ot: v.tensor_tensor(
                        wt[:, ot, :], ps_t[:],
                        scb[:, ot * 128:(ot + 1) * 128], ALU.mult),
                    wait=[tm, sc_m], inc="dve")
            st_m = p.dma("gpsimd", lambda g, kc=kc: g.dma_start(
                out=bass.AP(wc_out.tensor,
                            wc_out.offset + wc_base + kc * 128,
                            [[n_kc * 128, 128], [n_kc * 16384, n_ot],
                             [1, 128]]),
                in_=wt[:]), wait=[last_ev], stream=tag + "st", slot=kc % 2)
        return st_m


def stage_ffn1_weights(nc, p, ins, tag="s4d"):
    cc_in = nc.dram_tensor(tag + "_cci", [FFN1_PER_RANK], BF16,
                           kind="Internal")
    cc_out = nc.dram_tensor(tag + "_cco", [NCORE * FFN1_PER_RANK], BF16,
                            kind="Internal")
    with ExitStack() as octx:
        ident = octx.enter_context(
            nc.sbuf_tensor(tag + "_id", [128, 128], BF16))
        idm = p.dma("gpsimd", lambda g: g.dma_start(
            out=ident[:], in_=ins["ident"]), stream=tag + "id")
        with ExitStack() as ctx:
            m1 = dequant_block(nc, p, ctx, tag + "g", ins["gu_pk"],
                               ins["gu_sc"], N_GU_OT, 16, cc_in.ap(), 0,
                               ident, idm)
        p.barrier(exclude=("cc",))
        with ExitStack() as ctx:
            m2 = dequant_block(nc, p, ctx, tag + "d", ins["dn_pk"],
                               ins["dn_sc"], N_DN_OT, 44, cc_in.ap(),
                               GU_ELEMS, ident, idm)
        agm = p.op("gpsimd", lambda g: g.collective_compute(
            "AllGather", ALU.bypass,
            replica_groups=[list(range(NCORE))],
            ins=[cc_in.ap().opt()], outs=[cc_out.ap().opt()]),
            wait=[m1, m2], inc="cc")
        p.barrier(exclude=("cc",))
    return (cc_out, agm)


def build_ffn1(nc, p, x_in, x_out, ins, tag="s4", staged=None):
    if staged is None:
        staged = stage_ffn1_weights(nc, p, ins, tag + "d")
    cc_out = staged[0]
    cco = cc_out.ap()

    def gu_chunkset(oc):
        r, loc = oc // N_GU_OT, oc % N_GU_OT
        off = r * FFN1_PER_RANK + loc * (16 * 16384)
        return bass.AP(cco.tensor, cco.offset + off,
                       [[16 * 128, 128], [128, 16], [1, 128]])

    def dn_chunkset(oc):
        r, loc = oc // N_DN_OT, oc % N_DN_OT
        off = r * FFN1_PER_RANK + GU_ELEMS + loc * (44 * 16384)
        return bass.AP(cco.tensor, cco.offset + off,
                       [[44 * 128, 128], [128, 44], [1, 128]])

    agm = staged[1]
    with ExitStack() as octx:
        p.barrier()

        with ExitStack() as ctx:
            sb = lambda n, s, d: ctx.enter_context(
                nc.sbuf_tensor(tag + n, s, d))
            xs_ = sb("_xs", [128, DC, 1024], BF16)
            hbuf = sb("_hb", [128, 44, 1024], BF16)
            ybf = xs_
            ln = LNCtx(nc, ctx, tag + "_ln", 1024)
            sqb = ln.sqbf
            silu_t = sb("_silu", [128, 512], F32)
            rmsrow = ln.m2row
            rb = ln.mbc
            tmp32 = ln.tmp32
            xres = ln.rbc
            ones1f = sb("_o1f", [1, 128], F32)
            onesb = sb("_o1b", [128, 1], BF16)
            wg_ = [sb(f"_wg{i}", [128, 16, 128], BF16) for i in range(2)]
            wu_ = [sb(f"_wu{i}", [128, 16, 128], BF16) for i in range(2)]
            wdb = [sb("_wd0", [128, 44, 128], BF16)] * 2
            p.op("vector", lambda v: v.memset(ones1f[:], 1.0), inc="dve")
            om = p.op("vector", lambda v: v.memset(onesb[:], 1.0), inc="dve")

            def bcast_row(row_ap, dst, ps_c, wait):
                b1 = p.op("tensor", lambda t: t.matmul(
                    ps_c[:, 0:512], ones1f[:], row_ap[:, 0:512],
                    start=True, stop=True), wait=wait, inc="pe")
                b2 = p.op("tensor", lambda t: t.matmul(
                    ps_c[:, 512:1024], ones1f[:], row_ap[:, 512:1024],
                    start=True, stop=True), inc="pe")
                return p.op("vector", lambda v: v.tensor_copy(
                    dst[:], ps_c[:]), wait=[b2], inc="dve")

            if True:
                st_m = None
                hm_prev = None
                for tsp in range(2):
                    t0 = tsp * 1024
                    xm2 = load_xbuf(p, x_in, xs_, t0=t0, T=1024, n_split=2,
                                    wait=[p.mark("pe"), p.mark("dve")]
                                    if tsp else ())
                    lnm = ln_compute(p, nc, ctx, ln, xs_, DC, xm2)
                    nmk = ln_normalize(p, ln, xs_, DC, lnm,
                                       extra=[st_m] if st_m else [])
                    pctx = ExitStack()
                    psum = lambda n, s, d: pctx.enter_context(
                        nc.psum_tensor(tag + n + str(tsp), s, d))
                    ps_g = [psum(f"psg{i}", [128, 512], F32) for i in (0, 1)]
                    ps_u = [psum(f"psu{i}", [128, 512], F32) for i in (0, 1)]
                    ps_c = psum("psc", [128, 1024], F32)
                    ps_s = psum("pss", [1, 1024], F32)
                    hmj = {}
                    slj = {}
                    for j in range(44):
                        jb = j % 2
                        gl = p.dma("scalar", lambda s, j=j, jb=jb: s.dma_start(
                            out=wg_[jb][:], in_=gu_chunkset(j)),
                            wait=[agm] if tsp == 0 and j == 0
                            else [hmj.get(j - 2) or hm_prev],
                            stream="f1wg", slot=jb)
                        ul = p.dma("sync", lambda s, j=j, jb=jb: s.dma_start(
                            out=wu_[jb][:], in_=gu_chunkset(44 + j)),
                            wait=[agm] if tsp == 0 and j == 0
                            else [hmj.get(j - 2) or hm_prev],
                            stream="f1wu", slot=jb)
                        for n2 in range(2):
                            nl = slice(n2 * 512, (n2 + 1) * 512)
                            for kc in range(16):
                                p.op("tensor",
                                     lambda t, kc=kc, nl=nl, jb=jb: t.matmul(
                                         ps_g[jb][:], wg_[jb][:, kc, :],
                                         xs_[:, kc, nl],
                                         start=(kc == 0), stop=(kc == 15)),
                                     wait=([gl, nmk, slj.get(j - 2)]
                                           if n2 == 0 else
                                           [slj.get((j, 0))])
                                     if kc == 0 else ())
                            for kc in range(16):
                                ua = p.op(
                                    "tensor",
                                    lambda t, kc=kc, nl=nl, jb=jb: t.matmul(
                                        ps_u[jb][:], wu_[jb][:, kc, :],
                                        xs_[:, kc, nl],
                                        start=(kc == 0), stop=(kc == 15)),
                                    wait=[ul, hmj.get((j - 2, 1) if n2 == 0
                                                      else (j, 0))]
                                    if kc == 0 else (),
                                    inc="pe" if kc == 15 else None)
                            slj[(j, n2)] = p.op(
                                "scalar", lambda s, jb=jb: s.activation(
                                    out=silu_t[:], in_=ps_g[jb][:],
                                    func=AF.Silu),
                                wait=[ua, hm_prev], inc="act")
                            hmj[(j, n2)] = hm_prev = p.op(
                                "vector",
                                lambda v, j=j, nl=nl, jb=jb: v.tensor_mul(
                                    hbuf[:, j, nl], silu_t[:], ps_u[jb][:]),
                                wait=[slj[(j, n2)]], inc="dve")
                        slj[j - 1] = slj.get((j - 1, 1))
                        hmj[j - 1] = hmj.get((j - 1, 1))
                    hmj[43] = hmj[(43, 1)]
                    slj[43] = slj[(43, 1)]
                    hmk = hm_prev
                    yev = {}
                    sm_ = None
                    for dc in range(16):
                        db = dc % 2
                        dl = p.dma("scalar", lambda s, dc=dc, db=db:
                                   s.dma_start(
                                       out=wdb[db][:], in_=dn_chunkset(dc)),
                                   wait=[yev.get(dc - 1) or hmk],
                                   stream="f1wd", slot=db)
                        for n2 in range(2):
                            nl = slice(n2 * 512, (n2 + 1) * 512)
                            for kc in range(44):
                                da = p.op(
                                    "tensor",
                                    lambda t, kc=kc, nl=nl, dc=dc, n2=n2,
                                    db=db: t.matmul(
                                        ps_g[db][:] if n2 == 0
                                        else ps_u[db][:],
                                        wdb[db][:, kc, :],
                                        hbuf[:, kc, nl],
                                        start=(kc == 0), stop=(kc == 43)),
                                    wait=[dl, hmk, yev.get(dc - 2)]
                                    if kc == 0 and n2 == 0 else (),
                                    inc="pe" if kc == 43 else None)
                        p.op("vector", lambda v, dc=dc, db=db: v.tensor_copy(
                            ybf[:, dc, 0:512], ps_g[db][:]), wait=[da],
                            inc="dve")
                        ye = p.op("vector", lambda v, dc=dc, db=db:
                                  v.tensor_copy(
                                      ybf[:, dc, 512:1024], ps_u[db][:]),
                                  inc="dve")
                        sq = p.op("scalar", lambda s, dc=dc: s.activation(
                            out=sqb[:], in_=ybf[:, dc, :], func=AF.Square),
                            wait=[ye], inc="act")
                        for n2 in range(2):
                            nl = slice(n2 * 512, (n2 + 1) * 512)
                            sm_ = p.op(
                                "tensor", lambda t, nl=nl, dc=dc: t.matmul(
                                    ps_s[:, nl], onesb[:], sqb[:, nl],
                                    start=(dc == 0), stop=(dc == 15),
                                    skip_group_check=True),
                                wait=[sq, om] if n2 == 0 else (),
                                inc="pe" if dc == 15 and n2 == 1 else None)
                        yev[dc] = ye
                    p.op("vector", lambda v: v.tensor_scalar(
                        rmsrow[:], ps_s[:], 1.0 / 2048.0, float(EPS_RMS),
                        ALU.mult, ALU.add), wait=[sm_], inc="dve")
                    lnr = p.op("scalar", lambda s: s.activation(
                        out=rmsrow[:], in_=rmsrow[:], func=AF.Ln),
                        wait=[p.mark("dve")], inc="act")
                    p.op("scalar", lambda s: s.activation(
                        out=rmsrow[:], in_=rmsrow[:], func=AF.Exp,
                        scale=-0.5), inc="act")
                    rbm = bcast_row(rmsrow, rb, ps_c,
                                    [p.mark("act"), p.mark("dve")])
                    xr_m = None
                    for dc in range(16):
                        xr_m = p.dma("sync", lambda s, dc=dc, t0=t0:
                                     s.dma_start(
                                         out=xres[:],
                                         in_=x_in.rearrange(
                                             "(kc p) t -> p kc t", p=128)[
                                             :, dc, t0:t0 + 1024]),
                                     wait=[st_m, rbm], stream="f1xres")
                        p.op("vector", lambda v, dc=dc: v.tensor_mul(
                            tmp32[:], ybf[:, dc, :], rb[:]),
                            wait=[rbm, st_m], inc="dve")
                        ob = p.op("vector",
                                  lambda v: v.scalar_tensor_tensor(
                                      tmp32[:], tmp32[:], 0.5, xres[:],
                                      ALU.mult, ALU.add),
                                  wait=[xr_m], inc="dve")
                        st_m = p.dma("gpsimd", lambda g, dc=dc, t0=t0:
                                     g.dma_start(
                                         out=x_out.rearrange(
                                             "(kc p) t -> p kc t", p=128)[
                                             :, dc, t0:t0 + 1024],
                                         in_=tmp32[:]),
                                     wait=[ob], stream="f1xst")
                    pctx.close()
        p.barrier()
    return (cc_out, agm)




def prep_ffn1_inputs(gu_pk, gu_sc, dn_pk, dn_sc):
    out = []
    for r in range(NCORE):
        g = np.ascontiguousarray(
            gu_pk[1408 * r:1408 * (r + 1)].reshape(N_GU_OT, 128, 256))
        gs = np.ascontiguousarray(gu_sc[:, 1408 * r:1408 * (r + 1)])
        d = np.ascontiguousarray(
            dn_pk[256 * r:256 * (r + 1)].reshape(N_DN_OT, 128, 704))
        ds = np.ascontiguousarray(dn_sc[:, 256 * r:256 * (r + 1)])
        out.append((g, gs, d, ds))
    return out


def make_ffn1_launch():
    nc = bass.Bass("TRN2", target_bir_lowering=False, debug=False)
    ctx = ExitStack()
    x_in = nc.dram_tensor("x", [D, TLOC], F32, kind="ExternalInput").ap()
    gu_pk = nc.dram_tensor("gu_pk", [N_GU_OT, 128, 256], I32,
                           kind="ExternalInput").ap()
    gu_sc = nc.dram_tensor("gu_sc", [16, 1408], F16, kind="ExternalInput").ap()
    dn_pk = nc.dram_tensor("dn_pk", [N_DN_OT, 128, 704], I32,
                           kind="ExternalInput").ap()
    dn_sc = nc.dram_tensor("dn_sc", [44, 256], F16, kind="ExternalInput").ap()
    ident = nc.dram_tensor("ident", [128, 128], BF16, kind="ExternalInput").ap()
    x_out = nc.dram_tensor("xout", [D, TLOC], F32, kind="ExternalOutput").ap()
    ins = {"gu_pk": gu_pk, "gu_sc": gu_sc, "dn_pk": dn_pk, "dn_sc": dn_sc,
           "ident": ident}
    with ctx:
        p = Prog(nc, ctx)
        build_ffn1(nc, p, x_in, x_out, ins)
        p.finalize([p.mark("ds_f1xst_0")])
    return nc


def run_ffn1(x_slices, gu_pk, gu_sc, dn_pk, dn_sc):
    nc = make_ffn1_launch()
    sl = prep_ffn1_inputs(gu_pk, gu_sc, dn_pk, dn_sc)
    idb = np.eye(128, dtype=np.float32).astype(ml_dtypes.bfloat16)
    in_maps = [{"x": x_slices[c], "gu_pk": sl[c][0], "gu_sc": sl[c][1],
                "dn_pk": sl[c][2], "dn_sc": sl[c][3], "ident": idb}
               for c in range(NCORE)]
    res = run_bass_kernel_spmd(nc, in_maps, core_ids=list(range(NCORE)))
    return [res.results[c]["xout"] for c in range(NCORE)]


def build_attn1(nc, p, x_in, x_out, wflat_ext, ins, tag="s5",
                staged=None):
    NW1 = 2 * 128 * 16 * 128
    per_rank = 4 * NW1
    if staged is None:
        wfull, wmark = stage_weights_ag(p, nc, wflat_ext, per_rank, tag + "w")
    else:
        wfull, wmark = staged
    wf = wfull.ap()

    def w_chunkset(widx, oc):
        r, i = oc // 2, oc % 2
        off = r * per_rank + widx * NW1 + i * (128 * 16 * 128)
        return wf[off:off + 128 * 16 * 128].rearrange(
            "(p kc m) -> p kc m", p=128, kc=16)

    ek_d = nc.dram_tensor(tag + "_ek", [16, 128, TLOC], F32, kind="Internal")
    ekv_d = nc.dram_tensor(tag + "_ekv", [16, 128, TLOC], F32, kind="Internal")
    r_d = nc.dram_tensor(tag + "_r", [16, 128, TLOC], BF16, kind="Internal")
    ccb_in = nc.dram_tensor(tag + "_ccbi", [128, 16], F32, kind="Internal")
    ccb_out = nc.dram_tensor(tag + "_ccbo", [NCORE * 128, 16], F32,
                             kind="Internal")
    cct_in = nc.dram_tensor(tag + "_ccti", [128, 32], F32, kind="Internal")
    cct_out = nc.dram_tensor(tag + "_ccto", [NCORE * 128, 32], F32,
                             kind="Internal")

    octx = ExitStack()
    osb = lambda n, s, d: octx.enter_context(nc.sbuf_tensor(tag + n, s, d))
    m1row = osb("_m1r", [1, TLOC], F32)
    r1row = osb("_r1r", [1, TLOC], F32)
    vecs = osb("_vecs", [128, 7, 16], F32)
    maskbc = osb("_mkb", [128, 2, 8], F32)
    xbuf2 = osb("_xa", [128, DC, TLOC], BF16)
    sxcol = osb("_sx", [128, 16], F32)
    ones1f = osb("_o1f", [1, 128], F32)

    vm_ = p.dma("sync", lambda s: s.dma_start(
        out=vecs[:], in_=ins["vecs"]), stream=tag + "vec")
    p.op("vector", lambda v: v.memset(ones1f[:], 1.0), inc="dve")

    with ExitStack() as ctx:
        xbuf = ctx.enter_context(
            nc.sbuf_tensor(tag + "_u", [128, DC, TLOC], BF16))
        ln = LNCtx(nc, ctx, tag + "_ln", TLOC)
        pubcol = ctx.enter_context(
            nc.sbuf_tensor(tag + "_pub", [128, 16], F32))
        xm = load_xbuf(p, x_in, xbuf)
        lnm = ln_compute(p, nc, ctx, ln, xbuf, DC, xm)
        p.op("vector", lambda v: v.tensor_copy(m1row[:], ln.mrow[:]),
             wait=[lnm], inc="dve")
        rows_m = p.op("vector", lambda v: v.tensor_copy(
            r1row[:], ln.rrow[:]), inc="dve")
        nm1 = ln_normalize(p, ln, xbuf, DC, rows_m)
        lnm2 = ln_compute(p, nc, ctx, ln, xbuf, DC, nm1)
        am = None
        for kc in range(DC):
            p.op("vector", lambda v, kc=kc: v.tensor_sub(
                ln.tmp32[:], xbuf[:, kc, :], ln.mbc[:]),
                wait=[lnm2] if kc == 0 else (), inc="dve")
            p.op("vector", lambda v: v.tensor_mul(
                ln.tmp32[:], ln.tmp32[:], ln.rbc[:]), inc="dve")
            am = p.op("vector", lambda v, kc=kc: v.tensor_scalar(
                xbuf2[:, kc, :], ln.tmp32[:],
                vecs[:, 0, kc:kc + 1], vecs[:, 1, kc:kc + 1],
                ALU.mult, ALU.add), wait=[vm_] if kc == 0 else (),
                inc="dve")
        pm = None
        for kc in range(DC):
            pm = p.op("vector", lambda v, kc=kc: v.tensor_copy(
                pubcol[:, kc:kc + 1], xbuf2[:, kc, TLOC - 1:TLOC]),
                wait=[am] if kc == 0 else (), inc="dve")
        pbm = p.dma("gpsimd", lambda g: g.dma_start(
            out=ccb_in.ap(), in_=pubcol[:]), wait=[pm], stream=tag + "pb")
        agb = p.op("gpsimd", lambda g: g.collective_compute(
            "AllGather", ALU.bypass, replica_groups=[list(range(NCORE))],
            ins=[ccb_in.ap().opt()], outs=[ccb_out.ap().opt()]),
            wait=[pbm], inc="cc")
        with ExitStack() as pctx:
            ps_b = pctx.enter_context(
                nc.psum_tensor(tag + "psmk", [128, 8], F32))
            mkrow = pctx.enter_context(
                nc.sbuf_tensor(tag + "_mkr", [1, 16], F32))
            bm1 = p.dma("sync", lambda s: s.dma_start(
                out=mkrow[0:1, 0:8], in_=ins["bmask"]), stream=tag + "bm")
            bm2 = p.dma("sync", lambda s: s.dma_start(
                out=mkrow[0:1, 8:16], in_=ins["smask"]), stream=tag + "bm2")
            k1 = p.op("tensor", lambda t: t.matmul(
                ps_b[:], ones1f[:], mkrow[0:1, 0:8],
                start=True, stop=True), wait=[bm1, bm2, am], inc="pe")
            p.op("vector", lambda v: v.tensor_copy(
                maskbc[:, 0, :], ps_b[:]), wait=[k1], inc="dve")
            k2 = p.op("tensor", lambda t: t.matmul(
                ps_b[:], ones1f[:], mkrow[0:1, 8:16],
                start=True, stop=True), wait=[p.mark("dve")], inc="pe")
            mkm = p.op("vector", lambda v: v.tensor_copy(
                maskbc[:, 1, :], ps_b[:]), wait=[k2], inc="dve")
        with ExitStack() as bctx:
            bndl = bctx.enter_context(
                nc.sbuf_tensor(tag + "_bnd", [128, 8, 16], F32))
            blm = p.dma("sync", lambda s: s.dma_start(
                out=bndl[:], in_=ccb_out.ap().rearrange(
                    "(r p) c -> p r c", p=128)), wait=[agb, mkm],
                stream=tag + "bnl")
            fm = p.op("vector", lambda v: v.memset(sxcol[:], 0.0), inc="dve")
            for j in range(NCORE):
                fm = p.op("vector", lambda v, j=j: v.scalar_tensor_tensor(
                    sxcol[:], bndl[:, j, :], maskbc[:, 0, j:j + 1],
                    sxcol[:], ALU.mult, ALU.add),
                    wait=[blm, mkm] if j == 0 else (), inc="dve")
        p.barrier()

    with ExitStack() as ctx:
        sb = lambda n, s, d: ctx.enter_context(nc.sbuf_tensor(tag + n, s, d))
        xkb = sb("_xk", [128, DC, 512], BF16)
        xvb = sb("_xv", [128, DC, 512], BF16)
        xrb = sb("_xrm", [128, DC, 512], BF16)
        tmp32 = sb("_t32b", [128, 512], F32)
        kf32 = sb("_kf", [128, 512], F32)
        vf32 = sb("_vf", [128, 512], F32)
        ekS = [sb(f"_ekS{i}", [128, 512], F32) for i in range(2)]
        ekvS = [sb(f"_ekvS{i}", [128, 512], F32) for i in range(2)]
        rS = [sb(f"_rS{i}", [128, 512], BF16) for i in range(2)]
        wkb = [sb(f"_wk{i}", [128, 16, 128], BF16) for i in range(2)]
        wvb = [sb(f"_wv{i}", [128, 16, 128], BF16) for i in range(2)]
        wrb = [sb(f"_wr{i}", [128, 16, 128], BF16) for i in range(2)]
        with ExitStack() as pctx:
            ps_k = pctx.enter_context(
                nc.psum_tensor(tag + "psk", [128, 512], F32))
            ps_v = pctx.enter_context(
                nc.psum_tensor(tag + "psv", [128, 512], F32))
            ps_r = pctx.enter_context(
                nc.psum_tensor(tag + "psr", [128, 512], F32))
            ekst = {}
            ekvst = {}
            rst = {}
            kev = {}
            vev = {}
            rev = {}
            mixm = None
            for ts in range(4):
                t0 = ts * 512
                for kc in range(DC):
                    for mi, dst in ((2, xkb), (3, xvb), (4, xrb)):
                        if ts == 0:
                            p.op("vector", lambda v, kc=kc: v.tensor_sub(
                                tmp32[:, 0:1], xbuf2[:, kc, 0:1],
                                sxcol[:, kc:kc + 1]),
                                wait=[p.mark("pe")] if kc == 0 and mi == 2
                                else (), inc="dve")
                            p.op("vector",
                                 lambda v, kc=kc, mi=mi, dst=dst:
                                 v.scalar_tensor_tensor(
                                     dst[:, kc, 0:1], tmp32[:, 0:1],
                                     vecs[:, mi, kc:kc + 1],
                                     sxcol[:, kc:kc + 1],
                                     ALU.mult, ALU.add), inc="dve")
                            p.op("vector", lambda v, kc=kc: v.tensor_sub(
                                tmp32[:, 1:512], xbuf2[:, kc, 1:512],
                                xbuf2[:, kc, 0:511]), inc="dve")
                            mixm = p.op(
                                "vector", lambda v, kc=kc, mi=mi, dst=dst:
                                v.scalar_tensor_tensor(
                                    dst[:, kc, 1:512], tmp32[:, 1:512],
                                    vecs[:, mi, kc:kc + 1],
                                    xbuf2[:, kc, 0:511],
                                    ALU.mult, ALU.add), inc="dve")
                        else:
                            p.op("vector", lambda v, kc=kc, t0=t0:
                                 v.tensor_sub(
                                     tmp32[:], xbuf2[:, kc, t0:t0 + 512],
                                     xbuf2[:, kc, t0 - 1:t0 + 511]),
                                 wait=[p.mark("pe")] if kc == 0 and mi == 2
                                 else (), inc="dve")
                            mixm = p.op(
                                "vector", lambda v, kc=kc, mi=mi, dst=dst,
                                t0=t0: v.scalar_tensor_tensor(
                                    dst[:, kc, :], tmp32[:],
                                    vecs[:, mi, kc:kc + 1],
                                    xbuf2[:, kc, t0 - 1:t0 + 511],
                                    ALU.mult, ALU.add), inc="dve")
                for oc in range(16):
                    ob = oc % 2
                    lk = p.dma("sync", lambda s, oc=oc, ob=ob: s.dma_start(
                        out=wkb[ob][:], in_=w_chunkset(0, oc)),
                        wait=[wmark] if ts == 0 and oc < 2
                        else [kev.get((ts, oc - 2)) or kev.get((ts - 1, 14 + oc))],
                        stream=tag + "wk", slot=ob)
                    lv = p.dma("sync", lambda s, oc=oc, ob=ob: s.dma_start(
                        out=wvb[ob][:], in_=w_chunkset(1, oc)),
                        stream=tag + "wv", slot=ob)
                    lr = p.dma("sync", lambda s, oc=oc, ob=ob: s.dma_start(
                        out=wrb[ob][:], in_=w_chunkset(2, oc)),
                        stream=tag + "wr", slot=ob)
                    pk_g = kev.get((ts, oc - 1)) or kev.get((ts - 1, 15))
                    pv_g = vev.get((ts, oc - 1)) or vev.get((ts - 1, 15))
                    pr_g = rev.get((ts, oc - 1)) or rev.get((ts - 1, 15))
                    for kc in range(16):
                        p.op("tensor", lambda t, kc=kc, ob=ob: t.matmul(
                            ps_k[:], wkb[ob][:, kc, :], xkb[:, kc, :],
                            start=(kc == 0), stop=(kc == 15)),
                            wait=[lk, mixm, pk_g] if kc == 0 else ())
                    for kc in range(16):
                        km = p.op("tensor", lambda t, kc=kc, ob=ob: t.matmul(
                            ps_v[:], wvb[ob][:, kc, :], xvb[:, kc, :],
                            start=(kc == 0), stop=(kc == 15)),
                            wait=[lv, pv_g] if kc == 0 else ())
                    for kc in range(16):
                        rm_ = p.op("tensor", lambda t, kc=kc, ob=ob: t.matmul(
                            ps_r[:], wrb[ob][:, kc, :], xrb[:, kc, :],
                            start=(kc == 0), stop=(kc == 15)),
                            wait=[lr, pr_g] if kc == 0 else (),
                            inc="pe" if kc == 15 else None)
                    ke = p.op("vector", lambda v: v.tensor_copy(
                        kf32[:], ps_k[:]), wait=[rm_], inc="dve")
                    kev[(ts, oc)] = ke
                    ve = p.op("vector", lambda v: v.tensor_copy(
                        vf32[:], ps_v[:]), inc="dve")
                    vev[(ts, oc)] = ve
                    eb = oc % 2
                    ekm = p.op("scalar", lambda s, eb=eb: s.activation(
                        out=ekS[eb][:], in_=kf32[:], func=AF.Exp),
                        wait=[ke, ekst.get((ts, oc - 2)) or
                              ekst.get((ts - 1, 14 + oc))], inc="act")
                    ekvm = p.op("vector", lambda v, eb=eb: v.tensor_mul(
                        ekvS[eb][:], ekS[eb][:], vf32[:]),
                        wait=[ekm, ekvst.get((ts, oc - 2)) or
                              ekvst.get((ts - 1, 14 + oc))], inc="dve")
                    rsm = p.op("scalar", lambda s, eb=eb: s.activation(
                        out=rS[eb][:], in_=ps_r[:], func=AF.Sigmoid),
                        wait=[rm_, rst.get((ts, oc - 2)) or
                              rst.get((ts - 1, 14 + oc))], inc="act")
                    rev[(ts, oc)] = rsm
                    ekst[(ts, oc)] = p.dma(
                        "gpsimd", lambda g, oc=oc, t0=t0, eb=eb: g.dma_start(
                            out=ek_d.ap()[oc, :, t0:t0 + 512],
                            in_=ekS[eb][:]), wait=[ekm],
                        stream=tag + "eks", slot=eb)
                    ekvst[(ts, oc)] = p.dma(
                        "gpsimd", lambda g, oc=oc, t0=t0, eb=eb: g.dma_start(
                            out=ekv_d.ap()[oc, :, t0:t0 + 512],
                            in_=ekvS[eb][:]), wait=[ekvm],
                        stream=tag + "ekvs", slot=eb)
                    rst[(ts, oc)] = p.dma(
                        "gpsimd", lambda g, oc=oc, t0=t0, eb=eb: g.dma_start(
                            out=r_d.ap()[oc, :, t0:t0 + 512],
                            in_=rS[eb][:]), wait=[rsm],
                        stream=tag + "rs", slot=eb)
        p.barrier()

    with ExitStack() as ctx:
        sb = lambda n, s, d: ctx.enter_context(nc.sbuf_tensor(tag + n, s, d))
        ekb = sb("_ekb", [128, TLOC], F32)
        ekvb = sb("_ekvb", [128, TLOC], F32)
        numo = sb("_no", [128, TLOC], F32)
        deno = sb("_do", [128, TLOC], F32)
        onest = sb("_ones", [128, TLOC], F32)
        ewt = sb("_ewt", [128, TLOC], F32)
        dcol = sb("_dcol", [128, 1], F32)
        ewcol = sb("_ewc", [128, 1], F32)
        tailb = sb("_tail", [128, 32], F32)
        p.op("vector", lambda v: v.memset(onest[:], 1.0), inc="dve")
        for ch in range(16):
            lek = p.dma("sync", lambda s, ch=ch: s.dma_start(
                out=ekb[:], in_=ek_d.ap()[ch]),
                wait=[p.mark("dve")] if ch > 0 else [],
                stream=tag + "ekl")
            lekv = p.dma("sync", lambda s, ch=ch: s.dma_start(
                out=ekvb[:], in_=ekv_d.ap()[ch]), stream=tag + "ekvl")
            p.op("scalar", lambda s, ch=ch: s.activation(
                out=dcol[:], in_=vecs[:, 5, ch:ch + 1], func=AF.Exp),
                wait=[vm_] if ch == 0 else (), inc="act")
            p.op("vector", lambda v: v.tensor_scalar_mul(
                dcol[:], dcol[:], -1.0), wait=[p.mark("act")], inc="dve")
            ewm = p.op("scalar", lambda s: s.activation(
                out=ewcol[:], in_=dcol[:], func=AF.Exp),
                wait=[p.mark("dve")], inc="act")
            etm = p.op("vector", lambda v: v.tensor_scalar(
                ewt[:], onest[:], ewcol[:], None, ALU.mult, ALU.bypass),
                wait=[ewm], inc="dve")
            sn = p.op("vector", lambda v: v.tensor_tensor_scan(
                numo[:], ewt[:], ekvb[:], 0.0, ALU.mult, ALU.add),
                wait=[lekv, etm], inc="dve")
            sd = p.op("vector", lambda v: v.tensor_tensor_scan(
                deno[:], ewt[:], ekb[:], 0.0, ALU.mult, ALU.add),
                wait=[lek], inc="dve")
            p.op("vector", lambda v, ch=ch: v.tensor_copy(
                tailb[:, 2 * ch:2 * ch + 1],
                numo[:, TLOC - 1:TLOC]), inc="dve")
            tm = p.op("vector", lambda v, ch=ch: v.tensor_copy(
                tailb[:, 2 * ch + 1:2 * ch + 2],
                deno[:, TLOC - 1:TLOC]), inc="dve")
        ptm = p.dma("gpsimd", lambda g: g.dma_start(
            out=cct_in.ap(), in_=tailb[:]), wait=[tm], stream=tag + "tp")
        agt = p.op("gpsimd", lambda g: g.collective_compute(
            "AllGather", ALU.bypass, replica_groups=[list(range(NCORE))],
            ins=[cct_in.ap().opt()], outs=[cct_out.ap().opt()]),
            wait=[ptm], inc="cc")
        p.barrier()

    rwkvb = osb("_rwkv", [128, DC, TLOC], BF16)
    with ExitStack() as ctx:
        sb = lambda n, s, d: ctx.enter_context(nc.sbuf_tensor(tag + n, s, d))
        ekb = sb("_ekb2", [128, TLOC], F32)
        ekvb = sb("_ekvb2", [128, TLOC], F32)
        numb = sb("_numb", [128, TLOC + 1], F32)
        denb = sb("_denb", [128, TLOC + 1], F32)
        wkvb = sb("_wkvb", [128, TLOC], F32)
        rl = sb("_rl", [128, TLOC], BF16)
        onest = sb("_ones2", [128, TLOC], F32)
        ewt = sb("_ewt2", [128, TLOC], F32)
        dcol = sb("_dcol2", [128, 1], F32)
        ewcol = sb("_ewc2", [128, 1], F32)
        ewCc = sb("_ewC", [128, 1], F32)
        eucol = sb("_euc", [128, 1], F32)
        t1c = sb("_t1c", [128, 1], F32)
        snc = sb("_snc", [128, 1], F32)
        sdc = sb("_sdc", [128, 1], F32)
        tlb = sb("_tlb", [128, 8, 32], F32)
        p.op("vector", lambda v: v.memset(onest[:], 1.0), inc="dve")
        tlm = p.dma("sync", lambda s: s.dma_start(
            out=tlb[:], in_=cct_out.ap().rearrange("(r p) c -> p r c",
                                                   p=128)),
            wait=[agt], stream=tag + "tll")
        for ch in range(16):
            lek = p.dma("sync", lambda s, ch=ch: s.dma_start(
                out=ekb[:], in_=ek_d.ap()[ch]),
                wait=[p.mark("dve")] if ch > 0 else [],
                stream=tag + "ekl2")
            lekv = p.dma("sync", lambda s, ch=ch: s.dma_start(
                out=ekvb[:], in_=ekv_d.ap()[ch]), stream=tag + "ekvl2")
            lrc = p.dma("sync", lambda s, ch=ch: s.dma_start(
                out=rl[:], in_=r_d.ap()[ch]), stream=tag + "rl")
            p.op("scalar", lambda s, ch=ch: s.activation(
                out=dcol[:], in_=vecs[:, 5, ch:ch + 1], func=AF.Exp),
                inc="act")
            p.op("vector", lambda v: v.tensor_scalar_mul(
                t1c[:], dcol[:], -float(TLOC)), wait=[p.mark("act")],
                inc="dve")
            p.op("scalar", lambda s: s.activation(
                out=ewCc[:], in_=t1c[:], func=AF.Exp),
                wait=[p.mark("dve")], inc="act")
            p.op("vector", lambda v: v.tensor_scalar_mul(
                dcol[:], dcol[:], -1.0), inc="dve")
            ewm = p.op("scalar", lambda s: s.activation(
                out=ewcol[:], in_=dcol[:], func=AF.Exp),
                wait=[p.mark("dve")], inc="act")
            eum = p.op("scalar", lambda s, ch=ch: s.activation(
                out=eucol[:], in_=vecs[:, 6, ch:ch + 1], func=AF.Exp),
                inc="act")
            etm = p.op("vector", lambda v: v.tensor_scalar(
                ewt[:], onest[:], ewcol[:], None, ALU.mult, ALU.bypass),
                wait=[ewm], inc="dve")
            p.op("vector", lambda v: v.memset(snc[:], 0.0), inc="dve")
            p.op("vector", lambda v: v.memset(sdc[:], 0.0), inc="dve")
            fmm = p.mark("dve")
            for j in range(NCORE):
                p.op("vector", lambda v: v.tensor_scalar_add(
                    t1c[:], ewCc[:], -1.0),
                    wait=[tlm, fmm] if j == 0 else (), inc="dve")
                p.op("vector", lambda v, j=j: v.tensor_scalar(
                    t1c[:], t1c[:], maskbc[:, 1, j:j + 1], 1.0,
                    ALU.mult, ALU.add), inc="dve")
                p.op("vector", lambda v: v.tensor_mul(
                    snc[:], snc[:], t1c[:]), inc="dve")
                p.op("vector", lambda v, j=j, ch=ch:
                     v.scalar_tensor_tensor(
                         snc[:], tlb[:, j, 2 * ch:2 * ch + 1],
                         maskbc[:, 1, j:j + 1], snc[:],
                         ALU.mult, ALU.add), inc="dve")
                p.op("vector", lambda v: v.tensor_mul(
                    sdc[:], sdc[:], t1c[:]), inc="dve")
                p.op("vector", lambda v, j=j, ch=ch:
                     v.scalar_tensor_tensor(
                         sdc[:], tlb[:, j, 2 * ch + 1:2 * ch + 2],
                         maskbc[:, 1, j:j + 1], sdc[:],
                         ALU.mult, ALU.add), inc="dve")
            p.op("vector", lambda v: v.tensor_copy(
                numb[:, 0:1], snc[:]), inc="dve")
            p.op("vector", lambda v: v.tensor_copy(
                denb[:, 0:1], sdc[:]), inc="dve")
            p.op("vector", lambda v: v.tensor_tensor_scan(
                numb[:, 1:TLOC + 1], ewt[:], ekvb[:], snc[:],
                ALU.mult, ALU.add), wait=[lekv, etm], inc="dve")
            p.op("vector", lambda v: v.tensor_tensor_scan(
                denb[:, 1:TLOC + 1], ewt[:], ekb[:], sdc[:],
                ALU.mult, ALU.add), wait=[lek], inc="dve")
            p.op("vector", lambda v: v.tensor_scalar(
                wkvb[:], ekvb[:], eucol[:], None, ALU.mult, ALU.bypass),
                wait=[eum], inc="dve")
            p.op("vector", lambda v: v.tensor_add(
                wkvb[:], wkvb[:], numb[:, 0:TLOC]), inc="dve")
            p.op("vector", lambda v: v.tensor_scalar(
                ewt[:], ekb[:], eucol[:], None, ALU.mult, ALU.bypass),
                inc="dve")
            dnm = p.op("vector", lambda v: v.tensor_add(
                ewt[:], ewt[:], denb[:, 0:TLOC]), inc="dve")
            p.op("scalar", lambda s: s.activation(
                out=ewt[:], in_=ewt[:], func=AF.Ln), wait=[dnm], inc="act")
            rcm = p.op("scalar", lambda s: s.activation(
                out=ewt[:], in_=ewt[:], func=AF.Exp, scale=-1.0),
                inc="act")
            p.op("vector", lambda v: v.tensor_mul(
                wkvb[:], wkvb[:], ewt[:]), wait=[rcm], inc="dve")
            p.op("vector", lambda v, ch=ch: v.tensor_mul(
                rwkvb[:, ch, :], wkvb[:], rl[:]), wait=[lrc], inc="dve")
        p.barrier()

    with ExitStack() as ctx:
        sb = lambda n, s, d: ctx.enter_context(nc.sbuf_tensor(tag + n, s, d))
        wob = [sb(f"_wo{i}", [128, 16, 128], BF16) for i in range(2)]
        m1bc = sb("_m1bc", [128, 512], F32)
        r1bc = sb("_r1bc", [128, 512], F32)
        xres = sb("_xr2", [128, 512], F32)
        u32 = sb("_u32", [128, 512], F32)
        o32 = sb("_o32", [128, 512], F32)
        with ExitStack() as pctx:
            ps_o = [pctx.enter_context(
                nc.psum_tensor(tag + f"pso{i}", [128, 512], F32))
                for i in range(2)]
            ps_c = pctx.enter_context(
                nc.psum_tensor(tag + "psbc", [128, 512], F32))
            st_m = None
            oev = {}
            wo_m = {}
            for ts in range(4):
                t0 = ts * 512
                b1 = p.op("tensor", lambda t, t0=t0: t.matmul(
                    ps_c[:], ones1f[:], m1row[:, t0:t0 + 512],
                    start=True, stop=True),
                    wait=[rows_m, p.mark("dve")], inc="pe")
                p.op("vector", lambda v: v.tensor_copy(m1bc[:], ps_c[:]),
                     wait=[b1], inc="dve")
                b2 = p.op("tensor", lambda t, t0=t0: t.matmul(
                    ps_c[:], ones1f[:], r1row[:, t0:t0 + 512],
                    start=True, stop=True),
                    wait=[p.mark("dve")], inc="pe")
                bcm = p.op("vector", lambda v: v.tensor_copy(
                    r1bc[:], ps_c[:]), wait=[b2], inc="dve")
                for dc in range(16):
                    ob = dc % 2
                    prevkey = (ts, dc - 2) if dc >= 2 else (ts - 1, 14 + dc)
                    wo_m[(ts, dc)] = p.dma(
                        "sync", lambda s, dc=dc, ob=ob: s.dma_start(
                            out=wob[ob][:], in_=w_chunkset(3, dc)),
                        wait=[oev.get(prevkey) or wmark],
                        stream=tag + "wo", slot=ob)
                    for kc in range(16):
                        om_ = p.op("tensor",
                                   lambda t, kc=kc, ob=ob, t0=t0: t.matmul(
                                       ps_o[ob][:], wob[ob][:, kc, :],
                                       rwkvb[:, kc, t0:t0 + 512],
                                       start=(kc == 0), stop=(kc == 15)),
                                   wait=[wo_m[(ts, dc)],
                                         oev.get(prevkey)]
                                   if kc == 0 else (),
                                   inc="pe" if kc == 15 else None)
                    xr_m = p.dma("sync", lambda s, dc=dc, t0=t0: s.dma_start(
                        out=xres[:],
                        in_=x_in.rearrange("(kc p) t -> p kc t", p=128)[
                            :, dc, t0:t0 + 512]),
                        wait=[st_m], stream="f1xres")
                    p.op("vector", lambda v: v.tensor_sub(
                        u32[:], xres[:], m1bc[:]),
                        wait=[xr_m, bcm], inc="dve")
                    p.op("vector", lambda v: v.tensor_mul(
                        u32[:], u32[:], r1bc[:]), inc="dve")
                    p.op("vector", lambda v: v.tensor_add(
                        u32[:], u32[:], xres[:]), inc="dve")
                    oev[(ts, dc)] = p.op("vector", lambda v, ob=ob:
                                         v.tensor_add(
                                             o32[:], u32[:], ps_o[ob][:]),
                                         wait=[om_, st_m], inc="dve")
                    st_m = p.dma(
                        "gpsimd", lambda g, dc=dc, t0=t0: g.dma_start(
                            out=x_out.rearrange("(kc p) t -> p kc t",
                                                p=128)[:, dc, t0:t0 + 512],
                            in_=o32[:]),
                        wait=[oev[(ts, dc)]], stream="f1xst")
        p.barrier()
    octx.close()


def prep_attn1_inputs(Wk, Wv, Wr, Wo, lnw, lnb, mixk, mixv, mixr, td, ctw):
    ts_ = [tile_w_out_in(np.asarray(W), 16, 16) for W in (Wk, Wv, Wr, Wo)]
    wslices = []
    for r in range(NCORE):
        s = np.concatenate([t[2 * r:2 * r + 2].ravel() for t in ts_])
        wslices.append(np.ascontiguousarray(s))
    vecs = np.stack([np.asarray(v).reshape(16, 128).T.astype(np.float32)
                     for v in (lnw, lnb, mixk, mixv, mixr, td, ctw)],
                    axis=1)
    vecs = np.ascontiguousarray(vecs)
    bmasks, smasks = [], []
    for c in range(NCORE):
        b, qt = c // 4, c % 4
        bm = np.zeros((1, 8), np.float32)
        if qt > 0:
            bm[0, c - 1] = 1.0
        sm = np.zeros((1, 8), np.float32)
        for r in range(NCORE):
            if r // 4 == b and r % 4 < qt:
                sm[0, r] = 1.0
        bmasks.append(bm)
        smasks.append(sm)
    return wslices, vecs, bmasks, smasks


def make_attn1_launch():
    nc = bass.Bass("TRN2", target_bir_lowering=False, debug=False)
    ctx = ExitStack()
    x_in = nc.dram_tensor("x", [D, TLOC], F32, kind="ExternalInput").ap()
    per_rank = 4 * 2 * 128 * 16 * 128
    wflat = nc.dram_tensor("wflat", [per_rank], F32, kind="ExternalInput").ap()
    vecs = nc.dram_tensor("vecs", [128, 7, 16], F32, kind="ExternalInput").ap()
    bmask = nc.dram_tensor("bmask", [1, 8], F32, kind="ExternalInput").ap()
    smask = nc.dram_tensor("smask", [1, 8], F32, kind="ExternalInput").ap()
    x_out = nc.dram_tensor("xout", [D, TLOC], F32, kind="ExternalOutput").ap()
    ins = {"vecs": vecs, "bmask": bmask, "smask": smask}
    with ctx:
        p = Prog(nc, ctx)
        build_attn1(nc, p, x_in, x_out, wflat, ins)
        p.finalize([p.mark("ds_f1xst_0")])
    return nc


def run_attn1(x_slices, Wk, Wv, Wr, Wo, lnw, lnb, mixk, mixv, mixr, td, ctw):
    nc = make_attn1_launch()
    wsl, vecs, bms, sms = prep_attn1_inputs(Wk, Wv, Wr, Wo, lnw, lnb,
                                            mixk, mixv, mixr, td, ctw)
    in_maps = [{"x": x_slices[c], "wflat": wsl[c], "vecs": vecs,
                "bmask": bms[c], "smask": sms[c]} for c in range(NCORE)]
    res = run_bass_kernel_spmd(nc, in_maps, core_ids=list(range(NCORE)))
    return [res.results[c]["xout"] for c in range(NCORE)]


def make_fused_launch():
    nc = bass.Bass("TRN2", target_bir_lowering=False, debug=False)
    ctx = ExitStack()
    x_in = nc.dram_tensor("x", [D, TLOC], F32, kind="ExternalInput").ap()
    pr_f0 = 2 * (8 * 16 * 128 * 128) + 2 * 64 * 128 * 128
    pr_a = 4 * 2 * 128 * 16 * 128
    w_f0 = nc.dram_tensor("w_f0", [pr_f0], F32, kind="ExternalInput").ap()
    h128c = nc.dram_tensor("h128", [128, 128], BF16, kind="ExternalInput").ap()
    w_a0 = nc.dram_tensor("w_a0", [pr_a], F32, kind="ExternalInput").ap()
    ident = nc.dram_tensor("ident", [128, 128], BF16, kind="ExternalInput").ap()
    masks = nc.dram_tensor("masks", [4, 128, 512], F32,
                           kind="ExternalInput").ap()
    zmask = nc.dram_tensor("zmask", [1, 32], F32, kind="ExternalInput").ap()
    smask = nc.dram_tensor("smask", [1, 8], F32, kind="ExternalInput").ap()
    betas = nc.dram_tensor("betas", [1, 2048], F32, kind="ExternalInput").ap()
    gu_pk = nc.dram_tensor("gu_pk", [N_GU_OT, 128, 256], I32,
                           kind="ExternalInput").ap()
    gu_sc = nc.dram_tensor("gu_sc", [16, 1408], F16, kind="ExternalInput").ap()
    dn_pk = nc.dram_tensor("dn_pk", [N_DN_OT, 128, 704], I32,
                           kind="ExternalInput").ap()
    dn_sc = nc.dram_tensor("dn_sc", [44, 256], F16, kind="ExternalInput").ap()
    w_a1 = nc.dram_tensor("w_a1", [pr_a], F32, kind="ExternalInput").ap()
    vecs = nc.dram_tensor("vecs", [128, 7, 16], F32, kind="ExternalInput").ap()
    bmask = nc.dram_tensor("bmask", [1, 8], F32, kind="ExternalInput").ap()
    x_out = nc.dram_tensor("xout", [D, TLOC], F32, kind="ExternalOutput").ap()
    xm_ = [nc.dram_tensor(f"xmid{i}", [D, TLOC], F32, kind="Internal").ap()
           for i in range(5)]
    a0c = {"ident": ident, "masks": masks, "zmask": zmask, "smask": smask,
           "betas": betas}
    f1c = {"gu_pk": gu_pk, "gu_sc": gu_sc, "dn_pk": dn_pk, "dn_sc": dn_sc,
           "ident": ident}
    a1c = {"vecs": vecs, "bmask": bmask, "smask": smask}
    with ctx:
        p = Prog(nc, ctx)
        n_w1 = 8 * 16 * 128 * 128
        n_w2 = 2 * 64 * 128 * 128
        st_w13 = stage_weights_ag(p, nc, w_f0, 2 * n_w1, "s1wa")
        st_w2 = stage_weights_ag(p, nc, w_f0, n_w2, "s1wb", base=2 * n_w1)
        st_f0 = (st_w13[0], st_w13[1], st_w2[0], st_w2[1])
        build_ffn0(nc, p, x_in, xm_[0], w_f0, h128c, tag="s1", staged=st_f0)
        st_a0 = stage_weights_ag(p, nc, w_a0, 4 * 2 * 128 * 16 * 128, "s2w")
        st_a1 = stage_weights_ag(p, nc, w_a1, 4 * 2 * 128 * 16 * 128, "s5w")
        st_f1 = stage_ffn1_weights(nc, p, f1c, "s4d")
        build_attn0(nc, p, xm_[0], xm_[1], w_a0, a0c, tag="s2", staged=st_a0)
        build_ffn0(nc, p, xm_[1], xm_[2], w_f0, h128c, tag="s3",
                   staged=st_f0)
        build_ffn1(nc, p, xm_[2], xm_[3], f1c, tag="s4", staged=st_f1)
        build_attn1(nc, p, xm_[3], xm_[4], w_a1, a1c, tag="s5",
                    staged=st_a1)
        build_ffn1(nc, p, xm_[4], x_out, f1c, tag="s6", staged=st_f1)
        p.finalize(list(p.pend["gpsimd"]))
    return nc


_FUSED = True


def kernel(**inputs):
    inp = {k: np.asarray(v) for k, v in inputs.items()}
    x = inp["L0_attn_x"].astype(np.float32, copy=False)
    xs = core_slices(x)

    ffn0_w = prep_ffn0_weights(inp["L0_ffn_W1"], inp["L0_ffn_W2"],
                               inp["L0_ffn_W3"])
    h128b = hadamard128().astype(ml_dtypes.bfloat16)
    attn0_w = prep_attn0_weights(inp["L0_attn_Wq"], inp["L0_attn_Wk"],
                                 inp["L0_attn_Wv"], inp["L0_attn_Wo"])
    idb = np.eye(128, dtype=np.float32).astype(ml_dtypes.bfloat16)
    msk = causal_masks()
    zs, ss = core_masks()
    brow = np.ascontiguousarray(
        inp["L0_attn_betas"].reshape(1, 2048).astype(np.float32))
    ffn1_in = prep_ffn1_inputs(inp["L1_ffn_gate_up_packed"],
                               inp["L1_ffn_gate_up_scales"],
                               inp["L1_ffn_down_packed"],
                               inp["L1_ffn_down_scales"])
    a1_w, a1_vecs, a1_bm, a1_sm = prep_attn1_inputs(
        inp["L1_attn_Wk"], inp["L1_attn_Wv"], inp["L1_attn_Wr"],
        inp["L1_attn_Wo"], inp["L1_attn_ln_w"], inp["L1_attn_ln_b"],
        inp["L1_attn_mix_k"], inp["L1_attn_mix_v"], inp["L1_attn_mix_r"],
        inp["L1_attn_time_decay"], inp["L1_attn_current_token_weight"])

    if _FUSED:
        nc = make_fused_launch()
        in_maps = [{
            "x": xs[c], "w_f0": ffn0_w[c], "h128": h128b,
            "w_a0": attn0_w[c], "ident": idb, "masks": msk,
            "zmask": zs[c], "smask": ss[c], "betas": brow,
            "gu_pk": ffn1_in[c][0], "gu_sc": ffn1_in[c][1],
            "dn_pk": ffn1_in[c][2], "dn_sc": ffn1_in[c][3],
            "w_a1": a1_w[c], "vecs": a1_vecs, "bmask": a1_bm[c],
        } for c in range(NCORE)]
        res = run_bass_kernel_spmd(nc, in_maps, core_ids=list(range(NCORE)))
        cur = [np.asarray(res.results[c]["xout"]) for c in range(NCORE)]
        out = np.empty((B, S, D), dtype=np.float32)
        for c in range(NCORE):
            b, q = c // 4, c % 4
            out[b, q * TLOC:(q + 1) * TLOC, :] = cur[c].T
        return out

    nc_ffn0 = make_ffn0_launch()
    nc_attn0 = make_attn0_launch()
    nc_ffn1 = make_ffn1_launch()
    nc_attn1 = make_attn1_launch()

    def run(nc, maps):
        res = run_bass_kernel_spmd(nc, maps, core_ids=list(range(NCORE)))
        return [np.asarray(res.results[c]["xout"]) for c in range(NCORE)]

    def ffn0_maps(cur):
        return [{"x": cur[c], "wflat": ffn0_w[c], "h128": h128b}
                for c in range(NCORE)]

    def attn0_maps(cur):
        return [{"x": cur[c], "wflat": attn0_w[c], "ident": idb,
                 "masks": msk, "zmask": zs[c], "smask": ss[c],
                 "betas": brow} for c in range(NCORE)]

    def ffn1_maps(cur):
        return [{"x": cur[c], "gu_pk": ffn1_in[c][0], "gu_sc": ffn1_in[c][1],
                 "dn_pk": ffn1_in[c][2], "dn_sc": ffn1_in[c][3],
                 "ident": idb} for c in range(NCORE)]

    def attn1_maps(cur):
        return [{"x": cur[c], "wflat": a1_w[c], "vecs": a1_vecs,
                 "bmask": a1_bm[c], "smask": a1_sm[c]}
                for c in range(NCORE)]

    cur = xs
    cur = run(nc_ffn0, ffn0_maps(cur))
    cur = run(nc_attn0, attn0_maps(cur))
    cur = run(nc_ffn0, ffn0_maps(cur))
    cur = run(nc_ffn1, ffn1_maps(cur))
    cur = run(nc_attn1, attn1_maps(cur))
    cur = run(nc_ffn1, ffn1_maps(cur))

    out = np.empty((B, S, D), dtype=np.float32)
    for c in range(NCORE):
        b, q = c // 4, c % 4
        out[b, q * TLOC:(q + 1) * TLOC, :] = cur[c].T
    return out


# version 08 (final)
# speedup vs baseline: 1.0697x; 1.0463x over previous
from contextlib import ExitStack

import numpy as np
import ml_dtypes

import concourse.bass as bass
import concourse.mybir as mybir
from concourse.bass_utils import run_bass_kernel_spmd

F32 = mybir.dt.float32
BF16 = mybir.dt.bfloat16
I32 = mybir.dt.int32
F16 = mybir.dt.float16
AF = mybir.ActivationFunctionType
ALU = mybir.AluOpType
AXL = mybir.AxisListType

NCORE = 8
B, S, D = 2, 8192, 2048
TLOC = 2048
DC = D // 128
HFF = 8192
HC = HFF // 128
NHEAD, DKH, SEG = 16, 128, 512
NSEG_LOC = TLOC // SEG
I1 = 5632
EPS_LN = 1e-5
EPS_RMS = 1e-6
INV_SCALE_ATT = float(1.0 / np.float32(np.sqrt(np.float32(128.0))))
FWHT_SCALE = float(1.0 / np.sqrt(8192.0))

ENGS = ("sync", "scalar", "vector", "tensor", "gpsimd")
_S2_STOP = None
_S2_NHEADS = 16
_S2_NPROJ = 12


class Prog:
    def __init__(self, nc, ctx):
        self.nc = nc
        self.ctx = ctx
        self.ops = {e: [] for e in ENGS}
        self.cnt = {}
        self.sems = {}
        self.pend = {e: [] for e in ENGS}
        self.dirty = {e: False for e in ENGS}

    def sem(self, name):
        if name not in self.sems:
            self.sems[name] = self.ctx.enter_context(self.nc.semaphore(name))
            self.cnt[name] = 0
        return name

    AUTOCHAIN = ("vector", "scalar")

    def op(self, eng, fn, wait=(), inc=None, is_dma=False):
        waits = [w for w in (list(self.pend[eng]) + list(wait))
                 if w is not None and w[1] > 0]
        self.pend[eng] = []
        chain = eng in self.AUTOCHAIN and not is_dma
        if chain:
            prev = getattr(self, "chain_mark", {}).get(eng)
            if prev is not None:
                waits.append(prev)
            if inc is None:
                inc = eng + "_ch"
        incs = []
        mark = None
        if inc is not None:
            self.sem(inc)
            k = 16 if is_dma else 1
            self.cnt[inc] += k
            mark = (inc, self.cnt[inc])
            incs.append((inc, k))
        if chain:
            if not hasattr(self, "chain_mark"):
                self.chain_mark = {}
            self.chain_mark[eng] = mark
        for (s, _) in waits:
            self.sem(s)
        self.ops[eng].append([fn, waits, incs, is_dma])
        self.dirty[eng] = True
        return mark

    def dma(self, q, fn, wait=(), stream=None, slot=0):
        sem = f"ds_{stream}_{slot}" if stream else "d_" + q
        return self.op(q, fn, wait=wait, inc=sem, is_dma=True)

    def val(self, s):
        return self.cnt.get(s, 0)

    def mark(self, s):
        return (s, self.cnt.get(s, 0))

    def milestone(self, eng):
        assert self.ops[eng], f"milestone on empty engine {eng}"
        last = self.ops[eng][-1]
        if last[2]:
            s, k = last[2][-1]
            return (s, self.cnt[s])
        s = eng + "_ms"
        self.sem(s)
        k = 16 if last[3] else 1
        self.cnt[s] += k
        last[2].append((s, k))
        return (s, self.cnt[s])

    def barrier(self, exclude=()):
        marks = []
        for e in ENGS:
            if not self.dirty[e] or not self.ops[e]:
                continue
            last = self.ops[e][-1]
            if last[2]:
                sname, k = last[2][-1]
                marks.append((sname, self.cnt[sname]))
            else:
                sname = "bm_" + e
                self.sem(sname)
                k = 16 if last[3] else 1
                self.cnt[sname] += k
                last[2].append((sname, k))
                marks.append((sname, self.cnt[sname]))
            self.dirty[e] = False
        for s, v in self.cnt.items():
            if (s.startswith("d_") or s.startswith("ds_")) and v > 0:
                marks.append((s, v))
        marks = [(s, v) for (s, v) in marks if s not in exclude]
        marks = list({s: (s, v) for (s, v) in marks}.values())
        for e in ENGS:
            self.pend[e] = self.pend[e] + marks
        return marks

    def finalize(self, final_waits):
        for (s, v) in final_waits:
            if v > 0:
                self.sem(s)
                self.ops["gpsimd"].append([None, [(s, v)], [], False])
        blk = self.ctx.enter_context(self.nc.Block())
        prog = self

        def runner(ename):
            def run(eng):
                for fn, waits, incs, _ in prog.ops[ename]:
                    for (s, v) in waits:
                        eng.wait_ge(prog.sems[s], v)
                    if fn is None:
                        continue
                    inst = fn(eng)
                    for (s, k) in incs:
                        inst.then_inc(prog.sems[s], k)
            return run

        blk.sync(runner("sync"))
        blk.scalar(runner("scalar"))
        blk.vector(runner("vector"))
        blk.tensor(runner("tensor"))
        blk.gpsimd(runner("gpsimd"))


def tile_w_out_in(W, n_out_chunks, n_k_chunks):
    O, K = W.shape
    t = W.reshape(n_out_chunks, 128, n_k_chunks, 128).transpose(0, 2, 3, 1)
    return np.ascontiguousarray(t.transpose(0, 2, 1, 3))


def tile_w_in_out(W, n_out_chunks, n_k_chunks):
    t = W.reshape(n_k_chunks, 128, n_out_chunks, 128).transpose(2, 0, 1, 3)
    return np.ascontiguousarray(t.transpose(0, 2, 1, 3))


def hadamard128():
    H = np.array([[1.0]], dtype=np.float32)
    for _ in range(7):
        H = np.block([[H, H], [H, -H]])
    return np.ascontiguousarray(H.astype(np.float32))


def causal_masks():
    m = np.zeros((4, 128, 512), dtype=np.float32)
    t = np.arange(512)[None, :]
    for j in range(4):
        i = np.arange(128)[:, None] + 128 * j
        m[j] = np.where(t <= i, 0.0, np.float32(-1e38)).astype(np.float32)
    return m


def core_slices(x):
    out = []
    for c in range(NCORE):
        b, q = c // 4, c % 4
        out.append(np.ascontiguousarray(x[b, q * TLOC:(q + 1) * TLOC, :].T))
    return out


def stage_weights_ag(p, nc, ext_flat, n_per_rank, tag):
    wbf = nc.dram_tensor(f"{tag}_bf", [n_per_rank], BF16, kind="Internal")
    wfull = nc.dram_tensor(f"{tag}_full", [NCORE * n_per_rank], BF16,
                           kind="Internal")
    CH = 4 * 1024 * 1024
    last = None
    for off in range(0, n_per_rank, CH):
        n = min(CH, n_per_rank - off)
        last = p.dma("gpsimd", lambda g, off=off, n=n: g.dma_start(
            out=wbf.ap()[off:off + n], in_=ext_flat[off:off + n]))
    mark = p.op("gpsimd", lambda g: g.collective_compute(
        "AllGather", ALU.bypass, replica_groups=[list(range(NCORE))],
        ins=[wbf.ap().opt()], outs=[wfull.ap().opt()]),
        wait=[last], inc="cc")
    return wfull, mark


class LNCtx:

    def __init__(self, nc, ctx, tag, T):
        sb = lambda n, s, d: ctx.enter_context(nc.sbuf_tensor(tag + n, s, d))
        self.T = T
        self.sqbf = sb("_sqbf", [128, T], BF16)
        self.tmp32 = sb("_tmp32", [128, T], F32)
        self.mbc = sb("_mbc", [128, T], F32)
        self.rbc = sb("_rbc", [128, T], F32)
        self.mrow = sb("_mrow", [1, T], F32)
        self.vrow = sb("_vrow", [1, T], F32)
        self.m2row = sb("_m2row", [1, T], F32)
        self.rrow = sb("_rrow", [1, T], F32)
        self.onesb = sb("_onesb", [128, 1], BF16)
        self.ones1f = sb("_ones1f", [1, 128], F32)
        self.init_done = False


_LN_UID = [0]


def ln_compute(p, nc, ctx, ln, xbuf, n_chunks, xload_mark, eps=EPS_LN):
    T = ln.T
    nsp = T // 512
    _LN_UID[0] += 1
    uid = _LN_UID[0]
    if not ln.init_done:
        p.op("vector", lambda v: v.memset(ln.onesb[:], 1.0), inc="dve")
        p.op("vector", lambda v: v.memset(ln.ones1f[:], 1.0), inc="dve")
        ln.init_done = True
    ones_mark = p.mark("dve")
    with ExitStack() as pctx:
        ps_sx = pctx.enter_context(
            nc.psum_tensor(f"lnps_sx{uid}", [1, T], F32))
        ps_sxx = pctx.enter_context(
            nc.psum_tensor(f"lnps_sxx{uid}", [1, T], F32))
        sq_marks = {}
        pe_kc = {}
        for kc in range(n_chunks):
            w = [xload_mark, ones_mark] if kc == 0 else [pe_kc.get(kc - 1)]
            sq_marks[kc] = p.op(
                "scalar", lambda s, kc=kc: s.activation(
                    out=ln.sqbf[:], in_=xbuf[:, kc, :], func=AF.Square),
                wait=[w_ for w_ in w if w_], inc="act")
            for tsp in range(nsp):
                tl = slice(tsp * 512, (tsp + 1) * 512)
                p.op("tensor", lambda t, kc=kc, tl=tl: t.matmul(
                    ps_sx[:, tl], ln.onesb[:], xbuf[:, kc, tl],
                    start=(kc == 0), stop=(kc == n_chunks - 1),
                    skip_group_check=True),
                    wait=[xload_mark, ones_mark] if kc == 0 and tsp == 0
                    else ())
                pe_kc[kc] = p.op("tensor", lambda t, kc=kc, tl=tl: t.matmul(
                    ps_sxx[:, tl], ln.onesb[:], ln.sqbf[:, tl],
                    start=(kc == 0), stop=(kc == n_chunks - 1),
                    skip_group_check=True),
                    wait=[sq_marks[kc]] if tsp == 0 else (),
                    inc="pe" if tsp == nsp - 1 else None)
        stats_pe = pe_kc[n_chunks - 1]
        n_el = 128 * n_chunks
        p.op("vector", lambda v: v.tensor_scalar_mul(
            ln.mrow[:], ps_sx[:], 1.0 / n_el), wait=[stats_pe], inc="dve")
        p.op("vector", lambda v: v.tensor_scalar_mul(
            ln.vrow[:], ps_sxx[:], 1.0 / n_el), inc="dve")
        p.op("vector", lambda v: v.tensor_mul(
            ln.m2row[:], ln.mrow[:], ln.mrow[:]), inc="dve")
        p.op("vector", lambda v: v.tensor_sub(
            ln.vrow[:], ln.vrow[:], ln.m2row[:]), inc="dve")
        rows_dve = p.op("vector", lambda v: v.tensor_scalar_add(
            ln.vrow[:], ln.vrow[:], float(eps)), inc="dve")
    p.op("scalar", lambda s: s.activation(
        out=ln.vrow[:], in_=ln.vrow[:], func=AF.Ln),
        wait=[rows_dve], inc="act")
    rstd_act = p.op("scalar", lambda s: s.activation(
        out=ln.rrow[:], in_=ln.vrow[:], func=AF.Exp, scale=-0.5), inc="act")
    with ExitStack() as pctx:
        ps_bc = pctx.enter_context(
            nc.psum_tensor(f"lnps_bc{uid}", [128, T], F32))
        ps_bc2 = pctx.enter_context(
            nc.psum_tensor(f"lnps_bc2{uid}", [128, T], F32))
        for tsp in range(nsp):
            tl = slice(tsp * 512, (tsp + 1) * 512)
            p.op("tensor", lambda t, tl=tl: t.matmul(
                ps_bc[:, tl], ln.ones1f[:], ln.mrow[:, tl],
                start=True, stop=True),
                wait=[rstd_act] if tsp == 0 else ())
            bc_pe = p.op("tensor", lambda t, tl=tl: t.matmul(
                ps_bc2[:, tl], ln.ones1f[:], ln.rrow[:, tl],
                start=True, stop=True),
                inc="pe" if tsp == nsp - 1 else None)
        p.op("vector", lambda v: v.tensor_copy(ln.mbc[:], ps_bc[:]),
             wait=[bc_pe], inc="dve")
        bc_dve = p.op("vector", lambda v: v.tensor_copy(ln.rbc[:], ps_bc2[:]),
                      inc="dve")
    return bc_dve


def ln_normalize(p, ln, xbuf, n_chunks, stats_mark, affine=None, extra=()):
    m = None
    for kc in range(n_chunks):
        p.op("vector", lambda v, kc=kc: v.tensor_sub(
            ln.tmp32[:], xbuf[:, kc, :], ln.mbc[:]),
            wait=[stats_mark] + list(extra) if kc == 0 else (), inc="dve")
        if affine is None:
            m = p.op("vector", lambda v, kc=kc: v.tensor_mul(
                xbuf[:, kc, :], ln.tmp32[:], ln.rbc[:]), inc="dve")
        else:
            wcol, bcol = affine
            p.op("vector", lambda v, kc=kc: v.tensor_mul(
                ln.tmp32[:], ln.tmp32[:], ln.rbc[:]), inc="dve")
            m = p.op("vector", lambda v, kc=kc: v.tensor_scalar(
                xbuf[:, kc, :], ln.tmp32[:],
                wcol[:, kc:kc + 1], bcol[:, kc:kc + 1],
                ALU.mult, ALU.add), inc="dve")
    return m


def load_xbuf(p, x_in, xbuf, t0=0, T=TLOC, n_split=4, wait=()):
    marks = []
    per = DC // n_split
    for i in range(n_split):
        marks.append(p.dma("gpsimd", lambda g, i=i: g.dma_start(
            out=xbuf[:, i * per:(i + 1) * per, :],
            in_=x_in.rearrange("(kc p) t -> p kc t", p=128)[
                :, i * per:(i + 1) * per, t0:t0 + T]),
            wait=wait if i == 0 else ()))
    return marks[-1]


def build_ffn0(nc, p, x_in, x_out, wflat_ext, h128_ext, tag="s1",
               staged=None):
    n_w1 = 8 * 16 * 128 * 128
    n_w2 = 2 * 64 * 128 * 128
    per_rank = 2 * n_w1 + n_w2
    if staged is None:
        wfull, wmark = stage_weights_ag(p, nc, wflat_ext, per_rank, tag + "w")
    else:
        wfull, wmark = staged
    wf = wfull.ap()

    def w1_chunk(hc, which):
        r, i = hc // 8, hc % 8
        off = r * per_rank + which * n_w1 + i * (16 * 128 * 128)
        return wf[off:off + 16 * 128 * 128].rearrange(
            "(p kc m) -> p kc m", p=128, kc=16)

    def w2_chunk(dc):
        r, i = dc // 2, dc % 2
        off = r * per_rank + 2 * n_w1 + i * (64 * 128 * 128)
        return wf[off:off + 64 * 128 * 128].rearrange(
            "(p cc m) -> p cc m", p=128, cc=64)

    uprime = nc.dram_tensor(tag + "_uprime", [4, 2, 128, HC, 256], BF16,
                            kind="Internal")

    with ExitStack() as ctx:
        sb = lambda n, s, d: ctx.enter_context(nc.sbuf_tensor(tag + n, s, d))
        xbuf = sb("_xbuf", [128, DC, TLOC], BF16)
        ln = LNCtx(nc, ctx, tag + "_ln", TLOC)
        h128 = sb("_h128", [128, 128], BF16)
        wb1 = [sb(f"_w1_{i}", [128, 16, 128], BF16) for i in range(2)]
        wb3 = [sb(f"_w3_{i}", [128, 16, 128], BF16) for i in range(2)]
        silu_t = [sb(f"_silu{i}", [128, 512], F32) for i in range(2)]
        ubf = [sb(f"_ubf{i}", [128, 512], BF16) for i in range(2)]
        upr = [sb(f"_upr_{i}", [128, 512], BF16) for i in range(2)]

        hm = p.dma("gpsimd", lambda g: g.dma_start(out=h128[:], in_=h128_ext),
                   stream="s1h")
        xm = load_xbuf(p, x_in, xbuf)
        lnm = ln_compute(p, nc, ctx, ln, xbuf, DC, xm)
        nm = ln_normalize(p, ln, xbuf, DC, lnm)

        with ExitStack() as pctx:
            ps_g = [pctx.enter_context(
                nc.psum_tensor(tag + f"psg{i}", [128, 512], F32))
                for i in (0, 1)]
            ps_l = [pctx.enter_context(
                nc.psum_tensor(tag + f"psl{i}", [128, 512], F32))
                for i in (0, 1)]
            ps_h = [pctx.enter_context(
                nc.psum_tensor(tag + f"psh{i}", [128, 512], F32))
                for i in (0, 1)]

            g_idx = 0
            silu_m = {}
            mul_m = {}
            evac_m = {}
            h128_m = {}
            stm_m = {}
            grp_last_pe = {}
            pending_h128 = None

            def emit_h128(g):
                h128_m[g] = p.op(
                    "tensor", lambda t, g=g: t.matmul(
                        ps_h[g % 2][:], h128[:], ubf[g % 2][:],
                        start=True, stop=True),
                    wait=[mul_m[g], evac_m.get(g - 2)]
                    + ([hm] if g == 0 else []),
                    inc="pe")
                ub = g % 2
                hc_, tsp_ = g // 4, g % 4
                evac_m[g] = p.op(
                    "vector", lambda v, ub=ub: v.tensor_scalar_mul(
                        upr[ub][:], ps_h[ub][:], FWHT_SCALE),
                    wait=[h128_m[g], stm_m.get(g - 2)], inc="dve")
                p.dma("gpsimd", lambda g_, hc_=hc_, tsp_=tsp_, ub=ub:
                      g_.dma_start(
                          out=uprime.ap()[tsp_, 0, :, hc_, :],
                          in_=upr[ub][:, 0:256]),
                      wait=[evac_m[g]],
                      stream="s1upst", slot=g % 2)
                stm_m[g] = p.dma(
                    "gpsimd", lambda g_, hc_=hc_, tsp_=tsp_, ub=ub:
                    g_.dma_start(
                        out=uprime.ap()[tsp_, 1, :, hc_, :],
                        in_=upr[ub][:, 256:512]),
                    wait=(), stream="s1upst", slot=g % 2)

            for hc in range(HC):
                wbs = hc % 2
                guard = [wmark] if hc < 2 else [grp_last_pe.get((hc - 2, 3))]
                m1 = p.dma("sync", lambda s, hc=hc, wbs=wbs: s.dma_start(
                    out=wb1[wbs][:], in_=w1_chunk(hc, 0)), wait=guard,
                    stream="s1w1", slot=hc % 2)
                m3 = p.dma("scalar", lambda s, hc=hc, wbs=wbs: s.dma_start(
                    out=wb3[wbs][:], in_=w1_chunk(hc, 1)), wait=guard,
                    stream="s1w3", slot=hc % 2)
                for tsp in range(4):
                    tl = slice(tsp * 512, (tsp + 1) * 512)
                    gb = g_idx % 2
                    w0 = [m1, nm] if g_idx == 0                         else [m1, silu_m.get(g_idx - 2)]
                    for kc in range(16):
                        p.op("tensor",
                             lambda t, kc=kc, wbs=wbs, tl=tl, gb=gb: t.matmul(
                                 ps_g[gb][:], wb1[wbs][:, kc, :],
                                 xbuf[:, kc, tl],
                                 start=(kc == 0), stop=(kc == 15)),
                             wait=w0 if kc == 0 else ())
                    w0 = [m3] if g_idx == 0 else [m3, mul_m.get(g_idx - 2)]
                    for kc in range(16):
                        lastmm = p.op(
                            "tensor",
                            lambda t, kc=kc, wbs=wbs, tl=tl, gb=gb: t.matmul(
                                ps_l[gb][:], wb3[wbs][:, kc, :],
                                xbuf[:, kc, tl],
                                start=(kc == 0), stop=(kc == 15)),
                            wait=w0 if kc == 0 else (),
                            inc="pe" if kc == 15 else None)
                    grp_last_pe[(hc, tsp)] = lastmm
                    silu_m[g_idx] = p.op(
                        "scalar", lambda s, gb=gb: s.activation(
                            out=silu_t[gb][:], in_=ps_g[gb][:],
                            func=AF.Silu),
                        wait=[lastmm, mul_m.get(g_idx - 2)], inc="act")
                    mul_m[g_idx] = p.op(
                        "vector", lambda v, gb=gb: v.tensor_mul(
                            ubf[gb][:], silu_t[gb][:], ps_l[gb][:]),
                        wait=[silu_m[g_idx], h128_m.get(g_idx - 2)],
                        inc="dve")
                    if pending_h128 is not None:
                        emit_h128(pending_h128)
                    pending_h128 = g_idx
                    g_idx += 1
            emit_h128(pending_h128)
            p.barrier()

    with ExitStack() as ctx:
        sb = lambda n, s, d: ctx.enter_context(nc.sbuf_tensor(tag + n, s, d))
        bfly = sb("_bfly", [128, 64, 256], F32)
        bflyb = sb("_bflyb", [128, 64, 256], BF16)
        w2b = [sb(f"_w2_{i}", [128, 64, 128], BF16) for i in range(2)]
        xres = sb("_xres", [128, 16, 256], F32)
        outb = sb("_outb", [128, 16, 256], F32)
        with ExitStack() as pctx:
            ps_o = [pctx.enter_context(
                nc.psum_tensor(tag + f"pso{i}", [128, 256], F32))
                for i in range(2)]

            cast_m = None
            store_m = None
            pb_pe = {}
            wload_m = {}
            evac_v = {}

            def bfly_ops(lm):
                ops = []
                for st_ in range(6):
                    h = 1 << st_
                    nb = 64 // (2 * h)

                    def apair(v, h=h, nb=nb):
                        a = bfly[:].rearrange(
                            "p (b two w) t -> p b two w t", b=nb, two=2, w=h)
                        return v.tensor_add(a[:, :, 0], a[:, :, 0], a[:, :, 1])

                    def bpair(v, h=h, nb=nb):
                        a = bfly[:].rearrange(
                            "p (b two w) t -> p b two w t", b=nb, two=2, w=h)
                        return v.scalar_tensor_tensor(
                            a[:, :, 1], a[:, :, 1], -2.0, a[:, :, 0],
                            ALU.mult, ALU.add)

                    ops.append((apair, [lm] if st_ == 0 else ()))
                    ops.append((bpair, ()))
                return ops

            def load_slab(ts2n, wait):
                tsp, half = ts2n // 2, ts2n % 2
                return p.dma(
                    "gpsimd", lambda g, tsp=tsp, half=half: g.dma_start(
                        out=bfly[:], in_=uprime.ap()[tsp, half]),
                    wait=wait, stream="s1upld")

            lm0 = load_slab(0, ())
            for fn, w in bfly_ops(lm0):
                p.op("vector", fn, wait=w, inc="dve")
            for ts2 in range(8):
                cast_m = p.op("vector", lambda v: v.tensor_copy(
                    bflyb[:], bfly[:]),
                    wait=[pb_pe.get((ts2 - 1, 15))], inc="dve")
                pend_bf = []
                if ts2 + 1 < 8:
                    lmn = load_slab(ts2 + 1, [cast_m])
                    pend_bf = bfly_ops(lmn)
                xm2 = p.dma("sync", lambda s, ts2=ts2: s.dma_start(
                    out=xres[:],
                    in_=x_in.rearrange("(kc p) t -> p kc t", p=128)[
                        :, :, ts2 * 256:(ts2 + 1) * 256]),
                    wait=[evac_v.get((ts2 - 1, 15))], stream="s1xres")
                for dc in range(16):
                    wbs = dc % 2
                    prevkey = (ts2, dc - 2) if dc >= 2 else (ts2 - 1, 14 + dc)
                    wload_m[(ts2, dc)] = p.dma(
                        "scalar" if dc % 2 else "sync",
                        lambda s, dc=dc, wbs=wbs: s.dma_start(
                            out=w2b[wbs][:], in_=w2_chunk(dc)),
                        wait=[pb_pe.get(prevkey) or wmark],
                        stream="s1w2", slot=dc % 2)
                    w0 = [wload_m[(ts2, dc)], cast_m, evac_v.get(prevkey)]
                    for cc in range(64):
                        mmark = p.op(
                            "tensor",
                            lambda t, cc=cc, wbs=wbs, dc=dc: t.matmul(
                                ps_o[dc % 2][:], w2b[wbs][:, cc, :],
                                bflyb[:, cc, :],
                                start=(cc == 0), stop=(cc == 63)),
                            wait=w0 if cc == 0 else (),
                            inc="pe" if cc == 63 else None)
                    pb_pe[(ts2, dc)] = mmark
                    evac_v[(ts2, dc)] = p.op(
                        "vector", lambda v, dc=dc: v.scalar_tensor_tensor(
                            outb[:, dc, :], ps_o[dc % 2][:], 0.5,
                            xres[:, dc, :], ALU.mult, ALU.add),
                        wait=[mmark, xm2, store_m], inc="dve")
                    if pend_bf:
                        fn, w = pend_bf.pop(0)
                        p.op("vector", fn, wait=w, inc="dve")
                assert not pend_bf
                store_m = p.dma("gpsimd", lambda g, ts2=ts2: g.dma_start(
                    out=x_out.rearrange("(kc p) t -> p kc t", p=128)[
                        :, :, ts2 * 256:(ts2 + 1) * 256],
                    in_=outb[:]), wait=[evac_v[(ts2, 15)]], stream="s1xst")
            p.barrier()
    return (wfull, wmark)


def prep_ffn0_weights(W1, W2, W3):
    t1 = tile_w_out_in(np.asarray(W1), HC, DC)
    t3 = tile_w_out_in(np.asarray(W3), HC, DC)
    t2 = tile_w_out_in(np.asarray(W2), DC, HC)
    slices = []
    for r in range(NCORE):
        s = np.concatenate([
            t1[8 * r:8 * r + 8].ravel(),
            t3[8 * r:8 * r + 8].ravel(),
            t2[2 * r:2 * r + 2].ravel()])
        slices.append(np.ascontiguousarray(s))
    return slices


def make_ffn0_launch():
    nc = bass.Bass("TRN2", target_bir_lowering=False, debug=False)
    ctx = ExitStack()
    x_in = nc.dram_tensor("x", [D, TLOC], F32, kind="ExternalInput").ap()
    per_rank = 2 * (8 * 16 * 128 * 128) + 2 * 64 * 128 * 128
    wflat = nc.dram_tensor("wflat", [per_rank], F32, kind="ExternalInput").ap()
    h128c = nc.dram_tensor("h128", [128, 128], BF16, kind="ExternalInput").ap()
    x_out = nc.dram_tensor("xout", [D, TLOC], F32, kind="ExternalOutput").ap()
    with ctx:
        p = Prog(nc, ctx)
        build_ffn0(nc, p, x_in, x_out, wflat, h128c)
        p.finalize([p.mark("d_gpsimd")])
    return nc


def run_ffn0(x_slices, W1, W2, W3):
    nc = make_ffn0_launch()
    wsl = prep_ffn0_weights(W1, W2, W3)
    h = hadamard128().astype(ml_dtypes.bfloat16)
    in_maps = [{"x": x_slices[c], "wflat": wsl[c], "h128": h}
               for c in range(NCORE)]
    res = run_bass_kernel_spmd(nc, in_maps, core_ids=list(range(NCORE)))
    return [res.results[c]["xout"] for c in range(NCORE)]


def build_attn0(nc, p, x_in, x_out, wflat_ext, consts, tag="s2",
                staged=None):
    NW1 = 2 * 128 * 16 * 128
    per_rank = 4 * NW1
    if staged is None:
        wfull, wmark = stage_weights_ag(p, nc, wflat_ext, per_rank, tag + "w")
    else:
        wfull, wmark = staged
    wf = wfull.ap()

    def w_chunk(widx, oc):
        r, i = oc // 2, oc % 2
        off = r * per_rank + widx * NW1 + i * (128 * 16 * 128)
        return wf[off:off + 128 * 16 * 128].rearrange(
            "(p kc m) -> p kc m", p=128, kc=16)

    sq_d = nc.dram_tensor(tag + "_sq", [16, 128, TLOC], BF16, kind="Internal")
    sk_d = nc.dram_tensor(tag + "_sk", [16, 128, TLOC], BF16, kind="Internal")
    sktm_d = nc.dram_tensor(tag + "_sktm", [16, 16, 128, 128], BF16,
                            kind="Internal")
    vvtm_d = nc.dram_tensor(tag + "_vvtm", [16, 16, 128, 128], BF16,
                            kind="Internal")
    adot_d = nc.dram_tensor(tag + "_adot", [16, 16, 128, 128], BF16,
                            kind="Internal")
    aq_d = nc.dram_tensor(tag + "_aq", [16, 4, 2, 128, 128], BF16,
                          kind="Internal")
    cc1_in = nc.dram_tensor(tag + "_cc1i", [128, 64], F32, kind="Internal")
    cc1_out = nc.dram_tensor(tag + "_cc1o", [NCORE * 128, 64], F32,
                             kind="Internal")
    cc2_in = nc.dram_tensor(tag + "_cc2i", [128, 4096], BF16, kind="Internal")
    cc2_out = nc.dram_tensor(tag + "_cc2o", [NCORE * 128, 4096], BF16,
                             kind="Internal")

    octx = ExitStack()
    osb = lambda n, s, d: octx.enter_context(nc.sbuf_tensor(tag + n, s, d))
    ident = osb("_id", [128, 128], BF16)
    colsum = osb("_colsum", [128, 64], F32)
    zall = osb("_zall", [128, 64], F32)
    zallbf = osb("_zallbf", [128, 64], BF16)

    idm = p.dma("gpsimd", lambda g: g.dma_start(
        out=ident[:], in_=consts["ident"]), stream="s2id")

    with ExitStack() as ctx:
        sb = lambda n, s, d: ctx.enter_context(nc.sbuf_tensor(tag + n, s, d))
        xbuf = sb("_xbuf", [128, DC, TLOC], BF16)
        masks = sb("_masks", [128, 4, 512], F32)
        mkm = p.dma("gpsimd", lambda g: g.dma_start(
            out=masks[:], in_=consts["masks"].rearrange("j p t -> p j t")),
            stream="s2mk")
        with ExitStack() as lctx:
            ln = LNCtx(nc, lctx, tag + "_ln", TLOC)
            xm = load_xbuf(p, x_in, xbuf)
            lnm = ln_compute(p, nc, lctx, ln, xbuf, DC, xm)
            nm = ln_normalize(p, ln, xbuf, DC, lnm)
        qh = [sb(f"_qh{i}", [128, TLOC], BF16) for i in range(2)]
        kh = [sb(f"_kh{i}", [128, TLOC], BF16) for i in range(2)]
        vh = [sb(f"_vh{i}", [128, TLOC], BF16) for i in range(2)]
        sqh = [sb(f"_sqh{i}", [128, TLOC], BF16) for i in range(2)]
        skh = [sb(f"_skh{i}", [128, TLOC], BF16) for i in range(2)]
        vvtm = [sb(f"_vvtmsb{i}", [128, 16, 128], BF16) for i in range(2)]
        t32a = sb("_t32a", [128, TLOC], F32)
        t32b = sb("_t32b", [128, TLOC], F32)
        scob = sb("_scob", [128, 512], F32)
        nmax = sb("_nmax", [128, 1], F32)
        nmaxs = sb("_nmaxs", [128, 1], F32)
        rsum = sb("_rsum", [128, 1], F32)
        rrec = sb("_rrec", [128, 1], F32)
        pnb = sb("_pnb", [128, 512], BF16)
        ptb = sb("_ptb", [128, 4, 128], BF16)
        tpe = sb("_tpe", [128, 128], BF16)
        ade = sb("_ade", [128, 128], BF16)
        wqb = [sb(f"_wq{i}", [128, 16, 128], BF16) for i in range(2)]
        wkb = [sb(f"_wk{i}", [128, 16, 128], BF16) for i in range(2)]
        wvb = [sb(f"_wv{i}", [128, 16, 128], BF16) for i in range(2)]

        with ExitStack() as pctx:
            ps_p = [pctx.enter_context(
                nc.psum_tensor(tag + f"psp{i}", [128, 512], F32))
                for i in (0, 1)]
            ps_s = pctx.enter_context(nc.psum_tensor(tag + "pss", [128, 512], F32))
            ps_t = pctx.enter_context(nc.psum_tensor(tag + "pst", [128, 128], BF16))
            ps_d = pctx.enter_context(nc.psum_tensor(tag + "psd", [128, 128], F32))

            pe_end = {}
            proj_evac = {}
            prev_proj_evac = [None, None]
            sq_stash_m = [None, None]
            sk_stash_m = [None, None]
            vvst_last = {}
            st = {"last_tp_evac": None, "last_adot_store": None,
                  "last_score_read": None}

            def stage1(h):
                hb = h % 2
                guard = [wmark, nm] if h < 2 else [pe_end[h - 2]]
                mwq = p.dma("sync", lambda s, h=h, hb=hb: s.dma_start(
                    out=wqb[hb][:], in_=w_chunk(0, h)), wait=guard,
                    stream="s2wq", slot=hb)
                mwk = p.dma("sync", lambda s, h=h, hb=hb: s.dma_start(
                    out=wkb[hb][:], in_=w_chunk(1, h)), wait=guard,
                    stream="s2wk", slot=hb)
                mwv = p.dma("sync", lambda s, h=h, hb=hb: s.dma_start(
                    out=wvb[hb][:], in_=w_chunk(2, h)), wait=guard,
                    stream="s2wv", slot=hb)
                for widx, (wb_, dst, wm_) in enumerate(
                        ((wqb, qh, mwq), (wkb, kh, mwk), (wvb, vh, mwv))):
                    for ts in range(4):
                        tl = slice(ts * 512, (ts + 1) * 512)
                        for kc in range(16):
                            mm = p.op(
                                "tensor",
                                lambda t, kc=kc, hb=hb, tl=tl, wb_=wb_:
                                t.matmul(ps_p[hb][:], wb_[hb][:, kc, :],
                                         xbuf[:, kc, tl],
                                         start=(kc == 0), stop=(kc == 15)),
                                wait=([wm_, nm, prev_proj_evac[hb]]
                                      if kc == 0 else ()),
                                inc="pe" if kc == 15 else None)
                        proj_evac[(h, widx, ts)] = prev_proj_evac[hb] = p.op(
                            "vector",
                            lambda v, dst=dst, tl=tl, hb=hb: v.tensor_copy(
                                dst[hb][:, tl], ps_p[hb][:]),
                            wait=[mm] + ([pe_end[h - 2]]
                                         if widx == 0 and ts == 0 and h >= 2
                                         else []),
                            inc="dve")

            def stage23(h):
                hb = h % 2
                qm = proj_evac[(h, 0, 3)]
                km = proj_evac[(h, 1, 3)]
                vm = proj_evac[(h, 2, 3)]

                def sigma(src, dst, wmk, stash_guard):
                    p.op("vector", lambda v, src=src: v.tensor_single_scalar(
                        t32a[:], src[hb][:], 0.0, ALU.min),
                        wait=[wmk], inc="dve")
                    am = p.op("scalar", lambda s: s.activation(
                        out=t32a[:], in_=t32a[:], func=AF.Exp),
                        wait=[p.mark("dve")], inc="act")
                    p.op("vector", lambda v, src=src: v.tensor_single_scalar(
                        t32b[:], src[hb][:], 0.0, ALU.max), inc="dve")
                    return p.op("vector", lambda v, dst=dst: v.tensor_add(
                        dst[hb][:], t32a[:], t32b[:]),
                        wait=[am, stash_guard], inc="dve")

                sqm = sigma(qh, sqh, qm, sq_stash_m[hb])
                skm = sigma(kh, skh, km, sk_stash_m[hb])
                sq_stash_m[hb] = p.dma("gpsimd", lambda g, h=h, hb=hb:
                                       g.dma_start(
                                           out=sq_d.ap()[h], in_=sqh[hb][:]),
                                       wait=[sqm], stream="s2sqst", slot=hb)
                sk_stash_m[hb] = p.dma("gpsimd", lambda g, h=h, hb=hb:
                                       g.dma_start(
                                           out=sk_d.ap()[h], in_=skh[hb][:]),
                                       wait=[skm], stream="s2skst", slot=hb)
                for sg in range(4):
                    p.op("vector", lambda v, h=h, sg=sg, hb=hb: v.reduce_sum(
                        colsum[:, h * 4 + sg:h * 4 + sg + 1],
                        skh[hb][:, sg * 512:(sg + 1) * 512], AXL.X),
                        wait=[skm] if sg == 0 else (), inc="dve")
                stm1 = None
                vvst_prev = vvst_last.get(h - 2)
                for tc in range(16):
                    t1 = p.op("tensor", lambda t, tc=tc, hb=hb: t.transpose(
                        ps_t[:], skh[hb][:, tc * 128:(tc + 1) * 128],
                        ident[:]),
                        wait=[skm, idm, st["last_tp_evac"]], inc="pe")
                    e1 = p.op("vector", lambda v: v.tensor_copy(
                        tpe[:], ps_t[:]), wait=[t1, stm1], inc="dve")
                    stm1 = p.dma("gpsimd", lambda g, h=h, tc=tc: g.dma_start(
                        out=sktm_d.ap()[h, tc], in_=tpe[:]), wait=[e1],
                        stream="s2tpst")
                    t2 = p.op("tensor", lambda t, tc=tc, hb=hb: t.transpose(
                        ps_t[:], vh[hb][:, tc * 128:(tc + 1) * 128],
                        ident[:]),
                        wait=[vm, e1], inc="pe")
                    e2 = p.op("vector", lambda v, tc=tc, hb=hb: v.tensor_copy(
                        vvtm[hb][:, tc, :], ps_t[:]),
                        wait=[t2, stm1, vvst_prev], inc="dve")
                    vvst_last[h] = p.dma(
                        "gpsimd", lambda g, h=h, tc=tc, hb=hb: g.dma_start(
                            out=vvtm_d.ap()[h, tc], in_=vvtm[hb][:, tc, :]),
                        wait=[e2], stream="s2vvst", slot=hb)
                    st["last_tp_evac"] = e2
                for sg in range(4):
                    sgl = slice(sg * 512, (sg + 1) * 512)
                    for sc in range(4):
                        scm = p.op(
                            "tensor", lambda t, sc=sc, sg=sg, sgl=sgl, hb=hb:
                            t.matmul(
                                ps_s[:],
                                qh[hb][:, sg * 512 + sc * 128:
                                       sg * 512 + (sc + 1) * 128],
                                kh[hb][:, sgl], start=True, stop=True),
                            wait=[qm, km, st["last_score_read"]], inc="pe")
                        p.op("vector", lambda v, sc=sc: v.tensor_add(
                            scob[:], ps_s[:], masks[:, sc, :]),
                            wait=[scm, mkm], inc="dve")
                        st["last_score_read"] = p.mark("dve")
                        p.op("vector", lambda v: v.tensor_reduce(
                            nmax[:], scob[:], AXL.X, ALU.max, negate=True),
                            inc="dve")
                        nsm = p.op("vector", lambda v: v.tensor_scalar_mul(
                            nmaxs[:], nmax[:], INV_SCALE_ATT), inc="dve")
                        em = p.op("scalar", lambda s: s.activation(
                            out=scob[:], in_=scob[:], func=AF.Exp,
                            bias=nmaxs[:], scale=INV_SCALE_ATT),
                            wait=[nsm], inc="act")
                        p.op("vector", lambda v: v.reduce_sum(
                            rsum[:], scob[:], AXL.X), wait=[em], inc="dve")
                        p.op("vector", lambda v: v.reciprocal(
                            rrec[:], rsum[:]), inc="dve")
                        pnm = p.op("vector", lambda v: v.tensor_scalar_mul(
                            pnb[:], scob[:], rrec[:]),
                            wait=[st["last_tp_evac"]], inc="dve")
                        for tb in range(4):
                            t3 = p.op(
                                "tensor", lambda t, tb=tb: t.transpose(
                                    ps_t[:], pnb[:, tb * 128:(tb + 1) * 128],
                                    ident[:]),
                                wait=[pnm, st["last_tp_evac"]], inc="pe")
                            st["last_tp_evac"] = p.op(
                                "vector", lambda v, tb=tb: v.tensor_copy(
                                    ptb[:, tb, :], ps_t[:]),
                                wait=[t3], inc="dve")
                        for tb in range(4):
                            admm = p.op(
                                "tensor",
                                lambda t, tb=tb, sg=sg, hb=hb: t.matmul(
                                    ps_d[:], ptb[:, tb, :],
                                    vvtm[hb][:, sg * 4 + tb, :],
                                    start=(tb == 0), stop=(tb == 3)),
                                wait=[st["last_tp_evac"],
                                      st["last_adot_store"]]
                                if tb == 0 else (),
                                inc="pe" if tb == 3 else None)
                        adm = p.op("vector", lambda v: v.tensor_copy(
                            ade[:], ps_d[:]), wait=[admm], inc="dve")
                        st["last_adot_store"] = p.dma(
                            "gpsimd", lambda g, h=h, sg=sg, sc=sc:
                            g.dma_start(out=adot_d.ap()[h, sg * 4 + sc],
                                        in_=ade[:]), wait=[adm],
                            stream="s2adst")
                pe_end[h] = p.milestone("tensor")

            stage1(0)
            for h in range(16):
                if h + 1 < 16:
                    stage1(h + 1)
                stage23(h)
            csm = p.dma("gpsimd", lambda g: g.dma_start(
                out=cc1_in.ap(), in_=colsum[:]), wait=[p.mark("dve")])
            ag1 = p.op("gpsimd", lambda g: g.collective_compute(
                "AllGather", ALU.bypass, replica_groups=[list(range(NCORE))],
                ins=[cc1_in.ap().opt()], outs=[cc1_out.ap().opt()]),
                wait=[csm], inc="cc")
        p.barrier()

    if _S2_STOP is not None and _S2_STOP != "b":
        return {}
    with ExitStack() as ctx:
        sb = lambda n, s, d: ctx.enter_context(nc.sbuf_tensor(tag + n, s, d))
        csall = sb("_csall", [128, 8, 64], F32)
        zmask = sb("_zmasksb", [128, 32], F32)
        onesr = sb("_onesr", [1, 128], F32)
        zmrow = sb("_zmrow", [1, 32], F32)
        skfm = sb("_skfm", [128, TLOC], BF16)
        sktm = sb("_sktmsb", [128, 16, 128], BF16)
        vvtm2 = sb("_vvtm2", [128, 16, 128], BF16)
        skdt = sb("_skdt", [128, 4, 128], BF16)
        dcols = sb("_dcols", [128, 16], F32)
        zt = sb("_zt", [128, 32], F32)
        zred = sb("_zred", [128, 1], F32)
        ab = sb("_ab", [128, 128], BF16)
        atb = sb("_atb", [128, 128], BF16)
        qb = sb("_qb", [128, 128], BF16)
        asg = sb("_asg", [128, 4, 128], BF16)
        qsg = sb("_qsg", [128, 4, 128], BF16)
        cmp32 = sb("_cmp32", [128, 128], F32)
        cmp32b = sb("_cmp32b", [128, 128], F32)

        om = p.op("vector", lambda v: v.memset(onesr[:], 1.0), inc="dve")
        zrm = p.dma("sync", lambda s: s.dma_start(
            out=zmrow[:], in_=consts["zmask"]), stream="s2zm")
        csm2 = p.dma("sync", lambda s: s.dma_start(
            out=csall[:], in_=cc1_out.ap().rearrange(
                "(r p) c -> p r c", p=128)), wait=[ag1], stream="s2cs")
        with ExitStack() as pctx:
            ps_b = pctx.enter_context(
                nc.psum_tensor(tag + "psb", [128, 32], F32))
            bm = p.op("tensor", lambda t: t.matmul(
                ps_b[:], onesr[:], zmrow[:], start=True, stop=True),
                wait=[om, zrm, csm2], inc="pe")
            zmm = p.op("vector", lambda v: v.tensor_copy(zmask[:], ps_b[:]),
                       wait=[bm], inc="dve")

        with ExitStack() as pctx:
            ps_z = pctx.enter_context(
                nc.psum_tensor(tag + "psz", [128, 16], F32))
            ps_a = pctx.enter_context(
                nc.psum_tensor(tag + "psa", [128, 128], F32))
            ps_q = pctx.enter_context(
                nc.psum_tensor(tag + "psq", [128, 128], F32))
            ps_c = pctx.enter_context(
                nc.psum_tensor(tag + "psc", [128, 128], F32))

            last_mm_2b = None
            last_dve_2b = None
            aqst_last = None
            pub_at_m = None
            pub_qb_m = None
            for h in range(16):
                lguard = [last_mm_2b, last_dve_2b] if h > 0 else []
                lm1 = p.dma("sync", lambda s, h=h: s.dma_start(
                    out=skfm[:], in_=sk_d.ap()[h]), wait=lguard,
                    stream="s2skfm")
                lm2 = p.dma("sync", lambda s, h=h: s.dma_start(
                    out=sktm[:], in_=sktm_d.ap()[h].rearrange(
                        "tc p m -> p tc m")), stream="s2sktmld")
                lm3 = p.dma("sync", lambda s, h=h: s.dma_start(
                    out=vvtm2[:], in_=vvtm_d.ap()[h].rearrange(
                        "tc p m -> p tc m")), stream="s2vvtmld")
                p.op("vector", lambda v, h=h: v.tensor_mul(
                    zt[:].rearrange("p (r s) -> p r s", r=8),
                    csall[:, :, h * 4:h * 4 + 4], zmask[:].rearrange(
                        "p (r s) -> p r s", r=8)),
                    wait=[csm2, zmm], inc="dve")
                p.op("vector", lambda v: v.reduce_sum(
                    zred[:], zt[:], AXL.X), inc="dve")
                p.op("vector", lambda v, h=h: v.tensor_scalar_add(
                    zall[:, h * 4:h * 4 + 1], zred[:], 1.0 / 128.0),
                    inc="dve")
                for s2 in range(1, 4):
                    p.op("vector", lambda v, h=h, s2=s2: v.tensor_add(
                        zall[:, h * 4 + s2:h * 4 + s2 + 1],
                        zall[:, h * 4 + s2 - 1:h * 4 + s2],
                        colsum[:, h * 4 + s2 - 1:h * 4 + s2]), inc="dve")
                p.op("vector", lambda v, h=h: v.tensor_copy(
                    zallbf[:, h * 4:h * 4 + 4], zall[:, h * 4:h * 4 + 4]),
                    inc="dve")
                zm = p.mark("dve")
                for j in range(16):
                    sg2 = j // 4
                    zc = p.op("tensor", lambda t, j=j, sg2=sg2, h=h: t.matmul(
                        ps_z[:, j:j + 1],
                        skfm[:, j * 128:(j + 1) * 128],
                        zallbf[:, h * 4 + sg2:h * 4 + sg2 + 1],
                        start=(j == 0), stop=(j == 15),
                        skip_group_check=True),
                        wait=[lm1, zm] if j == 0 else (),
                        inc="pe" if j == 15 else None)
                dcm = p.op("vector", lambda v: v.reciprocal(
                    dcols[:], ps_z[:]), wait=[zc], inc="dve")
                evacA = {}
                eq_prev = None
                aqst_prev = aqst_last
                for sg2 in range(4):
                    for tcw in range(4):
                        j = sg2 * 4 + tcw
                        p.op("vector", lambda v, j=j, tcw=tcw: v.tensor_scalar(
                            skdt[:, tcw, :], sktm[:, j, :],
                            dcols[:, j:j + 1], None, ALU.mult, ALU.bypass),
                            wait=[dcm, lm2] if j == 0 else (), inc="dve")
                    sdm = p.mark("dve")
                    for tcw in range(4):
                        j = sg2 * 4 + tcw
                        am = p.op("tensor", lambda t, j=j, tcw=tcw: t.matmul(
                            ps_a[:], sktm[:, j, :], skdt[:, tcw, :],
                            start=(tcw == 0), stop=(tcw == 3)),
                            wait=[sdm] if tcw == 0 else (),
                            inc="pe" if tcw == 3 else None)
                    for tcw in range(4):
                        j = sg2 * 4 + tcw
                        qmm = p.op("tensor", lambda t, j=j: t.matmul(
                            ps_q[:], sktm[:, j, :], vvtm2[:, j, :],
                            start=(j % 4 == 0), stop=(j % 4 == 3)),
                            wait=([lm3] if sg2 == 0 else [eq_prev])
                            if tcw == 0 else (),
                            inc="pe" if j % 4 == 3 else None)
                    ea = p.op("vector", lambda v, sg2=sg2: v.tensor_copy(
                        asg[:, sg2, :], ps_a[:]),
                        wait=[am, aqst_prev], inc="dve")
                    eq = p.op("vector", lambda v, sg2=sg2: v.tensor_copy(
                        qsg[:, sg2, :], ps_q[:]),
                        wait=[qmm, aqst_prev], inc="dve")
                    evacA[sg2] = (ea, eq)
                    eq_prev = eq
                    p.dma(
                        "gpsimd", lambda g, h=h, sg2=sg2: g.dma_start(
                            out=aq_d.ap()[h, sg2, 0], in_=asg[:, sg2, :]),
                        wait=[ea], stream="s2aqst")
                    aqst_last = p.dma(
                        "gpsimd", lambda g, h=h, sg2=sg2: g.dma_start(
                            out=aq_d.ap()[h, sg2, 1], in_=qsg[:, sg2, :]),
                        wait=[eq], stream="s2aqst")
                c0 = p.op("vector", lambda v: v.tensor_copy(
                    ab[:], asg[:, 0, :]), wait=[evacA[0][0]], inc="dve")
                p.op("vector", lambda v: v.tensor_copy(
                    atb[:], asg[:, 0, :]), wait=[pub_at_m], inc="dve")
                cm = p.op("vector", lambda v: v.tensor_copy(
                    qb[:], qsg[:, 0, :]),
                    wait=[evacA[0][1], pub_qb_m], inc="dve")
                for s2 in range(1, 4):
                    m1 = p.op("tensor", lambda t, s2=s2: t.matmul(
                        ps_c[:], asg[:, s2, :], ab[:],
                        start=True, stop=True),
                        wait=[cm, evacA[s2][0], evacA[s2][1]], inc="pe")
                    m2 = p.op("tensor", lambda t, s2=s2: t.matmul(
                        ps_a[:], ab[:], asg[:, s2, :],
                        start=True, stop=True), inc="pe")
                    m3 = p.op("tensor", lambda t, s2=s2: t.matmul(
                        ps_q[:], asg[:, s2, :], qb[:],
                        start=True, stop=True), inc="pe")
                    p.op("vector", lambda v, s2=s2: v.tensor_add(
                        cmp32[:], asg[:, s2, :], ab[:]), wait=[m3], inc="dve")
                    p.op("vector", lambda v: v.tensor_sub(
                        ab[:], cmp32[:], ps_c[:]), inc="dve")
                    p.op("vector", lambda v, s2=s2: v.tensor_add(
                        cmp32b[:], asg[:, s2, :], atb[:]), inc="dve")
                    p.op("vector", lambda v: v.tensor_sub(
                        atb[:], cmp32b[:], ps_a[:]), inc="dve")
                    p.op("vector", lambda v, s2=s2: v.tensor_add(
                        cmp32[:], qsg[:, s2, :], qb[:]), inc="dve")
                    cm = p.op("vector", lambda v: v.tensor_sub(
                        qb[:], cmp32[:], ps_q[:]), inc="dve")
                    last_mm_2b = m3
                pub_at_m = p.dma("gpsimd", lambda g, h=h: g.dma_start(
                    out=cc2_in.ap()[:, h * 256:h * 256 + 128], in_=atb[:]),
                    wait=[cm], stream="s2pub", slot=0)
                last_dve_2b = cm
                pub_qb_m = p.dma("gpsimd", lambda g, h=h: g.dma_start(
                    out=cc2_in.ap()[:, h * 256 + 128:h * 256 + 256],
                    in_=qb[:]), wait=(), stream="s2pub", slot=1)
            ag2 = p.op("gpsimd", lambda g: g.collective_compute(
                "AllGather", ALU.bypass, replica_groups=[list(range(NCORE))],
                ins=[cc2_in.ap().opt()], outs=[cc2_out.ap().opt()]),
                wait=[pub_at_m, pub_qb_m], inc="cc")
        p.barrier()

    if _S2_STOP == "b":
        return {}
    with ExitStack() as ctx:
        sb = lambda n, s, d: ctx.enter_context(nc.sbuf_tensor(tag + n, s, d))
        wob = sb("_wob", [128, 16, 16, 128], BF16)
        attout = sb("_attout", [128, 16, TLOC], BF16)
        smask = sb("_smasksb", [128, 8], F32)
        smrow = sb("_smrow", [1, 8], F32)
        onesr2 = sb("_onesr2", [1, 128], F32)
        betar = sb("_betar", [1, 2048], F32)
        betab = sb("_betab", [128, 16, 128], F32)
        agb = sb("_agb", [128, 8, 256], BF16)
        sqh2 = sb("_sqh2", [128, TLOC], BF16)
        adb = sb("_adb", [128, 16, 128], BF16)
        asg2 = sb("_asg2", [128, 4, 128], BF16)
        qsg2 = sb("_qsg2", [128, 4, 128], BF16)
        m32 = sb("_m32", [128, 128], F32)
        mbf = sb("_mbf", [128, 128], BF16)
        qd = sb("_qd", [128, 16], F32)
        am32 = sb("_am32", [128, 128], F32)
        t3b = sb("_t3b", [128, 128], F32)
        attb = sb("_attb", [128, 128], BF16)
        attfm = sb("_attfm", [128, TLOC], BF16)
        xres2 = sb("_xres2", [128, 512], F32)
        outb2 = sb("_outb2", [128, 512], F32)

        om2 = p.op("vector", lambda v: v.memset(onesr2[:], 1.0), inc="dve")
        smm = p.dma("sync", lambda s: s.dma_start(
            out=smrow[:], in_=consts["smask"]), stream="s2sm")
        btm = p.dma("sync", lambda s: s.dma_start(
            out=betar[:], in_=consts["betas"]), stream="s2bt")
        btsig = p.op("scalar", lambda s: s.activation(
            out=betar[:], in_=betar[:], func=AF.Sigmoid),
            wait=[btm], inc="act")
        wo_marks = []
        for oc in range(16):
            wo_marks.append(p.dma(
                "scalar", lambda s, oc=oc: s.dma_start(
                    out=wob[:, oc, :, :], in_=w_chunk(3, oc)), wait=[wmark]))
        wom = wo_marks[-1]
        with ExitStack() as pctx:
            ps_bc = pctx.enter_context(
                nc.psum_tensor(tag + "psbc", [128, 128], F32))
            bmk = None
            for hh in range(16):
                bmk = p.op("tensor", lambda t, hh=hh: t.matmul(
                    ps_bc[:], onesr2[:], betar[:, hh * 128:(hh + 1) * 128],
                    start=True, stop=True),
                    wait=[om2, btsig] if hh == 0 else [p.mark("dve")],
                    inc="pe")
                p.op("vector", lambda v, hh=hh: v.tensor_copy(
                    betab[:, hh, :], ps_bc[:]), wait=[bmk], inc="dve")
            smb = p.op("tensor", lambda t: t.matmul(
                ps_bc[:, 0:8], onesr2[:], smrow[:], start=True, stop=True),
                wait=[smm, p.mark("dve")], inc="pe")
            p.op("vector", lambda v: v.tensor_copy(smask[:], ps_bc[:, 0:8]),
                 wait=[smb], inc="dve")

        with ExitStack() as pctx:
            ps_f = pctx.enter_context(
                nc.psum_tensor(tag + "psf", [128, 128], F32))
            ps_zq = pctx.enter_context(
                nc.psum_tensor(tag + "pszq", [128, 16], F32))
            ps_m = pctx.enter_context(
                nc.psum_tensor(tag + "psm", [128, 128], F32))
            ps_t2 = pctx.enter_context(
                nc.psum_tensor(tag + "pst2", [128, 128], BF16))
            ps_w = [pctx.enter_context(
                nc.psum_tensor(tag + f"psw{i}", [128, 128], F32))
                for i in range(2)]

            pe_prev = None
            dve_prev = None
            for h in range(16):
                lg = [pe_prev, dve_prev] if h > 0 else [ag2]
                lag = p.dma("sync", lambda s, h=h: s.dma_start(
                    out=agb[:], in_=cc2_out.ap().rearrange(
                        "(r p) c -> p r c", p=128)[
                        :, :, h * 256:(h + 1) * 256]),
                    wait=lg + [ag2] if h == 0 else lg)
                lsq = p.dma("sync", lambda s, h=h: s.dma_start(
                    out=sqh2[:], in_=sq_d.ap()[h]))
                lad = p.dma("sync", lambda s, h=h: s.dma_start(
                    out=adb[:], in_=adot_d.ap()[h].rearrange(
                        "tc p m -> p tc m")))
                laq = p.dma("sync", lambda s, h=h: s.dma_start(
                    out=asg2[:], in_=aq_d.ap()[h, :, 0].rearrange(
                        "sg p m -> p sg m")))
                laq2 = p.dma("sync", lambda s, h=h: s.dma_start(
                    out=qsg2[:], in_=aq_d.ap()[h, :, 1].rearrange(
                        "sg p m -> p sg m")))
                hlm = p.mark("d_sync")
                mm0 = p.op("vector", lambda v: v.memset(m32[:], 0.0),
                           wait=[pe_prev] if h > 0 else [], inc="dve")
                mmb = p.op("vector", lambda v: v.memset(mbf[:], 0.0),
                           inc="dve")
                for j in range(NCORE):
                    fm = p.op("tensor", lambda t, j=j: t.matmul(
                        ps_f[:], agb[:, j, 0:128], mbf[:],
                        start=True, stop=True),
                        wait=[hlm, mmb] if j == 0 else [mmb], inc="pe")
                    p.op("vector", lambda v, j=j: v.tensor_sub(
                        t3b[:], agb[:, j, 128:256], ps_f[:]),
                        wait=[fm], inc="dve")
                    p.op("vector", lambda v, j=j: v.scalar_tensor_tensor(
                        m32[:], t3b[:], smask[:, j:j + 1], m32[:],
                        ALU.mult, ALU.add), inc="dve")
                    mmb = p.op("vector", lambda v: v.tensor_copy(
                        mbf[:], m32[:]), inc="dve")
                for j in range(16):
                    sg2 = j // 4
                    zq = p.op("tensor", lambda t, j=j, sg2=sg2, h=h: t.matmul(
                        ps_zq[:, j:j + 1],
                        sqh2[:, j * 128:(j + 1) * 128],
                        zallbf[:, h * 4 + sg2:h * 4 + sg2 + 1],
                        start=(j == 0), stop=(j == 15),
                        skip_group_check=True),
                        wait=[hlm] if j == 0 else (),
                        inc="pe" if j == 15 else None)
                qdm = p.op("vector", lambda v: v.reciprocal(
                    qd[:], ps_zq[:]), wait=[zq], inc="dve")
                tpev_m = None
                amev = None
                for sg2 in range(4):
                    for tcw in range(4):
                        j = sg2 * 4 + tcw
                        amm = p.op("tensor", lambda t, j=j: t.matmul(
                            ps_m[:], sqh2[:, j * 128:(j + 1) * 128], mbf[:],
                            start=True, stop=True),
                            wait=[mmb, qdm, amev] if j == 0
                            else [mmb, amev], inc="pe")
                        amev = p.op("vector", lambda v, j=j: v.tensor_scalar(
                            am32[:], ps_m[:], qd[:, j:j + 1], None,
                            ALU.mult, ALU.bypass), wait=[amm, hlm],
                            inc="dve")
                        p.op("vector", lambda v, j=j: v.tensor_sub(
                            t3b[:], am32[:], adb[:, j, :]), inc="dve")
                        p.op("vector", lambda v, h=h: v.tensor_mul(
                            t3b[:], t3b[:], betab[:, h, :]), inc="dve")
                        abm = p.op("vector", lambda v, j=j: v.tensor_add(
                            attb[:], t3b[:], adb[:, j, :]),
                            wait=[tpev_m], inc="dve")
                        tpm = p.op("tensor", lambda t, j=j: t.transpose(
                            ps_t2[:], attb[:], ident[:]),
                            wait=[abm], inc="pe")
                        tpev_m = p.op("vector", lambda v, j=j: v.tensor_copy(
                            attfm[:, j * 128:(j + 1) * 128], ps_t2[:]),
                            wait=[tpm], inc="dve")
                    if sg2 < 3:
                        um = p.op("tensor", lambda t, sg2=sg2: t.matmul(
                            ps_f[:], asg2[:, sg2, :], mbf[:],
                            start=True, stop=True),
                            wait=[hlm] if sg2 == 0 else (), inc="pe")
                        p.op("vector", lambda v, sg2=sg2: v.tensor_sub(
                            t3b[:], qsg2[:, sg2, :], ps_f[:]),
                            wait=[um], inc="dve")
                        p.op("vector", lambda v: v.tensor_add(
                            m32[:], m32[:], t3b[:]), inc="dve")
                        mmb = p.op("vector", lambda v: v.tensor_copy(
                            mbf[:], m32[:]), inc="dve")
                att_r = attfm[:].rearrange(
                    "p (sg sb k) -> p k sg sb", sg=4, sb=32, k=16)
                out_r = attout[:].rearrange(
                    "p dc (sg q sb) -> p dc sg q sb", sg=4, q=16, sb=32)
                wo_ev = None
                for dc in range(16):
                    for kc in range(16):
                        wmm = p.op(
                            "tensor", lambda t, dc=dc, kc=kc: t.matmul(
                                ps_w[dc % 2][:].rearrange(
                                    "p (sg sb) -> p sg sb", sg=4),
                                wob[:, dc, kc, :], att_r[:, kc],
                                start=(kc == 0), stop=(kc == 15)),
                            wait=([tpev_m, wom] if dc == 0 else
                                  [p.mark("dve")]) if kc == 0 else (),
                            inc="pe" if kc == 15 else None)
                    wo_ev = p.op("vector", lambda v, dc=dc, h=h: v.tensor_copy(
                        out_r[:, dc, :, h, :],
                        ps_w[dc % 2][:].rearrange("p (sg sb) -> p sg sb",
                                                  sg=4)),
                        wait=[wmm], inc="dve")
                pe_prev = p.milestone("tensor")
                dve_prev = wo_ev
            som = None
            for dc in range(16):
                for ts in range(4):
                    tl = slice(ts * 512, (ts + 1) * 512)
                    xm3 = p.dma("sync", lambda s, dc=dc, tl=tl: s.dma_start(
                        out=xres2[:],
                        in_=x_in.rearrange("(kc p) t -> p kc t", p=128)[
                            :, dc, tl]),
                        wait=[som], stream="s2xres")
                    aom = p.op("vector", lambda v, dc=dc, tl=tl: v.tensor_add(
                        outb2[:], attout[:, dc, tl], xres2[:]),
                        wait=[xm3, dve_prev], inc="dve")
                    som = p.dma("gpsimd", lambda g, dc=dc, tl=tl: g.dma_start(
                        out=x_out.rearrange("(kc p) t -> p kc t", p=128)[
                            :, dc, tl],
                        in_=outb2[:]), wait=[aom], stream="s2xst")
        p.barrier()
    octx.close()
    return {}


def prep_attn0_weights(Wq, Wk, Wv, Wo):
    ts_ = [tile_w_in_out(np.asarray(W), 16, 16) for W in (Wq, Wk, Wv, Wo)]
    slices = []
    for r in range(NCORE):
        s = np.concatenate([t[2 * r:2 * r + 2].ravel() for t in ts_])
        slices.append(np.ascontiguousarray(s))
    return slices


def make_attn0_launch():
    nc = bass.Bass("TRN2", target_bir_lowering=False, debug=False)
    ctx = ExitStack()
    x_in = nc.dram_tensor("x", [D, TLOC], F32, kind="ExternalInput").ap()
    per_rank = 4 * 2 * 128 * 16 * 128
    wflat = nc.dram_tensor("wflat", [per_rank], F32, kind="ExternalInput").ap()
    ident = nc.dram_tensor("ident", [128, 128], BF16, kind="ExternalInput").ap()
    masks = nc.dram_tensor("masks", [4, 128, 512], F32, kind="ExternalInput").ap()
    zmask = nc.dram_tensor("zmask", [1, 32], F32, kind="ExternalInput").ap()
    smask = nc.dram_tensor("smask", [1, 8], F32, kind="ExternalInput").ap()
    betas = nc.dram_tensor("betas", [1, 2048], F32, kind="ExternalInput").ap()
    x_out = nc.dram_tensor("xout", [D, TLOC], F32, kind="ExternalOutput").ap()
    consts = {"ident": ident, "masks": masks, "zmask": zmask, "smask": smask,
              "betas": betas}
    with ctx:
        p = Prog(nc, ctx)
        build_attn0(nc, p, x_in, x_out, wflat, consts)
        p.finalize([p.mark("d_gpsimd")])
    return nc


def core_masks():
    zs, ss = [], []
    for c in range(NCORE):
        b, qt = c // 4, c % 4
        zm = np.zeros((1, 32), np.float32)
        sm = np.zeros((1, 8), np.float32)
        for r in range(NCORE):
            if r // 4 != b:
                continue
            if r % 4 < qt:
                sm[0, r] = 1.0
            for sg in range(4):
                if 4 * (r % 4) + sg < 4 * qt:
                    zm[0, r * 4 + sg] = 1.0
        zs.append(zm)
        ss.append(sm)
    return zs, ss


def run_attn0(x_slices, Wq, Wk, Wv, Wo, betas):
    nc = make_attn0_launch()
    wsl = prep_attn0_weights(Wq, Wk, Wv, Wo)
    idb = np.eye(128, dtype=np.float32).astype(ml_dtypes.bfloat16)
    msk = causal_masks()
    zs, ss = core_masks()
    brow = np.ascontiguousarray(
        np.asarray(betas).reshape(16, 128).reshape(1, 2048).astype(np.float32))
    in_maps = [{"x": x_slices[c], "wflat": wsl[c], "ident": idb,
                "masks": msk, "zmask": zs[c], "smask": ss[c], "betas": brow}
               for c in range(NCORE)]
    res = run_bass_kernel_spmd(nc, in_maps, core_ids=list(range(NCORE)))
    return [res.results[c]["xout"] for c in range(NCORE)]


N_GU_OT = 11
N_DN_OT = 2
GU_ELEMS = N_GU_OT * 16 * 128 * 128
DN_ELEMS = N_DN_OT * 44 * 128 * 128
FFN1_PER_RANK = GU_ELEMS + DN_ELEMS


def dequant_block(nc, p, ctx, tag, pk_ext, sc_ext, n_ot, n_kc, wc_out,
                  wc_base, ident, idm):
    W = n_kc * 16
    sb = lambda n, s, d: ctx.enter_context(nc.sbuf_tensor(tag + n, s, d))
    pk = sb("_pk", [128, W], I32)
    m3i = sb("_m3i", [128, W], I32)
    sfi = sb("_sfi", [128, W], I32)
    m3f = sb("_m3f", [128, W], F32)
    sff = sb("_sff", [128, W], F32)
    t1b = sb("_t1", [128, W], F32)
    t2b = sb("_t2", [128, W], F32)
    vb = sb("_v", [128, W], F32)
    fb = sb("_f", [128, W], F32)
    dq = sb("_dq", [128, n_ot, n_kc * 128], F32)
    scb = sb("_scb", [128, n_ot * 128], F32)
    cvt = sb("_cvt", [128, 128], BF16)
    wt = sb("_wt", [128, n_ot, 128], BF16)
    with ExitStack() as pctx:
        ps_t = pctx.enter_context(nc.psum_tensor(tag + "pst", [128, 128], BF16))
        pk_m = None
        dq_m = None
        for ot in range(n_ot):
            pk_m = p.dma("sync", lambda s, ot=ot: s.dma_start(
                out=pk[:], in_=pk_ext[ot]),
                wait=[dq_m] if dq_m else [], stream=tag + "pk")
            for k in range(8):
                p.op("vector", lambda v, k=k: v.tensor_scalar(
                    m3i[:], pk[:], 4 * k, 7,
                    ALU.logical_shift_right, ALU.bitwise_and),
                    wait=[pk_m], inc="dve")
                p.op("vector", lambda v, k=k: v.tensor_scalar(
                    sfi[:], pk[:], 4 * k + 3, 1,
                    ALU.logical_shift_right, ALU.bitwise_and), inc="dve")
                p.op("vector", lambda v: v.tensor_copy(m3f[:], m3i[:]),
                     inc="dve")
                p.op("vector", lambda v: v.tensor_copy(sff[:], sfi[:]),
                     inc="dve")
                p.op("vector", lambda v: v.tensor_scalar(
                    t1b[:], m3f[:], -4.0, 0.0, ALU.add, ALU.max), inc="dve")
                p.op("vector", lambda v: v.tensor_scalar(
                    t2b[:], m3f[:], -6.0, 0.0, ALU.add, ALU.max), inc="dve")
                p.op("vector", lambda v: v.tensor_add(
                    vb[:], m3f[:], t1b[:]), inc="dve")
                p.op("vector", lambda v: v.scalar_tensor_tensor(
                    vb[:], t2b[:], 2.0, vb[:], ALU.mult, ALU.add), inc="dve")
                p.op("vector", lambda v: v.tensor_scalar(
                    fb[:], sff[:], -0.5, 0.25, ALU.mult, ALU.add), inc="dve")
                dq_m = p.op(
                    "vector", lambda v, ot=ot, k=k: v.tensor_mul(
                        dq[:, ot, :].rearrange("p (w e) -> p w e", e=8)
                        [:, :, k], vb[:], fb[:]), inc="dve")
        st_m = None
        last_ev = None
        for kc in range(n_kc):
            sc_m = p.dma("gpsimd", lambda g, kc=kc: g.dma_start(
                out=scb[:],
                in_=bass.AP(sc_ext.tensor,
                            sc_ext.offset + kc * (n_ot * 128),
                            [[0, 128], [1, n_ot * 128]])),
                wait=[st_m] if st_m else [dq_m],
                stream=tag + "sc", slot=kc % 2)
            for ot in range(n_ot):
                cm = p.op("vector", lambda v, kc=kc, ot=ot: v.tensor_copy(
                    cvt[:], dq[:, ot, kc * 128:(kc + 1) * 128]),
                    wait=[dq_m, last_ev], inc="dve")
                tm = p.op("tensor", lambda t: t.transpose(
                    ps_t[:], cvt[:], ident[:]),
                    wait=[cm, idm, last_ev], inc="pe")
                last_ev = p.op(
                    "vector", lambda v, kc=kc, ot=ot: v.tensor_tensor(
                        wt[:, ot, :], ps_t[:],
                        scb[:, ot * 128:(ot + 1) * 128], ALU.mult),
                    wait=[tm, sc_m], inc="dve")
            st_m = p.dma("gpsimd", lambda g, kc=kc: g.dma_start(
                out=bass.AP(wc_out.tensor,
                            wc_out.offset + wc_base + kc * 128,
                            [[n_kc * 128, 128], [n_kc * 16384, n_ot],
                             [1, 128]]),
                in_=wt[:]), wait=[last_ev], stream=tag + "st", slot=kc % 2)
        return st_m


def stage_ffn1_weights(nc, p, ins, tag="s4d"):
    cc_in = nc.dram_tensor(tag + "_cci", [FFN1_PER_RANK], BF16,
                           kind="Internal")
    cc_out = nc.dram_tensor(tag + "_cco", [NCORE * FFN1_PER_RANK], BF16,
                            kind="Internal")
    with ExitStack() as octx:
        ident = octx.enter_context(
            nc.sbuf_tensor(tag + "_id", [128, 128], BF16))
        idm = p.dma("gpsimd", lambda g: g.dma_start(
            out=ident[:], in_=ins["ident"]), stream=tag + "id")
        with ExitStack() as ctx:
            m1 = dequant_block(nc, p, ctx, tag + "g", ins["gu_pk"],
                               ins["gu_sc"], N_GU_OT, 16, cc_in.ap(), 0,
                               ident, idm)
        p.barrier(exclude=("cc",))
        with ExitStack() as ctx:
            m2 = dequant_block(nc, p, ctx, tag + "d", ins["dn_pk"],
                               ins["dn_sc"], N_DN_OT, 44, cc_in.ap(),
                               GU_ELEMS, ident, idm)
        agm = p.op("gpsimd", lambda g: g.collective_compute(
            "AllGather", ALU.bypass,
            replica_groups=[list(range(NCORE))],
            ins=[cc_in.ap().opt()], outs=[cc_out.ap().opt()]),
            wait=[m1, m2], inc="cc")
        p.barrier(exclude=("cc",))
    return (cc_out, agm)


def build_ffn1(nc, p, x_in, x_out, ins, tag="s4", staged=None):
    if staged is None:
        staged = stage_ffn1_weights(nc, p, ins, tag + "d")
    cc_out = staged[0]
    cco = cc_out.ap()

    def gu_chunkset(oc):
        r, loc = oc // N_GU_OT, oc % N_GU_OT
        off = r * FFN1_PER_RANK + loc * (16 * 16384)
        return bass.AP(cco.tensor, cco.offset + off,
                       [[16 * 128, 128], [128, 16], [1, 128]])

    def dn_chunkset(oc):
        r, loc = oc // N_DN_OT, oc % N_DN_OT
        off = r * FFN1_PER_RANK + GU_ELEMS + loc * (44 * 16384)
        return bass.AP(cco.tensor, cco.offset + off,
                       [[44 * 128, 128], [128, 44], [1, 128]])

    agm = staged[1]
    with ExitStack() as octx:
        p.barrier()

        with ExitStack() as ctx:
            sb = lambda n, s, d: ctx.enter_context(
                nc.sbuf_tensor(tag + n, s, d))
            xs_ = sb("_xs", [128, DC, 1024], BF16)
            hbuf = sb("_hb", [128, 44, 1024], BF16)
            ybf = xs_
            ln = LNCtx(nc, ctx, tag + "_ln", 1024)
            sqb = ln.sqbf
            silu_t = sb("_silu", [128, 512], F32)
            rmsrow = ln.m2row
            rb = ln.mbc
            tmp32 = ln.tmp32
            xres = ln.rbc
            ones1f = sb("_o1f", [1, 128], F32)
            onesb = sb("_o1b", [128, 1], BF16)
            wg_ = [sb(f"_wg{i}", [128, 16, 128], BF16) for i in range(2)]
            wu_ = [sb(f"_wu{i}", [128, 16, 128], BF16) for i in range(2)]
            wdb = [sb("_wd0", [128, 44, 128], BF16)] * 2
            p.op("vector", lambda v: v.memset(ones1f[:], 1.0), inc="dve")
            om = p.op("vector", lambda v: v.memset(onesb[:], 1.0), inc="dve")

            def bcast_row(row_ap, dst, ps_c, wait):
                b1 = p.op("tensor", lambda t: t.matmul(
                    ps_c[:, 0:512], ones1f[:], row_ap[:, 0:512],
                    start=True, stop=True), wait=wait, inc="pe")
                b2 = p.op("tensor", lambda t: t.matmul(
                    ps_c[:, 512:1024], ones1f[:], row_ap[:, 512:1024],
                    start=True, stop=True), inc="pe")
                return p.op("vector", lambda v: v.tensor_copy(
                    dst[:], ps_c[:]), wait=[b2], inc="dve")

            if True:
                st_m = None
                hm_prev = None
                for tsp in range(2):
                    t0 = tsp * 1024
                    xm2 = load_xbuf(p, x_in, xs_, t0=t0, T=1024, n_split=2,
                                    wait=[p.mark("pe"), p.mark("dve")]
                                    if tsp else ())
                    lnm = ln_compute(p, nc, ctx, ln, xs_, DC, xm2)
                    nmk = ln_normalize(p, ln, xs_, DC, lnm,
                                       extra=[st_m] if st_m else [])
                    pctx = ExitStack()
                    psum = lambda n, s, d: pctx.enter_context(
                        nc.psum_tensor(tag + n + str(tsp), s, d))
                    ps_g = [psum(f"psg{i}", [128, 512], F32) for i in (0, 1)]
                    ps_u = [psum(f"psu{i}", [128, 512], F32) for i in (0, 1)]
                    ps_c = psum("psc", [128, 1024], F32)
                    ps_s = psum("pss", [1, 1024], F32)
                    hmj = {}
                    slj = {}
                    for j in range(44):
                        jb = j % 2
                        gl = p.dma("scalar", lambda s, j=j, jb=jb: s.dma_start(
                            out=wg_[jb][:], in_=gu_chunkset(j)),
                            wait=[agm] if tsp == 0 and j == 0
                            else [hmj.get(j - 2) or hm_prev],
                            stream="f1wg", slot=jb)
                        ul = p.dma("sync", lambda s, j=j, jb=jb: s.dma_start(
                            out=wu_[jb][:], in_=gu_chunkset(44 + j)),
                            wait=[agm] if tsp == 0 and j == 0
                            else [hmj.get(j - 2) or hm_prev],
                            stream="f1wu", slot=jb)
                        for n2 in range(2):
                            nl = slice(n2 * 512, (n2 + 1) * 512)
                            for kc in range(16):
                                p.op("tensor",
                                     lambda t, kc=kc, nl=nl, jb=jb: t.matmul(
                                         ps_g[jb][:], wg_[jb][:, kc, :],
                                         xs_[:, kc, nl],
                                         start=(kc == 0), stop=(kc == 15)),
                                     wait=([gl, nmk, slj.get(j - 2)]
                                           if n2 == 0 else
                                           [slj.get((j, 0))])
                                     if kc == 0 else ())
                            for kc in range(16):
                                ua = p.op(
                                    "tensor",
                                    lambda t, kc=kc, nl=nl, jb=jb: t.matmul(
                                        ps_u[jb][:], wu_[jb][:, kc, :],
                                        xs_[:, kc, nl],
                                        start=(kc == 0), stop=(kc == 15)),
                                    wait=[ul, hmj.get((j - 2, 1) if n2 == 0
                                                      else (j, 0))]
                                    if kc == 0 else (),
                                    inc="pe" if kc == 15 else None)
                            slj[(j, n2)] = p.op(
                                "scalar", lambda s, jb=jb: s.activation(
                                    out=silu_t[:], in_=ps_g[jb][:],
                                    func=AF.Silu),
                                wait=[ua, hm_prev], inc="act")
                            hmj[(j, n2)] = hm_prev = p.op(
                                "vector",
                                lambda v, j=j, nl=nl, jb=jb: v.tensor_mul(
                                    hbuf[:, j, nl], silu_t[:], ps_u[jb][:]),
                                wait=[slj[(j, n2)]], inc="dve")
                        slj[j - 1] = slj.get((j - 1, 1))
                        hmj[j - 1] = hmj.get((j - 1, 1))
                    hmj[43] = hmj[(43, 1)]
                    slj[43] = slj[(43, 1)]
                    hmk = hm_prev
                    yev = {}
                    sm_ = None
                    for dc in range(16):
                        db = dc % 2
                        dl = p.dma("scalar", lambda s, dc=dc, db=db:
                                   s.dma_start(
                                       out=wdb[db][:], in_=dn_chunkset(dc)),
                                   wait=[yev.get(dc - 1) or hmk],
                                   stream="f1wd", slot=db)
                        for n2 in range(2):
                            nl = slice(n2 * 512, (n2 + 1) * 512)
                            for kc in range(44):
                                da = p.op(
                                    "tensor",
                                    lambda t, kc=kc, nl=nl, dc=dc, n2=n2,
                                    db=db: t.matmul(
                                        ps_g[db][:] if n2 == 0
                                        else ps_u[db][:],
                                        wdb[db][:, kc, :],
                                        hbuf[:, kc, nl],
                                        start=(kc == 0), stop=(kc == 43)),
                                    wait=[dl, hmk, yev.get(dc - 2)]
                                    if kc == 0 and n2 == 0 else (),
                                    inc="pe" if kc == 43 else None)
                        p.op("vector", lambda v, dc=dc, db=db: v.tensor_copy(
                            ybf[:, dc, 0:512], ps_g[db][:]), wait=[da],
                            inc="dve")
                        ye = p.op("vector", lambda v, dc=dc, db=db:
                                  v.tensor_copy(
                                      ybf[:, dc, 512:1024], ps_u[db][:]),
                                  inc="dve")
                        sq = p.op("scalar", lambda s, dc=dc: s.activation(
                            out=sqb[:], in_=ybf[:, dc, :], func=AF.Square),
                            wait=[ye], inc="act")
                        for n2 in range(2):
                            nl = slice(n2 * 512, (n2 + 1) * 512)
                            sm_ = p.op(
                                "tensor", lambda t, nl=nl, dc=dc: t.matmul(
                                    ps_s[:, nl], onesb[:], sqb[:, nl],
                                    start=(dc == 0), stop=(dc == 15),
                                    skip_group_check=True),
                                wait=[sq, om] if n2 == 0 else (),
                                inc="pe" if dc == 15 and n2 == 1 else None)
                        yev[dc] = ye
                    p.op("vector", lambda v: v.tensor_scalar(
                        rmsrow[:], ps_s[:], 1.0 / 2048.0, float(EPS_RMS),
                        ALU.mult, ALU.add), wait=[sm_], inc="dve")
                    lnr = p.op("scalar", lambda s: s.activation(
                        out=rmsrow[:], in_=rmsrow[:], func=AF.Ln),
                        wait=[p.mark("dve")], inc="act")
                    p.op("scalar", lambda s: s.activation(
                        out=rmsrow[:], in_=rmsrow[:], func=AF.Exp,
                        scale=-0.5), inc="act")
                    rbm = bcast_row(rmsrow, rb, ps_c,
                                    [p.mark("act"), p.mark("dve")])
                    xr_m = None
                    for dc in range(16):
                        xr_m = p.dma("sync", lambda s, dc=dc, t0=t0:
                                     s.dma_start(
                                         out=xres[:],
                                         in_=x_in.rearrange(
                                             "(kc p) t -> p kc t", p=128)[
                                             :, dc, t0:t0 + 1024]),
                                     wait=[st_m, rbm], stream="f1xres")
                        p.op("vector", lambda v, dc=dc: v.tensor_mul(
                            tmp32[:], ybf[:, dc, :], rb[:]),
                            wait=[rbm, st_m], inc="dve")
                        ob = p.op("vector",
                                  lambda v: v.scalar_tensor_tensor(
                                      tmp32[:], tmp32[:], 0.5, xres[:],
                                      ALU.mult, ALU.add),
                                  wait=[xr_m], inc="dve")
                        st_m = p.dma("gpsimd", lambda g, dc=dc, t0=t0:
                                     g.dma_start(
                                         out=x_out.rearrange(
                                             "(kc p) t -> p kc t", p=128)[
                                             :, dc, t0:t0 + 1024],
                                         in_=tmp32[:]),
                                     wait=[ob], stream="f1xst")
                    pctx.close()
        p.barrier()
    return (cc_out, agm)




def prep_ffn1_inputs(gu_pk, gu_sc, dn_pk, dn_sc):
    out = []
    for r in range(NCORE):
        g = np.ascontiguousarray(
            gu_pk[1408 * r:1408 * (r + 1)].reshape(N_GU_OT, 128, 256))
        gs = np.ascontiguousarray(gu_sc[:, 1408 * r:1408 * (r + 1)])
        d = np.ascontiguousarray(
            dn_pk[256 * r:256 * (r + 1)].reshape(N_DN_OT, 128, 704))
        ds = np.ascontiguousarray(dn_sc[:, 256 * r:256 * (r + 1)])
        out.append((g, gs, d, ds))
    return out


def make_ffn1_launch():
    nc = bass.Bass("TRN2", target_bir_lowering=False, debug=False)
    ctx = ExitStack()
    x_in = nc.dram_tensor("x", [D, TLOC], F32, kind="ExternalInput").ap()
    gu_pk = nc.dram_tensor("gu_pk", [N_GU_OT, 128, 256], I32,
                           kind="ExternalInput").ap()
    gu_sc = nc.dram_tensor("gu_sc", [16, 1408], F16, kind="ExternalInput").ap()
    dn_pk = nc.dram_tensor("dn_pk", [N_DN_OT, 128, 704], I32,
                           kind="ExternalInput").ap()
    dn_sc = nc.dram_tensor("dn_sc", [44, 256], F16, kind="ExternalInput").ap()
    ident = nc.dram_tensor("ident", [128, 128], BF16, kind="ExternalInput").ap()
    x_out = nc.dram_tensor("xout", [D, TLOC], F32, kind="ExternalOutput").ap()
    ins = {"gu_pk": gu_pk, "gu_sc": gu_sc, "dn_pk": dn_pk, "dn_sc": dn_sc,
           "ident": ident}
    with ctx:
        p = Prog(nc, ctx)
        build_ffn1(nc, p, x_in, x_out, ins)
        p.finalize([p.mark("ds_f1xst_0")])
    return nc


def run_ffn1(x_slices, gu_pk, gu_sc, dn_pk, dn_sc):
    nc = make_ffn1_launch()
    sl = prep_ffn1_inputs(gu_pk, gu_sc, dn_pk, dn_sc)
    idb = np.eye(128, dtype=np.float32).astype(ml_dtypes.bfloat16)
    in_maps = [{"x": x_slices[c], "gu_pk": sl[c][0], "gu_sc": sl[c][1],
                "dn_pk": sl[c][2], "dn_sc": sl[c][3], "ident": idb}
               for c in range(NCORE)]
    res = run_bass_kernel_spmd(nc, in_maps, core_ids=list(range(NCORE)))
    return [res.results[c]["xout"] for c in range(NCORE)]


def build_attn1(nc, p, x_in, x_out, wflat_ext, ins, tag="s5",
                staged=None):
    NW1 = 2 * 128 * 16 * 128
    per_rank = 4 * NW1
    if staged is None:
        wfull, wmark = stage_weights_ag(p, nc, wflat_ext, per_rank, tag + "w")
    else:
        wfull, wmark = staged
    wf = wfull.ap()

    def w_chunkset(widx, oc):
        r, i = oc // 2, oc % 2
        off = r * per_rank + widx * NW1 + i * (128 * 16 * 128)
        return wf[off:off + 128 * 16 * 128].rearrange(
            "(p kc m) -> p kc m", p=128, kc=16)

    ek_d = nc.dram_tensor(tag + "_ek", [16, 128, TLOC], F32, kind="Internal")
    ekv_d = nc.dram_tensor(tag + "_ekv", [16, 128, TLOC], F32, kind="Internal")
    r_d = nc.dram_tensor(tag + "_r", [16, 128, TLOC], BF16, kind="Internal")
    ccb_in = nc.dram_tensor(tag + "_ccbi", [128, 16], F32, kind="Internal")
    ccb_out = nc.dram_tensor(tag + "_ccbo", [NCORE * 128, 16], F32,
                             kind="Internal")
    cct_in = nc.dram_tensor(tag + "_ccti", [128, 32], F32, kind="Internal")
    cct_out = nc.dram_tensor(tag + "_ccto", [NCORE * 128, 32], F32,
                             kind="Internal")

    octx = ExitStack()
    osb = lambda n, s, d: octx.enter_context(nc.sbuf_tensor(tag + n, s, d))
    m1row = osb("_m1r", [1, TLOC], F32)
    r1row = osb("_r1r", [1, TLOC], F32)
    vecs = osb("_vecs", [128, 7, 16], F32)
    maskbc = osb("_mkb", [128, 2, 8], F32)
    xbuf2 = osb("_xa", [128, DC, TLOC], BF16)
    sxcol = osb("_sx", [128, 16], F32)
    ones1f = osb("_o1f", [1, 128], F32)

    vm_ = p.dma("sync", lambda s: s.dma_start(
        out=vecs[:], in_=ins["vecs"]), stream=tag + "vec")
    p.op("vector", lambda v: v.memset(ones1f[:], 1.0), inc="dve")

    with ExitStack() as ctx:
        xbuf = ctx.enter_context(
            nc.sbuf_tensor(tag + "_u", [128, DC, TLOC], BF16))
        ln = LNCtx(nc, ctx, tag + "_ln", TLOC)
        pubcol = ctx.enter_context(
            nc.sbuf_tensor(tag + "_pub", [128, 16], F32))
        xm = load_xbuf(p, x_in, xbuf)
        lnm = ln_compute(p, nc, ctx, ln, xbuf, DC, xm)
        p.op("vector", lambda v: v.tensor_copy(m1row[:], ln.mrow[:]),
             wait=[lnm], inc="dve")
        rows_m = p.op("vector", lambda v: v.tensor_copy(
            r1row[:], ln.rrow[:]), inc="dve")
        nm1 = ln_normalize(p, ln, xbuf, DC, rows_m)
        lnm2 = ln_compute(p, nc, ctx, ln, xbuf, DC, nm1)
        am = None
        for kc in range(DC):
            p.op("vector", lambda v, kc=kc: v.tensor_sub(
                ln.tmp32[:], xbuf[:, kc, :], ln.mbc[:]),
                wait=[lnm2] if kc == 0 else (), inc="dve")
            p.op("vector", lambda v: v.tensor_mul(
                ln.tmp32[:], ln.tmp32[:], ln.rbc[:]), inc="dve")
            am = p.op("vector", lambda v, kc=kc: v.tensor_scalar(
                xbuf2[:, kc, :], ln.tmp32[:],
                vecs[:, 0, kc:kc + 1], vecs[:, 1, kc:kc + 1],
                ALU.mult, ALU.add), wait=[vm_] if kc == 0 else (),
                inc="dve")
        pm = None
        for kc in range(DC):
            pm = p.op("vector", lambda v, kc=kc: v.tensor_copy(
                pubcol[:, kc:kc + 1], xbuf2[:, kc, TLOC - 1:TLOC]),
                wait=[am] if kc == 0 else (), inc="dve")
        pbm = p.dma("gpsimd", lambda g: g.dma_start(
            out=ccb_in.ap(), in_=pubcol[:]), wait=[pm], stream=tag + "pb")
        agb = p.op("gpsimd", lambda g: g.collective_compute(
            "AllGather", ALU.bypass, replica_groups=[list(range(NCORE))],
            ins=[ccb_in.ap().opt()], outs=[ccb_out.ap().opt()]),
            wait=[pbm], inc="cc")
        with ExitStack() as pctx:
            ps_b = pctx.enter_context(
                nc.psum_tensor(tag + "psmk", [128, 8], F32))
            mkrow = pctx.enter_context(
                nc.sbuf_tensor(tag + "_mkr", [1, 16], F32))
            bm1 = p.dma("sync", lambda s: s.dma_start(
                out=mkrow[0:1, 0:8], in_=ins["bmask"]), stream=tag + "bm")
            bm2 = p.dma("sync", lambda s: s.dma_start(
                out=mkrow[0:1, 8:16], in_=ins["smask"]), stream=tag + "bm2")
            k1 = p.op("tensor", lambda t: t.matmul(
                ps_b[:], ones1f[:], mkrow[0:1, 0:8],
                start=True, stop=True), wait=[bm1, bm2, am], inc="pe")
            p.op("vector", lambda v: v.tensor_copy(
                maskbc[:, 0, :], ps_b[:]), wait=[k1], inc="dve")
            k2 = p.op("tensor", lambda t: t.matmul(
                ps_b[:], ones1f[:], mkrow[0:1, 8:16],
                start=True, stop=True), wait=[p.mark("dve")], inc="pe")
            mkm = p.op("vector", lambda v: v.tensor_copy(
                maskbc[:, 1, :], ps_b[:]), wait=[k2], inc="dve")
        with ExitStack() as bctx:
            bndl = bctx.enter_context(
                nc.sbuf_tensor(tag + "_bnd", [128, 8, 16], F32))
            blm = p.dma("sync", lambda s: s.dma_start(
                out=bndl[:], in_=ccb_out.ap().rearrange(
                    "(r p) c -> p r c", p=128)), wait=[agb, mkm],
                stream=tag + "bnl")
            fm = p.op("vector", lambda v: v.memset(sxcol[:], 0.0), inc="dve")
            for j in range(NCORE):
                fm = p.op("vector", lambda v, j=j: v.scalar_tensor_tensor(
                    sxcol[:], bndl[:, j, :], maskbc[:, 0, j:j + 1],
                    sxcol[:], ALU.mult, ALU.add),
                    wait=[blm, mkm] if j == 0 else (), inc="dve")
        p.barrier()

    with ExitStack() as ctx:
        sb = lambda n, s, d: ctx.enter_context(nc.sbuf_tensor(tag + n, s, d))
        xkb = sb("_xk", [128, DC, 512], BF16)
        xvb = sb("_xv", [128, DC, 512], BF16)
        xrb = sb("_xrm", [128, DC, 512], BF16)
        tmp32 = sb("_t32b", [128, 512], F32)
        kf32 = sb("_kf", [128, 512], F32)
        vf32 = sb("_vf", [128, 512], F32)
        ekS = [sb(f"_ekS{i}", [128, 512], F32) for i in range(2)]
        ekvS = [sb(f"_ekvS{i}", [128, 512], F32) for i in range(2)]
        rS = [sb(f"_rS{i}", [128, 512], BF16) for i in range(2)]
        wkb = [sb(f"_wk{i}", [128, 16, 128], BF16) for i in range(2)]
        wvb = [sb(f"_wv{i}", [128, 16, 128], BF16) for i in range(2)]
        wrb = [sb(f"_wr{i}", [128, 16, 128], BF16) for i in range(2)]
        with ExitStack() as pctx:
            ps_k = pctx.enter_context(
                nc.psum_tensor(tag + "psk", [128, 512], F32))
            ps_v = pctx.enter_context(
                nc.psum_tensor(tag + "psv", [128, 512], F32))
            ps_r = pctx.enter_context(
                nc.psum_tensor(tag + "psr", [128, 512], F32))
            ekst = {}
            ekvst = {}
            rst = {}
            kev = {}
            vev = {}
            rev = {}
            mixm = None
            for ts in range(4):
                t0 = ts * 512
                for kc in range(DC):
                    for mi, dst in ((2, xkb), (3, xvb), (4, xrb)):
                        if ts == 0:
                            p.op("vector", lambda v, kc=kc: v.tensor_sub(
                                tmp32[:, 0:1], xbuf2[:, kc, 0:1],
                                sxcol[:, kc:kc + 1]),
                                wait=[p.mark("pe")] if kc == 0 and mi == 2
                                else (), inc="dve")
                            p.op("vector",
                                 lambda v, kc=kc, mi=mi, dst=dst:
                                 v.scalar_tensor_tensor(
                                     dst[:, kc, 0:1], tmp32[:, 0:1],
                                     vecs[:, mi, kc:kc + 1],
                                     sxcol[:, kc:kc + 1],
                                     ALU.mult, ALU.add), inc="dve")
                            p.op("vector", lambda v, kc=kc: v.tensor_sub(
                                tmp32[:, 1:512], xbuf2[:, kc, 1:512],
                                xbuf2[:, kc, 0:511]), inc="dve")
                            mixm = p.op(
                                "vector", lambda v, kc=kc, mi=mi, dst=dst:
                                v.scalar_tensor_tensor(
                                    dst[:, kc, 1:512], tmp32[:, 1:512],
                                    vecs[:, mi, kc:kc + 1],
                                    xbuf2[:, kc, 0:511],
                                    ALU.mult, ALU.add), inc="dve")
                        else:
                            p.op("vector", lambda v, kc=kc, t0=t0:
                                 v.tensor_sub(
                                     tmp32[:], xbuf2[:, kc, t0:t0 + 512],
                                     xbuf2[:, kc, t0 - 1:t0 + 511]),
                                 wait=[p.mark("pe")] if kc == 0 and mi == 2
                                 else (), inc="dve")
                            mixm = p.op(
                                "vector", lambda v, kc=kc, mi=mi, dst=dst,
                                t0=t0: v.scalar_tensor_tensor(
                                    dst[:, kc, :], tmp32[:],
                                    vecs[:, mi, kc:kc + 1],
                                    xbuf2[:, kc, t0 - 1:t0 + 511],
                                    ALU.mult, ALU.add), inc="dve")
                for oc in range(16):
                    ob = oc % 2
                    lk = p.dma("sync", lambda s, oc=oc, ob=ob: s.dma_start(
                        out=wkb[ob][:], in_=w_chunkset(0, oc)),
                        wait=[wmark] if ts == 0 and oc < 2
                        else [kev.get((ts, oc - 2)) or kev.get((ts - 1, 14 + oc))],
                        stream=tag + "wk", slot=ob)
                    lv = p.dma("sync", lambda s, oc=oc, ob=ob: s.dma_start(
                        out=wvb[ob][:], in_=w_chunkset(1, oc)),
                        stream=tag + "wv", slot=ob)
                    lr = p.dma("sync", lambda s, oc=oc, ob=ob: s.dma_start(
                        out=wrb[ob][:], in_=w_chunkset(2, oc)),
                        stream=tag + "wr", slot=ob)
                    pk_g = kev.get((ts, oc - 1)) or kev.get((ts - 1, 15))
                    pv_g = vev.get((ts, oc - 1)) or vev.get((ts - 1, 15))
                    pr_g = rev.get((ts, oc - 1)) or rev.get((ts - 1, 15))
                    for kc in range(16):
                        p.op("tensor", lambda t, kc=kc, ob=ob: t.matmul(
                            ps_k[:], wkb[ob][:, kc, :], xkb[:, kc, :],
                            start=(kc == 0), stop=(kc == 15)),
                            wait=[lk, mixm, pk_g] if kc == 0 else ())
                    for kc in range(16):
                        km = p.op("tensor", lambda t, kc=kc, ob=ob: t.matmul(
                            ps_v[:], wvb[ob][:, kc, :], xvb[:, kc, :],
                            start=(kc == 0), stop=(kc == 15)),
                            wait=[lv, pv_g] if kc == 0 else ())
                    for kc in range(16):
                        rm_ = p.op("tensor", lambda t, kc=kc, ob=ob: t.matmul(
                            ps_r[:], wrb[ob][:, kc, :], xrb[:, kc, :],
                            start=(kc == 0), stop=(kc == 15)),
                            wait=[lr, pr_g] if kc == 0 else (),
                            inc="pe" if kc == 15 else None)
                    ke = p.op("vector", lambda v: v.tensor_copy(
                        kf32[:], ps_k[:]), wait=[rm_], inc="dve")
                    kev[(ts, oc)] = ke
                    ve = p.op("vector", lambda v: v.tensor_copy(
                        vf32[:], ps_v[:]), inc="dve")
                    vev[(ts, oc)] = ve
                    eb = oc % 2
                    ekm = p.op("scalar", lambda s, eb=eb: s.activation(
                        out=ekS[eb][:], in_=kf32[:], func=AF.Exp),
                        wait=[ke, ekst.get((ts, oc - 2)) or
                              ekst.get((ts - 1, 14 + oc))], inc="act")
                    ekvm = p.op("vector", lambda v, eb=eb: v.tensor_mul(
                        ekvS[eb][:], ekS[eb][:], vf32[:]),
                        wait=[ekm, ekvst.get((ts, oc - 2)) or
                              ekvst.get((ts - 1, 14 + oc))], inc="dve")
                    rsm = p.op("scalar", lambda s, eb=eb: s.activation(
                        out=rS[eb][:], in_=ps_r[:], func=AF.Sigmoid),
                        wait=[rm_, rst.get((ts, oc - 2)) or
                              rst.get((ts - 1, 14 + oc))], inc="act")
                    rev[(ts, oc)] = rsm
                    ekst[(ts, oc)] = p.dma(
                        "gpsimd", lambda g, oc=oc, t0=t0, eb=eb: g.dma_start(
                            out=ek_d.ap()[oc, :, t0:t0 + 512],
                            in_=ekS[eb][:]), wait=[ekm],
                        stream=tag + "eks", slot=eb)
                    ekvst[(ts, oc)] = p.dma(
                        "gpsimd", lambda g, oc=oc, t0=t0, eb=eb: g.dma_start(
                            out=ekv_d.ap()[oc, :, t0:t0 + 512],
                            in_=ekvS[eb][:]), wait=[ekvm],
                        stream=tag + "ekvs", slot=eb)
                    rst[(ts, oc)] = p.dma(
                        "gpsimd", lambda g, oc=oc, t0=t0, eb=eb: g.dma_start(
                            out=r_d.ap()[oc, :, t0:t0 + 512],
                            in_=rS[eb][:]), wait=[rsm],
                        stream=tag + "rs", slot=eb)
        p.barrier()

    with ExitStack() as ctx:
        sb = lambda n, s, d: ctx.enter_context(nc.sbuf_tensor(tag + n, s, d))
        ekb = sb("_ekb", [128, TLOC], F32)
        ekvb = sb("_ekvb", [128, TLOC], F32)
        numo = sb("_no", [128, TLOC], F32)
        deno = sb("_do", [128, TLOC], F32)
        onest = sb("_ones", [128, TLOC], F32)
        ewt = sb("_ewt", [128, TLOC], F32)
        dcol = sb("_dcol", [128, 1], F32)
        ewcol = sb("_ewc", [128, 1], F32)
        tailb = sb("_tail", [128, 32], F32)
        p.op("vector", lambda v: v.memset(onest[:], 1.0), inc="dve")
        for ch in range(16):
            lek = p.dma("sync", lambda s, ch=ch: s.dma_start(
                out=ekb[:], in_=ek_d.ap()[ch]),
                wait=[p.mark("dve")] if ch > 0 else [],
                stream=tag + "ekl")
            lekv = p.dma("sync", lambda s, ch=ch: s.dma_start(
                out=ekvb[:], in_=ekv_d.ap()[ch]), stream=tag + "ekvl")
            p.op("scalar", lambda s, ch=ch: s.activation(
                out=dcol[:], in_=vecs[:, 5, ch:ch + 1], func=AF.Exp),
                wait=[vm_] if ch == 0 else (), inc="act")
            p.op("vector", lambda v: v.tensor_scalar_mul(
                dcol[:], dcol[:], -1.0), wait=[p.mark("act")], inc="dve")
            ewm = p.op("scalar", lambda s: s.activation(
                out=ewcol[:], in_=dcol[:], func=AF.Exp),
                wait=[p.mark("dve")], inc="act")
            etm = p.op("vector", lambda v: v.tensor_scalar(
                ewt[:], onest[:], ewcol[:], None, ALU.mult, ALU.bypass),
                wait=[ewm], inc="dve")
            sn = p.op("vector", lambda v: v.tensor_tensor_scan(
                numo[:], ewt[:], ekvb[:], 0.0, ALU.mult, ALU.add),
                wait=[lekv, etm], inc="dve")
            sd = p.op("vector", lambda v: v.tensor_tensor_scan(
                deno[:], ewt[:], ekb[:], 0.0, ALU.mult, ALU.add),
                wait=[lek], inc="dve")
            p.op("vector", lambda v, ch=ch: v.tensor_copy(
                tailb[:, 2 * ch:2 * ch + 1],
                numo[:, TLOC - 1:TLOC]), inc="dve")
            tm = p.op("vector", lambda v, ch=ch: v.tensor_copy(
                tailb[:, 2 * ch + 1:2 * ch + 2],
                deno[:, TLOC - 1:TLOC]), inc="dve")
        ptm = p.dma("gpsimd", lambda g: g.dma_start(
            out=cct_in.ap(), in_=tailb[:]), wait=[tm], stream=tag + "tp")
        agt = p.op("gpsimd", lambda g: g.collective_compute(
            "AllGather", ALU.bypass, replica_groups=[list(range(NCORE))],
            ins=[cct_in.ap().opt()], outs=[cct_out.ap().opt()]),
            wait=[ptm], inc="cc")
        p.barrier()

    rwkvb = osb("_rwkv", [128, DC, TLOC], BF16)
    with ExitStack() as ctx:
        sb = lambda n, s, d: ctx.enter_context(nc.sbuf_tensor(tag + n, s, d))
        ekb = sb("_ekb2", [128, TLOC], F32)
        ekvb = sb("_ekvb2", [128, TLOC], F32)
        numb = sb("_numb", [128, TLOC + 1], F32)
        denb = sb("_denb", [128, TLOC + 1], F32)
        wkvb = sb("_wkvb", [128, TLOC], F32)
        rl = sb("_rl", [128, TLOC], BF16)
        onest = sb("_ones2", [128, TLOC], F32)
        ewt = sb("_ewt2", [128, TLOC], F32)
        dcol = sb("_dcol2", [128, 1], F32)
        ewcol = sb("_ewc2", [128, 1], F32)
        ewCc = sb("_ewC", [128, 1], F32)
        eucol = sb("_euc", [128, 1], F32)
        t1c = sb("_t1c", [128, 1], F32)
        snc = sb("_snc", [128, 1], F32)
        sdc = sb("_sdc", [128, 1], F32)
        tlb = sb("_tlb", [128, 8, 32], F32)
        p.op("vector", lambda v: v.memset(onest[:], 1.0), inc="dve")
        tlm = p.dma("sync", lambda s: s.dma_start(
            out=tlb[:], in_=cct_out.ap().rearrange("(r p) c -> p r c",
                                                   p=128)),
            wait=[agt], stream=tag + "tll")
        for ch in range(16):
            lek = p.dma("sync", lambda s, ch=ch: s.dma_start(
                out=ekb[:], in_=ek_d.ap()[ch]),
                wait=[p.mark("dve")] if ch > 0 else [],
                stream=tag + "ekl2")
            lekv = p.dma("sync", lambda s, ch=ch: s.dma_start(
                out=ekvb[:], in_=ekv_d.ap()[ch]), stream=tag + "ekvl2")
            lrc = p.dma("sync", lambda s, ch=ch: s.dma_start(
                out=rl[:], in_=r_d.ap()[ch]), stream=tag + "rl")
            p.op("scalar", lambda s, ch=ch: s.activation(
                out=dcol[:], in_=vecs[:, 5, ch:ch + 1], func=AF.Exp),
                inc="act")
            p.op("vector", lambda v: v.tensor_scalar_mul(
                t1c[:], dcol[:], -float(TLOC)), wait=[p.mark("act")],
                inc="dve")
            p.op("scalar", lambda s: s.activation(
                out=ewCc[:], in_=t1c[:], func=AF.Exp),
                wait=[p.mark("dve")], inc="act")
            p.op("vector", lambda v: v.tensor_scalar_mul(
                dcol[:], dcol[:], -1.0), inc="dve")
            ewm = p.op("scalar", lambda s: s.activation(
                out=ewcol[:], in_=dcol[:], func=AF.Exp),
                wait=[p.mark("dve")], inc="act")
            eum = p.op("scalar", lambda s, ch=ch: s.activation(
                out=eucol[:], in_=vecs[:, 6, ch:ch + 1], func=AF.Exp),
                inc="act")
            etm = p.op("vector", lambda v: v.tensor_scalar(
                ewt[:], onest[:], ewcol[:], None, ALU.mult, ALU.bypass),
                wait=[ewm], inc="dve")
            p.op("vector", lambda v: v.memset(snc[:], 0.0), inc="dve")
            p.op("vector", lambda v: v.memset(sdc[:], 0.0), inc="dve")
            fmm = p.mark("dve")
            for j in range(NCORE):
                p.op("vector", lambda v: v.tensor_scalar_add(
                    t1c[:], ewCc[:], -1.0),
                    wait=[tlm, fmm] if j == 0 else (), inc="dve")
                p.op("vector", lambda v, j=j: v.tensor_scalar(
                    t1c[:], t1c[:], maskbc[:, 1, j:j + 1], 1.0,
                    ALU.mult, ALU.add), inc="dve")
                p.op("vector", lambda v: v.tensor_mul(
                    snc[:], snc[:], t1c[:]), inc="dve")
                p.op("vector", lambda v, j=j, ch=ch:
                     v.scalar_tensor_tensor(
                         snc[:], tlb[:, j, 2 * ch:2 * ch + 1],
                         maskbc[:, 1, j:j + 1], snc[:],
                         ALU.mult, ALU.add), inc="dve")
                p.op("vector", lambda v: v.tensor_mul(
                    sdc[:], sdc[:], t1c[:]), inc="dve")
                p.op("vector", lambda v, j=j, ch=ch:
                     v.scalar_tensor_tensor(
                         sdc[:], tlb[:, j, 2 * ch + 1:2 * ch + 2],
                         maskbc[:, 1, j:j + 1], sdc[:],
                         ALU.mult, ALU.add), inc="dve")
            p.op("vector", lambda v: v.tensor_copy(
                numb[:, 0:1], snc[:]), inc="dve")
            p.op("vector", lambda v: v.tensor_copy(
                denb[:, 0:1], sdc[:]), inc="dve")
            p.op("vector", lambda v: v.tensor_tensor_scan(
                numb[:, 1:TLOC + 1], ewt[:], ekvb[:], snc[:],
                ALU.mult, ALU.add), wait=[lekv, etm], inc="dve")
            p.op("vector", lambda v: v.tensor_tensor_scan(
                denb[:, 1:TLOC + 1], ewt[:], ekb[:], sdc[:],
                ALU.mult, ALU.add), wait=[lek], inc="dve")
            p.op("vector", lambda v: v.tensor_scalar(
                wkvb[:], ekvb[:], eucol[:], None, ALU.mult, ALU.bypass),
                wait=[eum], inc="dve")
            p.op("vector", lambda v: v.tensor_add(
                wkvb[:], wkvb[:], numb[:, 0:TLOC]), inc="dve")
            p.op("vector", lambda v: v.tensor_scalar(
                ewt[:], ekb[:], eucol[:], None, ALU.mult, ALU.bypass),
                inc="dve")
            dnm = p.op("vector", lambda v: v.tensor_add(
                ewt[:], ewt[:], denb[:, 0:TLOC]), inc="dve")
            p.op("scalar", lambda s: s.activation(
                out=ewt[:], in_=ewt[:], func=AF.Ln), wait=[dnm], inc="act")
            rcm = p.op("scalar", lambda s: s.activation(
                out=ewt[:], in_=ewt[:], func=AF.Exp, scale=-1.0),
                inc="act")
            p.op("vector", lambda v: v.tensor_mul(
                wkvb[:], wkvb[:], ewt[:]), wait=[rcm], inc="dve")
            p.op("vector", lambda v, ch=ch: v.tensor_mul(
                rwkvb[:, ch, :], wkvb[:], rl[:]), wait=[lrc], inc="dve")
        p.barrier()

    with ExitStack() as ctx:
        sb = lambda n, s, d: ctx.enter_context(nc.sbuf_tensor(tag + n, s, d))
        wob = [sb(f"_wo{i}", [128, 16, 128], BF16) for i in range(2)]
        m1bc = sb("_m1bc", [128, 512], F32)
        r1bc = sb("_r1bc", [128, 512], F32)
        xres = sb("_xr2", [128, 512], F32)
        u32 = sb("_u32", [128, 512], F32)
        o32 = sb("_o32", [128, 512], F32)
        with ExitStack() as pctx:
            ps_o = [pctx.enter_context(
                nc.psum_tensor(tag + f"pso{i}", [128, 512], F32))
                for i in range(2)]
            ps_c = pctx.enter_context(
                nc.psum_tensor(tag + "psbc", [128, 512], F32))
            st_m = None
            oev = {}
            wo_m = {}
            for ts in range(4):
                t0 = ts * 512
                b1 = p.op("tensor", lambda t, t0=t0: t.matmul(
                    ps_c[:], ones1f[:], m1row[:, t0:t0 + 512],
                    start=True, stop=True),
                    wait=[rows_m, p.mark("dve")], inc="pe")
                p.op("vector", lambda v: v.tensor_copy(m1bc[:], ps_c[:]),
                     wait=[b1], inc="dve")
                b2 = p.op("tensor", lambda t, t0=t0: t.matmul(
                    ps_c[:], ones1f[:], r1row[:, t0:t0 + 512],
                    start=True, stop=True),
                    wait=[p.mark("dve")], inc="pe")
                bcm = p.op("vector", lambda v: v.tensor_copy(
                    r1bc[:], ps_c[:]), wait=[b2], inc="dve")
                for dc in range(16):
                    ob = dc % 2
                    prevkey = (ts, dc - 2) if dc >= 2 else (ts - 1, 14 + dc)
                    wo_m[(ts, dc)] = p.dma(
                        "sync", lambda s, dc=dc, ob=ob: s.dma_start(
                            out=wob[ob][:], in_=w_chunkset(3, dc)),
                        wait=[oev.get(prevkey) or wmark],
                        stream=tag + "wo", slot=ob)
                    for kc in range(16):
                        om_ = p.op("tensor",
                                   lambda t, kc=kc, ob=ob, t0=t0: t.matmul(
                                       ps_o[ob][:], wob[ob][:, kc, :],
                                       rwkvb[:, kc, t0:t0 + 512],
                                       start=(kc == 0), stop=(kc == 15)),
                                   wait=[wo_m[(ts, dc)],
                                         oev.get(prevkey)]
                                   if kc == 0 else (),
                                   inc="pe" if kc == 15 else None)
                    xr_m = p.dma("sync", lambda s, dc=dc, t0=t0: s.dma_start(
                        out=xres[:],
                        in_=x_in.rearrange("(kc p) t -> p kc t", p=128)[
                            :, dc, t0:t0 + 512]),
                        wait=[st_m], stream="f1xres")
                    p.op("vector", lambda v: v.tensor_sub(
                        u32[:], xres[:], m1bc[:]),
                        wait=[xr_m, bcm], inc="dve")
                    p.op("vector", lambda v: v.tensor_mul(
                        u32[:], u32[:], r1bc[:]), inc="dve")
                    p.op("vector", lambda v: v.tensor_add(
                        u32[:], u32[:], xres[:]), inc="dve")
                    oev[(ts, dc)] = p.op("vector", lambda v, ob=ob:
                                         v.tensor_add(
                                             o32[:], u32[:], ps_o[ob][:]),
                                         wait=[om_, st_m], inc="dve")
                    st_m = p.dma(
                        "gpsimd", lambda g, dc=dc, t0=t0: g.dma_start(
                            out=x_out.rearrange("(kc p) t -> p kc t",
                                                p=128)[:, dc, t0:t0 + 512],
                            in_=o32[:]),
                        wait=[oev[(ts, dc)]], stream="f1xst")
        p.barrier()
    octx.close()


def prep_attn1_inputs(Wk, Wv, Wr, Wo, lnw, lnb, mixk, mixv, mixr, td, ctw):
    ts_ = [tile_w_out_in(np.asarray(W), 16, 16) for W in (Wk, Wv, Wr, Wo)]
    wslices = []
    for r in range(NCORE):
        s = np.concatenate([t[2 * r:2 * r + 2].ravel() for t in ts_])
        wslices.append(np.ascontiguousarray(s))
    vecs = np.stack([np.asarray(v).reshape(16, 128).T.astype(np.float32)
                     for v in (lnw, lnb, mixk, mixv, mixr, td, ctw)],
                    axis=1)
    vecs = np.ascontiguousarray(vecs)
    bmasks, smasks = [], []
    for c in range(NCORE):
        b, qt = c // 4, c % 4
        bm = np.zeros((1, 8), np.float32)
        if qt > 0:
            bm[0, c - 1] = 1.0
        sm = np.zeros((1, 8), np.float32)
        for r in range(NCORE):
            if r // 4 == b and r % 4 < qt:
                sm[0, r] = 1.0
        bmasks.append(bm)
        smasks.append(sm)
    return wslices, vecs, bmasks, smasks


def make_attn1_launch():
    nc = bass.Bass("TRN2", target_bir_lowering=False, debug=False)
    ctx = ExitStack()
    x_in = nc.dram_tensor("x", [D, TLOC], F32, kind="ExternalInput").ap()
    per_rank = 4 * 2 * 128 * 16 * 128
    wflat = nc.dram_tensor("wflat", [per_rank], F32, kind="ExternalInput").ap()
    vecs = nc.dram_tensor("vecs", [128, 7, 16], F32, kind="ExternalInput").ap()
    bmask = nc.dram_tensor("bmask", [1, 8], F32, kind="ExternalInput").ap()
    smask = nc.dram_tensor("smask", [1, 8], F32, kind="ExternalInput").ap()
    x_out = nc.dram_tensor("xout", [D, TLOC], F32, kind="ExternalOutput").ap()
    ins = {"vecs": vecs, "bmask": bmask, "smask": smask}
    with ctx:
        p = Prog(nc, ctx)
        build_attn1(nc, p, x_in, x_out, wflat, ins)
        p.finalize([p.mark("ds_f1xst_0")])
    return nc


def run_attn1(x_slices, Wk, Wv, Wr, Wo, lnw, lnb, mixk, mixv, mixr, td, ctw):
    nc = make_attn1_launch()
    wsl, vecs, bms, sms = prep_attn1_inputs(Wk, Wv, Wr, Wo, lnw, lnb,
                                            mixk, mixv, mixr, td, ctw)
    in_maps = [{"x": x_slices[c], "wflat": wsl[c], "vecs": vecs,
                "bmask": bms[c], "smask": sms[c]} for c in range(NCORE)]
    res = run_bass_kernel_spmd(nc, in_maps, core_ids=list(range(NCORE)))
    return [res.results[c]["xout"] for c in range(NCORE)]


def make_fused_launch():
    nc = bass.Bass("TRN2", target_bir_lowering=False, debug=False)
    ctx = ExitStack()
    x_in = nc.dram_tensor("x", [D, TLOC], F32, kind="ExternalInput").ap()
    pr_f0 = 2 * (8 * 16 * 128 * 128) + 2 * 64 * 128 * 128
    pr_a = 4 * 2 * 128 * 16 * 128
    w_f0 = nc.dram_tensor("w_f0", [pr_f0], F32, kind="ExternalInput").ap()
    h128c = nc.dram_tensor("h128", [128, 128], BF16, kind="ExternalInput").ap()
    w_a0 = nc.dram_tensor("w_a0", [pr_a], F32, kind="ExternalInput").ap()
    ident = nc.dram_tensor("ident", [128, 128], BF16, kind="ExternalInput").ap()
    masks = nc.dram_tensor("masks", [4, 128, 512], F32,
                           kind="ExternalInput").ap()
    zmask = nc.dram_tensor("zmask", [1, 32], F32, kind="ExternalInput").ap()
    smask = nc.dram_tensor("smask", [1, 8], F32, kind="ExternalInput").ap()
    betas = nc.dram_tensor("betas", [1, 2048], F32, kind="ExternalInput").ap()
    gu_pk = nc.dram_tensor("gu_pk", [N_GU_OT, 128, 256], I32,
                           kind="ExternalInput").ap()
    gu_sc = nc.dram_tensor("gu_sc", [16, 1408], F16, kind="ExternalInput").ap()
    dn_pk = nc.dram_tensor("dn_pk", [N_DN_OT, 128, 704], I32,
                           kind="ExternalInput").ap()
    dn_sc = nc.dram_tensor("dn_sc", [44, 256], F16, kind="ExternalInput").ap()
    w_a1 = nc.dram_tensor("w_a1", [pr_a], F32, kind="ExternalInput").ap()
    vecs = nc.dram_tensor("vecs", [128, 7, 16], F32, kind="ExternalInput").ap()
    bmask = nc.dram_tensor("bmask", [1, 8], F32, kind="ExternalInput").ap()
    x_out = nc.dram_tensor("xout", [D, TLOC], F32, kind="ExternalOutput").ap()
    xm_ = [nc.dram_tensor(f"xmid{i}", [D, TLOC], F32, kind="Internal").ap()
           for i in range(5)]
    a0c = {"ident": ident, "masks": masks, "zmask": zmask, "smask": smask,
           "betas": betas}
    f1c = {"gu_pk": gu_pk, "gu_sc": gu_sc, "dn_pk": dn_pk, "dn_sc": dn_sc,
           "ident": ident}
    a1c = {"vecs": vecs, "bmask": bmask, "smask": smask}
    with ctx:
        p = Prog(nc, ctx)
        pr = 2 * (8 * 16 * 128 * 128) + 2 * 64 * 128 * 128
        st_f0 = stage_weights_ag(p, nc, w_f0, pr, "s1w")
        st_a0 = stage_weights_ag(p, nc, w_a0, 4 * 2 * 128 * 16 * 128, "s2w")
        st_a1 = stage_weights_ag(p, nc, w_a1, 4 * 2 * 128 * 16 * 128, "s5w")
        st_f1 = stage_ffn1_weights(nc, p, f1c, "s4d")
        build_ffn0(nc, p, x_in, xm_[0], w_f0, h128c, tag="s1", staged=st_f0)
        build_attn0(nc, p, xm_[0], xm_[1], w_a0, a0c, tag="s2", staged=st_a0)
        build_ffn0(nc, p, xm_[1], xm_[2], w_f0, h128c, tag="s3",
                   staged=st_f0)
        build_ffn1(nc, p, xm_[2], xm_[3], f1c, tag="s4", staged=st_f1)
        build_attn1(nc, p, xm_[3], xm_[4], w_a1, a1c, tag="s5",
                    staged=st_a1)
        build_ffn1(nc, p, xm_[4], x_out, f1c, tag="s6", staged=st_f1)
        p.finalize(list(p.pend["gpsimd"]))
    return nc


_FUSED = True


def kernel(**inputs):
    inp = {k: np.asarray(v) for k, v in inputs.items()}
    x = inp["L0_attn_x"].astype(np.float32, copy=False)
    xs = core_slices(x)

    ffn0_w = prep_ffn0_weights(inp["L0_ffn_W1"], inp["L0_ffn_W2"],
                               inp["L0_ffn_W3"])
    h128b = hadamard128().astype(ml_dtypes.bfloat16)
    attn0_w = prep_attn0_weights(inp["L0_attn_Wq"], inp["L0_attn_Wk"],
                                 inp["L0_attn_Wv"], inp["L0_attn_Wo"])
    idb = np.eye(128, dtype=np.float32).astype(ml_dtypes.bfloat16)
    msk = causal_masks()
    zs, ss = core_masks()
    brow = np.ascontiguousarray(
        inp["L0_attn_betas"].reshape(1, 2048).astype(np.float32))
    ffn1_in = prep_ffn1_inputs(inp["L1_ffn_gate_up_packed"],
                               inp["L1_ffn_gate_up_scales"],
                               inp["L1_ffn_down_packed"],
                               inp["L1_ffn_down_scales"])
    a1_w, a1_vecs, a1_bm, a1_sm = prep_attn1_inputs(
        inp["L1_attn_Wk"], inp["L1_attn_Wv"], inp["L1_attn_Wr"],
        inp["L1_attn_Wo"], inp["L1_attn_ln_w"], inp["L1_attn_ln_b"],
        inp["L1_attn_mix_k"], inp["L1_attn_mix_v"], inp["L1_attn_mix_r"],
        inp["L1_attn_time_decay"], inp["L1_attn_current_token_weight"])

    if _FUSED:
        nc = make_fused_launch()
        in_maps = [{
            "x": xs[c], "w_f0": ffn0_w[c], "h128": h128b,
            "w_a0": attn0_w[c], "ident": idb, "masks": msk,
            "zmask": zs[c], "smask": ss[c], "betas": brow,
            "gu_pk": ffn1_in[c][0], "gu_sc": ffn1_in[c][1],
            "dn_pk": ffn1_in[c][2], "dn_sc": ffn1_in[c][3],
            "w_a1": a1_w[c], "vecs": a1_vecs, "bmask": a1_bm[c],
        } for c in range(NCORE)]
        res = run_bass_kernel_spmd(nc, in_maps, core_ids=list(range(NCORE)))
        cur = [np.asarray(res.results[c]["xout"]) for c in range(NCORE)]
        out = np.empty((B, S, D), dtype=np.float32)
        for c in range(NCORE):
            b, q = c // 4, c % 4
            out[b, q * TLOC:(q + 1) * TLOC, :] = cur[c].T
        return out

    nc_ffn0 = make_ffn0_launch()
    nc_attn0 = make_attn0_launch()
    nc_ffn1 = make_ffn1_launch()
    nc_attn1 = make_attn1_launch()

    def run(nc, maps):
        res = run_bass_kernel_spmd(nc, maps, core_ids=list(range(NCORE)))
        return [np.asarray(res.results[c]["xout"]) for c in range(NCORE)]

    def ffn0_maps(cur):
        return [{"x": cur[c], "wflat": ffn0_w[c], "h128": h128b}
                for c in range(NCORE)]

    def attn0_maps(cur):
        return [{"x": cur[c], "wflat": attn0_w[c], "ident": idb,
                 "masks": msk, "zmask": zs[c], "smask": ss[c],
                 "betas": brow} for c in range(NCORE)]

    def ffn1_maps(cur):
        return [{"x": cur[c], "gu_pk": ffn1_in[c][0], "gu_sc": ffn1_in[c][1],
                 "dn_pk": ffn1_in[c][2], "dn_sc": ffn1_in[c][3],
                 "ident": idb} for c in range(NCORE)]

    def attn1_maps(cur):
        return [{"x": cur[c], "wflat": a1_w[c], "vecs": a1_vecs,
                 "bmask": a1_bm[c], "smask": a1_sm[c]}
                for c in range(NCORE)]

    cur = xs
    cur = run(nc_ffn0, ffn0_maps(cur))
    cur = run(nc_attn0, attn0_maps(cur))
    cur = run(nc_ffn0, ffn0_maps(cur))
    cur = run(nc_ffn1, ffn1_maps(cur))
    cur = run(nc_attn1, attn1_maps(cur))
    cur = run(nc_ffn1, ffn1_maps(cur))

    out = np.empty((B, S, D), dtype=np.float32)
    for c in range(NCORE):
        b, q = c // 4, c % 4
        out[b, q * TLOC:(q + 1) * TLOC, :] = cur[c].T
    return out
